# version 01
# baseline (speedup 1.0000x reference)
import numpy as np
import concourse.bass as bass
import concourse.mybir as mybir
from concourse.bass_utils import run_bass_kernel_spmd

F32 = mybir.dt.float32
BF16 = mybir.dt.bfloat16
A = mybir.AluOpType
AF = mybir.ActivationFunctionType
AX = mybir.AxisListType

B, T, D = 8, 2048, 1024
dA = 256
H, FH, K = 2, 128, 16
NSTEP = T // K
NT = T // 128
NE, EH, SHID = 16, 512, 1024
TP = T // 2
DEV_TAPS = []
STOP = 99
DEV_LN0_TILES = None
DEV_SCAN_STEPS = None


class Sem:
    def __init__(self, handle):
        self.h = handle
        self.val = 0


class P:
    def __init__(self):
        self.q = {e: [] for e in ("tensor", "vector", "scalar", "gpsimd", "sync")}

    def op(self, eng, fn):
        self.q[eng].append(fn)

    def wait(self, eng, sem, val):
        if val > 0:
            self.q[eng].append(lambda e, s=sem.h, v=val: e.wait_ge(s, v))


def build_program():
    nc = bass.Bass("TRN2", target_bir_lowering=False)

    def din(name, shape, dtype=F32):
        return nc.dram_tensor(name, list(shape), dtype, kind="ExternalInput").ap()

    xin = din("xin", (128, NT * D))
    dw_d = din("dw", (128, 8 * dA)); db_d = din("db", (128, 2))
    wq_d = din("wq", (128, 2 * dA)); wk_d = din("wk", (128, 2 * dA))
    wv_d = din("wv", (128, 2 * dA)); wo_d = din("wo", (128, 2 * dA))
    uw_d = din("uw", (128, 2 * D)); upb_d = din("upb", (1, D))
    w1_d = din("w1", (128, 2 * FH)); b1_d = din("b1", (1, 2 * FH))
    glw_d = din("glw", (1, 2 * FH)); glb_d = din("glb", (1, 2 * FH))
    lrw_d = din("lrw", (128, 4)); lrb_d = din("lrb", (2, 1))
    tki_d = din("tki", (1, K))
    pw_d = din("pw", (1, dA)); pb_d = din("pb", (1, dA))
    lw_d = din("lw", (1, dA)); lb_d = din("lb", (1, dA))
    gwt_d = din("gwt", (128, 8 * NE))
    wsin_d = din("wsin", (128, 8 * SHID)); vsin_d = din("vsin", (128, 8 * SHID))
    wsout_d = din("wsout", (128, 8 * D))
    ewin_d = din("ewin", (NE, 128, 8 * EH))
    evin_d = din("evin", (NE, 128, 8 * EH))
    ewout_d = din("ewout", (NE, 128, 4 * D))
    cst_d = din("cst", (128, 8))
    idf_d = din("idf", (128, 128))
    ones16_d = din("ones16", (1, 16))
    onesc_d = din("onesc", (1, 128))
    negu_d = din("negu", (16, 16))
    rk_d = din("rk", (1, K))

    out_d = nc.dram_tensor("out", [128, NT * D], F32, kind="ExternalOutput").ap()
    scr = nc.dram_tensor("scr", [2 * T], F32, kind="Internal").ap()
    scr2 = nc.dram_tensor("scr2", [128, NT * D], BF16, kind="Internal").ap()

    taps = {}
    tap_specs = {
        "lnt": [128, 8 * T], "ht": [128, 2 * T], "xqt": [128, 2 * T],
        "xkt": [128, 2 * T], "siglr": [2, T], "xkle": [128, NT * dA],
        "tgt2": [128, NT * dA], "zbar": [128, NT * dA], "x2": [128, NT * D],
        "lnt3": [128, 8 * T], "gval": [128, NT * NE], "combt": [16, T],
        "w1f": [128, 2 * FH], "b1f": [1, 2 * FH], "sigt": [128, 2 * K],
        "lrc": [16, 2 * NSTEP], "acc0": [128, 8 * TP],
        "he": [128, D], "stt": [128, 20], "xktmp": [128, dA],
        "mv": [16, 4], "rstd": [16, 2], "grb": [16, 2 * FH], "xhs": [16, 2 * FH],
        "gys": [16, 2 * FH], "ctb": [16, 2 * K], "m1n": [16, 2], "m2": [16, 2], "std": [16, 2], "sqs": [128, 4],
        "tme": [128, dA], "tmpa": [128, D], "tmpd": [128, D], "ptt": [128, dA],
    }
    for name in DEV_TAPS:
        dt_ = BF16 if name in ("combt",) else F32
        taps[name] = nc.dram_tensor("tap_" + name, tap_specs[name], dt_,
                                    kind="ExternalOutput").ap()

    p = P()

    from contextlib import ExitStack
    with ExitStack() as _es:
        XRES = _es.enter_context(nc.sbuf_tensor("XRES", [128, NT * D], BF16))
        LNT = _es.enter_context(nc.sbuf_tensor("LNT", [128, 8 * T], BF16))
        BIGP = _es.enter_context(nc.sbuf_tensor("BIGP", [128, 4 * T], BF16))
        BIGQ = _es.enter_context(nc.sbuf_tensor("BIGQ", [128, 4 * T], BF16))
        TGT2F = _es.enter_context(nc.sbuf_tensor("TGT2F", [128, 2 * NT * dA], BF16))
        HE = _es.enter_context(nc.sbuf_tensor("HE", [128, 4 * TP], BF16))
        WBIGA = _es.enter_context(nc.sbuf_tensor("WBIGA", [128, 8 * dA + 2 * D], BF16))
        WINbT = _es.enter_context(nc.sbuf_tensor("WINb", [128, 2 * 8 * EH], BF16))
        VINbT = _es.enter_context(nc.sbuf_tensor("VINb", [128, 2 * 8 * EH], BF16))
        CBX = _es.enter_context(nc.sbuf_tensor("CBX", [128, TP], BF16))
        TGT16R = _es.enter_context(nc.sbuf_tensor("TGT16R", [16, 4 * dA], BF16))
        XKLE16R = _es.enter_context(nc.sbuf_tensor("XKLE16R", [16, 4 * dA], BF16))
        ZB16 = _es.enter_context(nc.sbuf_tensor("ZB16", [16, 2 * dA], BF16))
        COMBROW = _es.enter_context(nc.sbuf_tensor("COMBROW", [1, TP], BF16))
        MTMP = _es.enter_context(nc.sbuf_tensor("MTMP", [128, TP], BF16))
        COMBT = _es.enter_context(nc.sbuf_tensor("COMBT", [16, T], BF16))
        DB = _es.enter_context(nc.sbuf_tensor("DB", [128, 2], F32))
        WQ = _es.enter_context(nc.sbuf_tensor("WQ", [128, 2 * dA], BF16))
        WK = _es.enter_context(nc.sbuf_tensor("WK", [128, 2 * dA], BF16))
        WV = _es.enter_context(nc.sbuf_tensor("WV", [128, 2 * dA], BF16))
        WO = _es.enter_context(nc.sbuf_tensor("WO", [128, 2 * dA], BF16))
        UPB = _es.enter_context(nc.sbuf_tensor("UPB", [1, D], BF16))
        W1f = _es.enter_context(nc.sbuf_tensor("W1f", [128, 2 * FH], F32))
        W1b = _es.enter_context(nc.sbuf_tensor("W1b", [128, 2 * FH], BF16))
        B1f = _es.enter_context(nc.sbuf_tensor("B1f", [1, 2 * FH], F32))
        LRWT = _es.enter_context(nc.sbuf_tensor("LRWT", [128, 4], BF16))
        LRB = _es.enter_context(nc.sbuf_tensor("LRB", [2, 1], F32))
        GWT = _es.enter_context(nc.sbuf_tensor("GWT", [128, 8 * NE], F32))
        CST = _es.enter_context(nc.sbuf_tensor("CST", [128, 8], F32))
        IDF = _es.enter_context(nc.sbuf_tensor("IDF", [128, 128], F32))
        IDB = _es.enter_context(nc.sbuf_tensor("IDB", [128, 128], BF16))
        ONES16f = _es.enter_context(nc.sbuf_tensor("ONES16f", [1, 16], F32))
        ONESCb = _es.enter_context(nc.sbuf_tensor("ONESCb", [1, 128], BF16))
        NEGU = _es.enter_context(nc.sbuf_tensor("NEGU", [16, 16], F32))
        RK = _es.enter_context(nc.sbuf_tensor("RK", [1, K], F32))
        TI = _es.enter_context(nc.sbuf_tensor("TI", [1, K], F32))
        TI15b = _es.enter_context(nc.sbuf_tensor("TI15b", [128, 1], F32))
        GB = _es.enter_context(nc.sbuf_tensor("GB", [128, 2 * FH], F32))
        BB = _es.enter_context(nc.sbuf_tensor("BB", [128, 2 * FH], F32))
        G2B = _es.enter_context(nc.sbuf_tensor("G2B", [128, 2 * FH], F32))
        PWB = _es.enter_context(nc.sbuf_tensor("PWB", [128, dA], F32))
        PBB = _es.enter_context(nc.sbuf_tensor("PBB", [128, dA], F32))
        LWB = _es.enter_context(nc.sbuf_tensor("LWB", [128, dA], F32))
        LBB = _es.enter_context(nc.sbuf_tensor("LBB", [128, dA], F32))
        SIGT = _es.enter_context(nc.sbuf_tensor("SIGT", [128, 2 * NT], F32))
        LRC = _es.enter_context(nc.sbuf_tensor("LRC", [16, 2 * NSTEP], F32))
        LETOK = _es.enter_context(nc.sbuf_tensor("LETOK", [128, 2 * NT], F32))
        LECB = _es.enter_context(nc.sbuf_tensor("LECB", [16, 2 * NSTEP], BF16))
        TOKLN = _es.enter_context(nc.sbuf_tensor("TOKLN", [16, 16], F32))
        BN6 = _es.enter_context(nc.sbuf_tensor("BN6", [16, 12], F32))
        MV = _es.enter_context(nc.sbuf_tensor("MV", [16, 4], F32))
        STD = _es.enter_context(nc.sbuf_tensor("STD", [16, 2], F32))
        RSTD = _es.enter_context(nc.sbuf_tensor("RSTD", [16, 2], F32))
        M1N = _es.enter_context(nc.sbuf_tensor("M1N", [16, 2], F32))
        M2 = _es.enter_context(nc.sbuf_tensor("M2", [16, 2], F32))
        CT1 = _es.enter_context(nc.sbuf_tensor("CT1", [16, 2 * K], F32))
        CTB = _es.enter_context(nc.sbuf_tensor("CTB", [16, 2 * K], BF16))
        TMPA = _es.enter_context(nc.sbuf_tensor("TMPA", [128, D], F32))
        TMPD = _es.enter_context(nc.sbuf_tensor("TMPD", [128, D], F32))
        TME = _es.enter_context(nc.sbuf_tensor("TME", [128, dA], F32))
        TME2 = _es.enter_context(nc.sbuf_tensor("TME2", [128, dA], F32))
        XKtmp = _es.enter_context(nc.sbuf_tensor("XKtmp", [128, dA], F32))
        TMB = _es.enter_context(nc.sbuf_tensor("TMB", [128, dA], BF16))
        PTT = _es.enter_context(nc.sbuf_tensor("PTT", [128, dA], BF16))
        STT = _es.enter_context(nc.sbuf_tensor("STT", [128, 20], F32))
        SQS = _es.enter_context(nc.sbuf_tensor("SQS", [128, 4], F32))
        GVAL = _es.enter_context(nc.sbuf_tensor("GVAL", [128, NT * NE], F32))
        PBIG0 = _es.enter_context(nc.psum_tensor("PBIG0", [128, 2048], F32))
        PBIG1 = _es.enter_context(nc.psum_tensor("PBIG1", [128, 1024], F32))
        PTRA = _es.enter_context(nc.psum_tensor("PTRA", [128, 1024], BF16))
        PTRB = _es.enter_context(nc.psum_tensor("PTRB", [128, 1024], BF16))
        _ld = _es.enter_context(nc.semaphore("s_ld"))
        _wt = _es.enter_context(nc.semaphore("s_wt"))
        _mm = _es.enter_context(nc.semaphore("s_mm"))
        _v = _es.enter_context(nc.semaphore("s_v"))
        _sc = _es.enter_context(nc.semaphore("s_sc"))
        _g = _es.enter_context(nc.semaphore("s_g"))
        _st = _es.enter_context(nc.semaphore("s_st"))
        block = _es.enter_context(nc.Block())

        ld, wt2, smm, sv, ssc, sg, sst = (Sem(_ld), Sem(_wt), Sem(_mm), Sem(_v),
                                          Sem(_sc), Sem(_g), Sem(_st))

        OUTT = TMPD
        SIGLR = WINbT[:].bitcast(F32)[0:2, 0:T]
        ROWS = VINbT[:].bitcast(F32)[0:1, 0:6 * dA]
        X2R = MTMP
        TI15 = TI15b[0:16, :]
        XHs = TMPD[0:16, 0:256]
        GYs = TMPD[0:16, 256:512]
        PRD = TMPD[0:16, 512:768]
        GRF = TMPD[0:16, 768:1024]
        GRB = TMPA[:].bitcast(BF16)[0:16, 0:256]
        Z1S = TMPA[0:16, 256:512]
        GWRK = TME2[:, 0:16]
        GEQ = TME2[:, 16:32]
        COMB = TME2[:, 32:48]
        GMAX = TME2[:, 48:49]
        RS = TME2[:, 49:50]
        WINb = [WINbT[:, 0:4096], WINbT[:, 4096:8192]]
        VINb = [VINbT[:, 0:4096], VINbT[:, 4096:8192]]
        XQT = BIGP[:, 0:2 * T]
        XKT = BIGP[:, 2 * T:4 * T]
        HT = BIGQ[:, 0:2 * T]
        XKLE = BIGQ[:, 2 * T:2 * T + NT * dA]
        WSIN = BIGP[:]
        VSIN = BIGQ[:]
        TGT2 = TGT2F[:, 0:NT * dA]
        WSOUT = TGT2F[:]
        DW = WBIGA[:, 0:8 * dA]
        UW = WBIGA[:, 8 * dA:8 * dA + 2 * D]
        WOUTb = WBIGA[:]
        ACCH = XRES[:].bitcast(F32)

        def bank(i):
            assert i < 6
            if i < 4:
                return PBIG0[:, i * 512:(i + 1) * 512]
            return PBIG1[:, (i - 4) * 512:(i - 4 + 1) * 512]

        def tap(name, src, sem=None, val=0):
            if name in taps:
                if sem is not None:
                    p.wait("gpsimd", sem, val)
                dst = taps[name]
                def fn(g, dst=dst, s=src):
                    g.dma_start(out=dst, in_=s).then_inc(sst.h, 16)
                p.op("gpsimd", fn)
                sst.val += 16

        def dma(eng, sem, dst, src, **kw):
            def fn(e, dst=dst, src=src, sem=sem, kw=kw):
                e.dma_start(out=dst, in_=src, **kw).then_inc(sem.h, 16)
            p.op(eng, fn)
            sem.val += 16

        for dst, src in [(CST[:], cst_d[:]), (IDF[:], idf_d[:]),
                         (ONES16f[:], ones16_d[:]), (NEGU[:], negu_d[:]),
                         (RK[:], rk_d[:]), (W1f[:], w1_d[:]), (B1f[:], b1_d[:]),
                         (TI[:], tki_d[:]), (DB[:], db_d[:]), (GWT[:], gwt_d[:])]:
            dma("sync", ld, dst, src)
        dma("sync", ld, LRB[:], lrb_d[:], allow_slow_non_contiguous=True)
        for i, src in enumerate([glw_d, glb_d, pw_d, pb_d, lw_d, lb_d]):
            dma("sync", ld, ROWS[:, i * dA:(i + 1) * dA][:, 0:src.shape[1]], src)
        ld_small = ld.val

        for dst, src in [(XRES[:], xin[:]), (DW, dw_d[:]), (WQ[:], wq_d[:]),
                         (WK[:], wk_d[:]), (WV[:], wv_d[:]), (WO[:], wo_d[:]),
                         (UW, uw_d[:]), (UPB[:], upb_d[:]), (LRWT[:], lrw_d[:]),
                         (IDB[:], idf_d[:]), (ONESCb[:], onesc_d[:]),
                         (W1b[:], w1_d[:])]:
            dma("gpsimd", wt2, dst, src)
        wt_attn = wt2.val

        p.wait("tensor", ld, ld_small)
        p.wait("tensor", wt2, wt_attn)
        ONES128f = ONES16f[0:1, 0:1].broadcast_to([1, 128])

        bc_dst = [GB, BB, PWB, PBB, LWB, LBB]
        bc_w = [2 * FH, 2 * FH, dA, dA, dA, dA]

        def f_bc_mm(t):
            for i in range(6):
                t.matmul(bank(i)[:, 0:bc_w[i]], ONES128f,
                         ROWS[0:1, i * dA: i * dA + bc_w[i]], start=True,
                         stop=True).then_inc(smm.h, 1)
        p.op("tensor", f_bc_mm)
        smm.val += 6
        p.wait("vector", smm, smm.val)

        def f_bc_cp(v):
            for i in range(6):
                v.tensor_copy(bc_dst[i][:, 0:bc_w[i]], bank(i)[:, 0:bc_w[i]])
            v.tensor_tensor(G2B[:], GB[:], GB[:], A.mult)
            v.tensor_tensor(TI[:], TI[:], RK[:], A.add)
            v.drain()
            v.tensor_relu(TI[:], TI[:])
            v.drain()
            v.tensor_copy(STT[0:1, 0:1], TI[0:1, 0:1]).then_inc(sv.h, 1)
        p.op("vector", f_bc_cp)
        sv.val += 1
        p.wait("tensor", sv, sv.val)

        def f_tok(t):
            t.matmul(bank(5)[0:16, 0:16], ONES16f[:], TI[:],
                     start=True, stop=True).then_inc(smm.h, 1)
        p.op("tensor", f_tok)
        smm.val += 1
        p.wait("vector", smm, smm.val)

        def f_tok2(v):
            v.tensor_tensor(TOKLN[:], bank(5)[0:16, 0:16], NEGU[:],
                            A.mult).then_inc(sv.h, 1)
        p.op("vector", f_tok2)
        sv.val += 1
        p.wait("tensor", sv, sv.val)

        def f_ti15(t):
            t.matmul(bank(5)[:, 16:17], ONES128f, TI[0:1, 15:16],
                     start=True, stop=True).then_inc(smm.h, 1)
        p.op("tensor", f_ti15)
        smm.val += 1
        p.wait("vector", smm, smm.val)

        def f_ti15c(v):
            v.tensor_copy(TI15b[:], bank(5)[:, 16:17]).then_inc(sv.h, 1)
        p.op("vector", f_ti15c)
        sv.val += 1
        ti15_done = sv.val

        def ln_stats(eng_src, width, eps_col, scrap):
            w = width

            def f1(v, src=eng_src, scrap=scrap, w=w):
                v.tensor_reduce(STT[:, 9:10], src, AX.X, A.add)
                v.tensor_copy(scrap[:, 0:w], src)
                v.tensor_tensor(scrap[:, 0:w], scrap[:, 0:w], scrap[:, 0:w],
                                A.mult)
                v.tensor_reduce(STT[:, 10:11], scrap[:, 0:w], AX.X, A.add)
                v.drain()
                v.tensor_scalar(STT[:, 12:13], STT[:, 9:10], 1.0 / w, None, A.mult)
                v.tensor_scalar(STT[:, 13:14], STT[:, 10:11], 1.0 / w, None, A.mult)
                v.drain()
                v.tensor_tensor(STT[:, 16:17], STT[:, 12:13], STT[:, 12:13], A.mult)
                v.drain()
                v.tensor_tensor(STT[:, 13:14], STT[:, 13:14], STT[:, 16:17],
                                A.subtract)
                v.drain()
                v.tensor_copy(STT[:, 8:9], STT[:, 13:14]).then_inc(sv.h, 1)
            p.op("vector", f1)
            sv.val += 1
            p.wait("scalar", sv, sv.val)

            def f2(sc, eps_col=eps_col):
                sc.activation(SQS[:, 2:3], STT[:, 13:14], AF.Sqrt,
                              bias=CST[:, eps_col:eps_col + 1], scale=1.0
                              ).then_inc(ssc.h, 1)
            p.op("scalar", f2)
            ssc.val += 1
            p.wait("vector", ssc, ssc.val)

        p.wait("vector", wt2, wt_attn)
        tr_done = {}
        p.wait("scalar", wt2, wt_attn)
        for it in range(NT if DEV_LN0_TILES is None else DEV_LN0_TILES):
            xsrc = XRES[:, it * D:(it + 1) * D]
            ln_stats(xsrc, D, 1, TMPA[:])
            if it >= 1:
                p.wait("vector", smm, tr_done[it - 1])

            def f3(v, src=xsrc):
                v.reciprocal(STT[:, 15:16], SQS[:, 2:3])
                v.drain()
                v.tensor_scalar(HE[:, 0:D], src, STT[:, 12:13], STT[:, 15:16],
                                A.subtract, A.mult).then_inc(sv.h, 1)
            p.op("vector", f3)
            sv.val += 1
            p.wait("tensor", sv, sv.val)
            if it >= 2:
                p.wait("tensor", ssc, ssc.val - 1)

            def ftr(t, it=it):
                ptr = PTRA if it % 2 == 0 else PTRB
                for c in range(8):
                    mmi = t.transpose(ptr[:, c * 128:(c + 1) * 128],
                                      HE[:, c * 128:(c + 1) * 128], IDB[:])
                    if c == 7:
                        mmi.then_inc(smm.h, 1)
            p.op("tensor", ftr)
            smm.val += 1
            tr_done[it] = smm.val
            p.wait("scalar", smm, smm.val)

            def fcp(s, it=it):
                ptr = PTRA if it % 2 == 0 else PTRB
                for c in range(8):
                    s.copy(LNT[:, c * T + it * 128: c * T + it * 128 + 128],
                           ptr[:, c * 128:(c + 1) * 128])
                s.copy(SQS[0:1, 3:4], CST[0:1, 3:4]).then_inc(ssc.h, 1)
            p.op("scalar", fcp)
            ssc.val += 1
        ln0_done_sc = ssc.val
        tap("lnt", LNT[:], ssc, ln0_done_sc)
        tap("he", HE[:, 0:D], ssc, ln0_done_sc)
        tap("stt", STT[:], ssc, ln0_done_sc)

        if STOP <= 1:
            _finish(p, sst, out_d, ACCH)
            return _close(block, p, nc)
        p.wait("tensor", ssc, ln0_done_sc)
        dp_evac = {}
        for fc in range(2):
            for jt in range(4):
                gi = fc * 4 + jt
                bk = gi % 6
                if gi >= 6:
                    p.wait("tensor", ssc, dp_evac[gi - 6])

                def fdp(t, fc=fc, jt=jt, bk=bk):
                    for dc in range(8):
                        mmi = t.matmul(
                            bank(bk),
                            DW[:, dc * dA + fc * 128: dc * dA + fc * 128 + 128],
                            LNT[:, dc * T + jt * 512: dc * T + jt * 512 + 512],
                            start=(dc == 0), stop=(dc == 7))
                        if dc == 7:
                            mmi.then_inc(smm.h, 1)
                p.op("tensor", fdp)
                smm.val += 1
                p.wait("scalar", smm, smm.val)

                def fsl(s, fc=fc, jt=jt, bk=bk):
                    s.activation(HT[:, fc * T + jt * 512: fc * T + jt * 512 + 512],
                                 bank(bk), AF.Silu, bias=DB[:, fc:fc + 1],
                                 scale=1.0).then_inc(ssc.h, 1)
                p.op("scalar", fsl)
                ssc.val += 1
                dp_evac[gi] = ssc.val
        ht_done = ssc.val
        tap("ht", HT, ssc, ht_done)

        if STOP <= 2:
            _finish(p, sst, out_d, ACCH)
            return _close(block, p, nc)
        p.wait("tensor", ssc, ht_done)
        for j in range(4):
            def flr(t, j=j):
                for dc in range(2):
                    mmi = t.matmul(bank(j)[0:2, 0:512],
                                   LRWT[:, dc * 2: dc * 2 + 2],
                                   HT[:, dc * T + j * 512: dc * T + j * 512 + 512],
                                   start=(dc == 0), stop=(dc == 1))
                    if dc == 1:
                        mmi.then_inc(smm.h, 1)
            p.op("tensor", flr)
            smm.val += 1
            p.wait("scalar", smm, smm.val)

            def fsg(s, j=j):
                s.activation(SIGLR[:, j * 512:(j + 1) * 512], bank(j)[0:2, 0:512],
                             AF.Sigmoid, bias=LRB[:], scale=1.0).then_inc(ssc.h, 1)
            p.op("scalar", fsg)
            ssc.val += 1
        siglr_done = ssc.val
        tap("siglr", SIGLR[:], ssc, siglr_done)

        p.wait("sync", ssc, siglr_done)
        dma("sync", ld, scr[:], SIGLR[:])
        p.wait("sync", ld, ld.val)
        for h in range(H):
            dma("sync", ld, SIGT[:, h * NT:(h + 1) * NT],
                scr[h * T:(h + 1) * T].rearrange("(w q) -> q w", q=128),
                allow_slow_non_contiguous=True)
            dma("sync", ld, LRC[:, h * NSTEP:(h + 1) * NSTEP],
                scr[h * T:(h + 1) * T].rearrange("(n k) -> k n", k=K),
                allow_slow_non_contiguous=True)
        sig_resh = ld.val
        tap("sigt", SIGT[:], ld, sig_resh)
        tap("lrc", LRC[:], ld, sig_resh)

        p.wait("vector", ld, sig_resh)
        p.wait("vector", sv, ti15_done)

        def fle(v):
            v.tensor_scalar(LETOK[:], SIGT[:], TI15b[:], 1.0 / 128.0, A.mult, A.mult)
            v.tensor_scalar(LECB[:], LRC[:], TI15[:], 1.0 / 128.0, A.mult,
                            A.mult).then_inc(sv.h, 1)
        p.op("vector", fle)
        sv.val += 1
        letok_done = sv.val

        if STOP <= 3:
            _finish(p, sst, out_d, ACCH)
            return _close(block, p, nc)
        p.wait("tensor", ssc, siglr_done)
        pj_evac = {}
        for wi_, (wsrc, dstt) in enumerate(((WQ, XQT), (WK, XKT))):
            for fc in range(2):
                for jt in range(4):
                    gi = wi_ * 8 + fc * 4 + jt
                    bk = gi % 6
                    if gi >= 6:
                        p.wait("tensor", sv, pj_evac[gi - 6])

                    def fpj(t, wsrc=wsrc, fc=fc, jt=jt, bk=bk):
                        for dc in range(2):
                            mmi = t.matmul(
                                bank(bk),
                                wsrc[:, dc * dA + fc * 128: dc * dA + fc * 128 + 128],
                                HT[:, dc * T + jt * 512: dc * T + jt * 512 + 512],
                                start=(dc == 0), stop=(dc == 1))
                            if dc == 1:
                                mmi.then_inc(smm.h, 1)
                    p.op("tensor", fpj)
                    smm.val += 1
                    p.wait("vector", smm, smm.val)

                    def fev(v, dstt=dstt, fc=fc, jt=jt, bk=bk):
                        v.tensor_copy(dstt[:, fc * T + jt * 512: fc * T + jt * 512 + 512],
                                      bank(bk)).then_inc(sv.h, 1)
                    p.op("vector", fev)
                    sv.val += 1
                    pj_evac[gi] = sv.val
        proj_fmaj_done = sv.val
        tap("xqt", XQT, sv, proj_fmaj_done)
        tap("xkt", XKT, sv, proj_fmaj_done)

        if STOP <= 4:
            _finish(p, sst, out_d, ACCH)
            return _close(block, p, nc)
        p.wait("vector", sv, letok_done)
        p.wait("tensor", sv, proj_fmaj_done)
        tokmaj_evac = {}
        for it in range(NT):
            bkk = it % 2
            bkv = 2 + it % 2
            if it >= 2:
                p.wait("tensor", sv, tokmaj_evac[it - 2])

            def fkv(t, it=it, bkk=bkk, bkv=bkv):
                for dc in range(2):
                    t.matmul(bank(bkk)[:, 0:256],
                             HT[:, dc * T + it * 128: dc * T + it * 128 + 128],
                             WK[:, dc * dA: dc * dA + dA],
                             start=(dc == 0), stop=(dc == 1))
                for dc in range(2):
                    mmi = t.matmul(bank(bkv)[:, 0:256],
                                   HT[:, dc * T + it * 128: dc * T + it * 128 + 128],
                                   WV[:, dc * dA: dc * dA + dA],
                                   start=(dc == 0), stop=(dc == 1))
                    if dc == 1:
                        mmi.then_inc(smm.h, 1)
            p.op("tensor", fkv)
            smm.val += 1
            p.wait("scalar", smm, smm.val)

            def fxk(s, bkk=bkk):
                s.copy(XKtmp[:], bank(bkk)[:, 0:256]).then_inc(ssc.h, 1)
            p.op("scalar", fxk)
            ssc.val += 1
            p.wait("vector", ssc, ssc.val)

            def fxv(v, it=it, bkv=bkv):
                for h in range(H):
                    v.tensor_scalar(XKLE[:, it * dA + h * FH: it * dA + h * FH + FH],
                                    XKtmp[:, h * FH:(h + 1) * FH],
                                    LETOK[:, h * NT + it: h * NT + it + 1], None,
                                    A.mult)
                v.tensor_tensor(TME[:], XKtmp[:], bank(bkv)[:, 0:256], A.subtract)
                v.tensor_tensor(TME[:], TME[:], BB[:], A.add)
                v.tensor_tensor(TGT2[:, it * dA:(it + 1) * dA], TME[:], GB[:],
                                A.mult).then_inc(sv.h, 1)
            p.op("vector", fxv)
            sv.val += 1
            tokmaj_evac[it] = sv.val
        tokmaj_done = sv.val
        tap("xkle", XKLE, sv, tokmaj_done)
        tap("xktmp", XKtmp[:], sv, tokmaj_done)
        tap("tgt2", TGT2[:], sv, tokmaj_done)

        if STOP <= 5:
            _finish(p, sst, out_d, ACCH)
            return _close(block, p, nc)
        p.wait("tensor", sv, tokmaj_done)
        p.wait("tensor", sv, proj_fmaj_done)

        zbar_done = {}
        cast_done = {}
        grad_ready = {}
        upd_done = {}
        mm_all = {}

        tgt_pf = {}
        xkle_pf = {}
        stash_ld = {}
        NS_ = NSTEP if DEV_SCAN_STEPS is None else DEV_SCAN_STEPS
        for n in range(NS_):
            q = n % 2
            bA, bB, bC = bank(q), bank(2 + q), bank(4 + q)
            itile, r0 = n // 8, (n % 8) * 16
            slot = n % 4

            p.wait("sync", sv, tokmaj_evac[n // 8])
            if n >= 4:
                p.wait("sync", sv, grad_ready[n - 4])
                p.wait("sync", smm, mm_all[n - 4])
            dma("sync", ld, TGT16R[:, slot * dA:(slot + 1) * dA],
                TGT2[r0:r0 + 16, itile * dA:(itile + 1) * dA])
            tgt_pf[n] = ld.val
            dma("sync", ld, XKLE16R[:, slot * dA:(slot + 1) * dA],
                XKLE[r0:r0 + 16, itile * dA:(itile + 1) * dA])
            xkle_pf[n] = ld.val

            if n >= 1:
                p.wait("tensor", ssc, cast_done[n - 1])
            if n >= 2:
                p.wait("tensor", sv, upd_done[n - 2])
                p.wait("tensor", ssc, zbar_done[n - 2])
            p.wait("tensor", ld, xkle_pf[n])

            def f_z1(t, n=n, bA=bA):
                for h in range(H):
                    t.matmul(bA[0:16, h * FH:(h + 1) * FH],
                             XKT[:, h * T + n * K: h * T + n * K + K],
                             W1b[:, h * FH:(h + 1) * FH], start=True, stop=False)
                    t.matmul(bA[0:16, h * FH:(h + 1) * FH], ONES16f[:],
                             B1f[0:1, h * FH:(h + 1) * FH], start=False, stop=True)
                for h in range(H):
                    mmi = t.matmul(bA[32:48, h * K:(h + 1) * K],
                                   XKT[:, h * T + n * K: h * T + n * K + K],
                                   XQT[:, h * T + n * K: h * T + n * K + K],
                                   start=True, stop=True)
                    if h == 1:
                        mmi.then_inc(smm.h, 1)
            p.op("tensor", f_z1)
            smm.val += 1

            mmz = smm.val
            p.wait("vector", smm, mmz)

            def f_sum(v, bA=bA):
                v.tensor_copy(Z1S[:], bA[0:16, 0:256])
                for h in range(H):
                    v.tensor_reduce(MV[:, h:h + 1], Z1S[:, h * FH:(h + 1) * FH],
                                    AX.X, A.add)
                    v.tensor_tensor(PRD[:, h * FH:(h + 1) * FH],
                                    Z1S[:, h * FH:(h + 1) * FH],
                                    Z1S[:, h * FH:(h + 1) * FH], A.mult)
                    v.tensor_reduce(MV[:, 2 + h:3 + h], PRD[:, h * FH:(h + 1) * FH],
                                    AX.X, A.add)
                v.drain()
                v.tensor_scalar(MV[:, 0:2], MV[:, 0:2], 1.0 / 128.0, None, A.mult)
                v.tensor_scalar(MV[:, 2:4], MV[:, 2:4], 1.0 / 128.0, None, A.mult)
                v.drain()
                v.tensor_tensor(STD[:], MV[:, 0:2], MV[:, 0:2], A.mult)
                v.drain()
                v.tensor_tensor(STD[:], MV[:, 2:4], STD[:], A.subtract)
                v.drain()
                v.tensor_copy(STT[0:1, 7:8], CST[0:1, 3:4]).then_inc(sv.h, 1)
            p.op("vector", f_sum)
            sv.val += 1
            p.wait("scalar", sv, sv.val)

            def f_sq(s):
                s.activation(SQS[0:16, 2:4], STD[:], AF.Sqrt, bias=CST[0:16, 0:1],
                             scale=1.0).then_inc(ssc.h, 1)
            p.op("scalar", f_sq)
            ssc.val += 1
            p.wait("vector", ssc, ssc.val)
            p.wait("vector", ld, tgt_pf[n])

            def f_grad(v, n=n, bA=bA, slot=slot):
                v.reciprocal(RSTD[:], SQS[0:16, 2:4])
                v.drain()
                for h in range(H):
                    v.tensor_scalar(XHs[:, h * FH:(h + 1) * FH],
                                    Z1S[:, h * FH:(h + 1) * FH],
                                    MV[:, h: h + 1], RSTD[:, h: h + 1],
                                    A.subtract, A.mult)
                for h in range(H):
                    v.tensor_scalar(CT1[:, h * K:(h + 1) * K],
                                    bA[32:48, h * K:(h + 1) * K], 1.0, None, A.add)
                v.drain()
                v.tensor_tensor(CT1[:, 0:K], CT1[:, 0:K], TOKLN[:], A.mult)
                v.tensor_tensor(CT1[:, K:2 * K], CT1[:, K:2 * K], TOKLN[:], A.mult)
                v.drain()
                for h in range(H):
                    v.tensor_scalar(CTB[:, h * K:(h + 1) * K],
                                    CT1[:, h * K:(h + 1) * K],
                                    LRC[:, h * NSTEP + n: h * NSTEP + n + 1], None,
                                    A.mult)
                v.tensor_tensor(GYs[:], XHs[:], G2B[0:16, :], A.mult)
                v.tensor_tensor(GYs[:], GYs[:],
                                TGT16R[:, slot * dA:(slot + 1) * dA], A.add)
                v.tensor_tensor(PRD[:], GYs[:], XHs[:], A.mult)
                for h in range(H):
                    v.tensor_reduce(M1N[:, h:h + 1], PRD[:, h * FH:(h + 1) * FH],
                                    AX.X, A.add, negate=True)
                v.drain()
                v.tensor_scalar(M1N[:], M1N[:], 1.0 / 128.0, None, A.mult)
                v.drain()
                for h in range(H):
                    v.tensor_scalar(PRD[:, h * FH:(h + 1) * FH],
                                    GYs[:, h * FH:(h + 1) * FH],
                                    1.0 / 128.0, 0.0, A.mult, A.add,
                                    accum_out=M2[:, h:h + 1])
                for h in range(H):
                    v.scalar_tensor_tensor(GRF[:, h * FH:(h + 1) * FH],
                                           XHs[:, h * FH:(h + 1) * FH],
                                           M1N[:, h:h + 1],
                                           GYs[:, h * FH:(h + 1) * FH],
                                           A.mult, A.add)
                v.drain()
                for h in range(H):
                    mmo = v.tensor_scalar(GRB[:, h * FH:(h + 1) * FH],
                                          GRF[:, h * FH:(h + 1) * FH],
                                          M2[:, h:h + 1], RSTD[:, h:h + 1],
                                          A.subtract, A.mult)
                    if h == 1:
                        mmo.then_inc(sv.h, 1)
            p.op("vector", f_grad)
            sv.val += 1
            grad_ready[n] = sv.val

            p.wait("tensor", sv, grad_ready[n])

            def f_zb(t, n=n, bA=bA, bB=bB, bC=bC, slot=slot):
                for h in range(H):
                    t.matmul(bB[0:16, h * FH:(h + 1) * FH],
                             XQT[:, h * T + n * K: h * T + n * K + K],
                             W1b[:, h * FH:(h + 1) * FH], start=True, stop=False)
                    t.matmul(bB[0:16, h * FH:(h + 1) * FH], ONES16f[:],
                             B1f[0:1, h * FH:(h + 1) * FH], start=False, stop=False)
                    t.matmul(bB[0:16, h * FH:(h + 1) * FH],
                             CTB[:, h * K:(h + 1) * K],
                             GRB[:, h * FH:(h + 1) * FH], start=False, stop=True)
                for h in range(H):
                    t.matmul(bC[0:128, h * FH:(h + 1) * FH],
                             XKLE16R[:, slot * dA + h * FH: slot * dA + (h + 1) * FH],
                             GRB[:, h * FH:(h + 1) * FH], start=True, stop=True)
                for h in range(H):
                    mmi = t.matmul(bA[64:65, 256 + h * FH: 256 + (h + 1) * FH],
                                   LECB[:, h * NSTEP + n: h * NSTEP + n + 1],
                                   GRB[:, h * FH:(h + 1) * FH], start=True, stop=True)
                    if h == 1:
                        mmi.then_inc(smm.h, 1)
            p.op("tensor", f_zb)
            smm.val += 1
            mm_all[n] = smm.val

            p.wait("scalar", smm, mm_all[n])
            if n >= 2:
                p.wait("scalar", ld, stash_ld[n - 2])

            def f_zc(s, bB=bB, n=n):
                s.copy(ZB16[:, (n % 2) * dA:(n % 2 + 1) * dA],
                       bB[0:16, 0:256]).then_inc(ssc.h, 1)
            p.op("scalar", f_zc)
            ssc.val += 1
            zbar_done[n] = ssc.val
            p.wait("sync", ssc, zbar_done[n])
            dma("sync", ld, TGT2[r0:r0 + 16, itile * dA:(itile + 1) * dA],
                ZB16[:, (n % 2) * dA:(n % 2 + 1) * dA])
            stash_ld[n] = ld.val

            p.wait("vector", smm, mm_all[n])

            def f_upd(v, bA=bA, bC=bC):
                v.tensor_tensor(W1f[:], W1f[:], bC[0:128, 0:256], A.subtract)
                v.tensor_tensor(B1f[:], B1f[:], bA[64:65, 256:512],
                                A.subtract).then_inc(sv.h, 1)
            p.op("vector", f_upd)
            sv.val += 1
            upd_done[n] = sv.val

            p.wait("scalar", sv, upd_done[n])

            def f_cast(s):
                s.copy(W1b[:], W1f[:]).then_inc(ssc.h, 1)
            p.op("scalar", f_cast)
            ssc.val += 1
            cast_done[n] = ssc.val

        scan_v, scan_sc = sv.val, ssc.val
        p.wait("gpsimd", ld, stash_ld[NS_ - 1])
        tap("zbar", TGT2[:], ssc, scan_sc)
        tap("w1f", W1f[:], ssc, scan_sc)
        tap("mv", MV[:], ssc, scan_sc)
        tap("rstd", RSTD[:], ssc, scan_sc)
        tap("grb", GRB, ssc, scan_sc)
        tap("xhs", XHs, ssc, scan_sc)
        tap("gys", GYs, ssc, scan_sc)
        tap("ctb", CTB[:], ssc, scan_sc)
        tap("m1n", M1N[:], ssc, scan_sc)
        tap("m2", M2[:], ssc, scan_sc)
        tap("std", STD[:], ssc, scan_sc)
        tap("sqs", SQS[:], ssc, scan_sc)
        tap("b1f", B1f[:], ssc, scan_sc)

        if STOP <= 6:
            _finish(p, sst, out_d, ACCH)
            return _close(block, p, nc)
        for eng in ("tensor", "vector", "scalar"):
            p.wait(eng, ssc, ssc.val)
            p.wait(eng, sv, sv.val)
            p.wait(eng, smm, smm.val)
            p.wait(eng, ld, ld.val)
            p.wait(eng, wt2, wt2.val)
        for it, dummy in [(0, True)] + [(i, False) for i in range(NT)]:
            b0, b1_, b3 = 0, 0, 1
            p.wait("tensor", sv, sv.val)

            def fq(t, it=it, b0=b0):
                for dc in range(2):
                    mmi = t.matmul(bank(b0)[:, 0:256],
                                   HT[:, dc * T + it * 128: dc * T + it * 128 + 128],
                                   WQ[:, dc * dA: dc * dA + dA],
                                   start=(dc == 0), stop=(dc == 1))
                    if dc == 1:
                        mmi.then_inc(smm.h, 1)
            p.op("tensor", fq)
            smm.val += 1

            zsrc = TGT2[:, it * dA:(it + 1) * dA]

            def fz1(v, zsrc=zsrc):
                v.tensor_reduce(STT[:, 0:1], zsrc[:, 0:128], AX.X, A.add)
                v.tensor_reduce(STT[:, 1:2], zsrc[:, 128:256], AX.X, A.add)
                v.tensor_tensor(TME2[:], zsrc, zsrc, A.mult)
                v.tensor_reduce(STT[:, 2:3], TME2[:, 0:128], AX.X, A.add)
                v.tensor_reduce(STT[:, 3:4], TME2[:, 128:256], AX.X, A.add)
                v.tensor_scalar(STT[:, 12:14], STT[:, 0:2], 1.0 / 128.0, None,
                                A.mult)
                v.tensor_scalar(STT[:, 2:4], STT[:, 2:4], 1.0 / 128.0, None,
                                A.mult)
                v.tensor_tensor(STT[:, 4:6], STT[:, 12:14], STT[:, 12:14], A.mult)
                v.tensor_tensor(STT[:, 2:4], STT[:, 2:4], STT[:, 4:6],
                                A.subtract).then_inc(sv.h, 1)
            p.op("vector", fz1)
            sv.val += 1
            p.wait("scalar", sv, sv.val)

            def fz2(s):
                s.activation(SQS[:, 2:4], STT[:, 2:4], AF.Sqrt,
                             bias=CST[:, 0:1], scale=1.0).then_inc(ssc.h, 1)
            p.op("scalar", fz2)
            ssc.val += 1
            p.wait("vector", ssc, ssc.val)
            p.wait("vector", smm, smm.val)
            def fz3(v, zsrc=zsrc, b0=b0):
                v.reciprocal(STT[:, 16:18], SQS[:, 2:4])
                v.drain()
                v.tensor_scalar(TME[:, 0:128], zsrc[:, 0:128], STT[:, 12:13],
                                STT[:, 16:17], A.subtract, A.mult)
                v.tensor_scalar(TME[:, 128:256], zsrc[:, 128:256], STT[:, 13:14],
                                STT[:, 17:18], A.subtract, A.mult)
                v.tensor_tensor(TME[:], TME[:], GB[:], A.mult)
                v.tensor_tensor(TME[:], TME[:], BB[:], A.add)
                v.tensor_tensor(TME[:], TME[:], bank(b0)[:, 0:256], A.add
                                ).then_inc(sv.h, 1)
            p.op("vector", fz3)
            sv.val += 1

            ln_stats(TME[:], dA, 1, TME2[:])
            def fp3(v):
                v.reciprocal(STT[:, 15:16], SQS[:, 2:3])
                v.drain()
                v.tensor_scalar(TME2[:], TME[:], STT[:, 12:13], STT[:, 15:16],
                                A.subtract, A.mult)
                v.tensor_tensor(TME2[:], TME2[:], PWB[:], A.mult)
                v.tensor_tensor(TMB[:], TME2[:], PBB[:], A.add).then_inc(sv.h, 1)
            p.op("vector", fp3)
            sv.val += 1
            p.wait("tensor", sv, sv.val)

            def ftw(t):
                t.transpose(PTRA[:, 0:128], TMB[:, 0:128], IDB[:])
                t.transpose(PTRA[:, 128:256], TMB[:, 128:256], IDB[:]
                            ).then_inc(smm.h, 1)
            p.op("tensor", ftw)
            smm.val += 1
            p.wait("scalar", smm, smm.val)
            def fpc(s):
                s.copy(PTT[:, 0:128], PTRA[:, 0:128])
                s.copy(PTT[:, 128:256], PTRA[:, 128:256]).then_inc(ssc.h, 1)
            p.op("scalar", fpc)
            ssc.val += 1
            p.wait("tensor", ssc, ssc.val)
            def fwo(t):
                for dc in range(2):
                    mmi = t.matmul(bank(0)[:, 256:512], PTT[:, dc * 128:(dc + 1) * 128],
                                   WO[:, dc * dA: dc * dA + dA],
                                   start=(dc == 0), stop=(dc == 1))
                    if dc == 1:
                        mmi.then_inc(smm.h, 1)
            p.op("tensor", fwo)
            smm.val += 1
            p.wait("vector", smm, smm.val)

            ln_stats(bank(0)[:, 256:512], dA, 1, TME2[:])
            def fl3(v):
                v.reciprocal(STT[:, 15:16], SQS[:, 2:3])
                v.drain()
                v.tensor_scalar(TME2[:], bank(0)[:, 256:512], STT[:, 12:13],
                                STT[:, 15:16], A.subtract, A.mult)
                v.tensor_tensor(TME2[:], TME2[:], LWB[:], A.mult)
                v.tensor_tensor(TMB[:], TME2[:], LBB[:], A.add).then_inc(sv.h, 1)
            p.op("vector", fl3)
            sv.val += 1
            p.wait("tensor", sv, sv.val)

            def ftu(t):
                t.transpose(PTRB[:, 0:128], TMB[:, 0:128], IDB[:])
                t.transpose(PTRB[:, 128:256], TMB[:, 128:256], IDB[:]
                            ).then_inc(smm.h, 1)
            p.op("tensor", ftu)
            smm.val += 1
            p.wait("scalar", smm, smm.val)
            def fpc2(s):
                s.copy(PTT[:, 0:128], PTRB[:, 0:128])
                s.copy(PTT[:, 128:256], PTRB[:, 128:256]).then_inc(ssc.h, 1)
            p.op("scalar", fpc2)
            ssc.val += 1
            p.wait("tensor", ssc, ssc.val)
            upps = PBIG1[:, 0:1024]
            def fup(t, upps=upps):
                for j in range(2):
                    for dc in range(2):
                        t.matmul(upps[:, j * 512:(j + 1) * 512],
                                 PTT[:, dc * 128:(dc + 1) * 128],
                                 UW[:, dc * D + j * 512: dc * D + j * 512 + 512],
                                 start=(dc == 0), stop=False)
                    mmi = t.matmul(upps[:, j * 512:(j + 1) * 512], ONESCb[:],
                                   UPB[0:1, j * 512:(j + 1) * 512],
                                   start=False, stop=True)
                    if j == 1:
                        mmi.then_inc(smm.h, 1)
            p.op("tensor", fup)
            smm.val += 1
            p.wait("vector", smm, smm.val)

            ln_stats(upps, D, 1, TMPA[:])
            xres_t = XRES[:, it * D:(it + 1) * D]
            def fx2(v, upps=upps, xres_t=xres_t, dummy=dummy):
                v.reciprocal(STT[:, 15:16], SQS[:, 2:3])
                v.drain()
                v.tensor_scalar(TMPA[:], upps, STT[:, 12:13], STT[:, 15:16],
                                A.subtract, A.mult)
                v.tensor_tensor(TMPD[:], xres_t, TMPA[:], A.add)
                if dummy:
                    v.tensor_copy(STT[:, 9:10], TMPD[:, 0:1]).then_inc(sv.h, 1)
                else:
                    v.tensor_copy(xres_t, TMPD[:]).then_inc(sv.h, 1)
            p.op("vector", fx2)
            sv.val += 1

            ln_stats(TMPD[:], D, 1, TMPA[:])
            def fl33(v):
                v.reciprocal(STT[:, 15:16], SQS[:, 2:3])
                v.drain()
                v.tensor_scalar(TMPA[:], TMPD[:], STT[:, 12:13], STT[:, 15:16],
                                A.subtract, A.mult).then_inc(sv.h, 1)
            p.op("vector", fl33)
            sv.val += 1
            p.wait("tensor", sv, sv.val)
            tb = PBIG0[:, 1024:2048]
            def ftr3(t, tb=tb):
                for c in range(8):
                    mmi = t.transpose(tb[:, c * 128:(c + 1) * 128],
                                      TMPA[:, c * 128:(c + 1) * 128], IDF[:])
                    if c == 7:
                        mmi.then_inc(smm.h, 1)
            p.op("tensor", ftr3)
            smm.val += 1
            p.wait("scalar", smm, smm.val)
            def fcp3(s, tb=tb):
                for c in range(8):
                    s.copy(TMPD[:, c * 128:(c + 1) * 128], tb[:, c * 128:(c + 1) * 128])
                s.copy(SQS[0:1, 3:4], CST[0:1, 3:4]).then_inc(ssc.h, 1)
            p.op("scalar", fcp3)
            ssc.val += 1
            p.wait("vector", ssc, ssc.val)
            p.wait("tensor", ssc, ssc.val)
            def fcst(v, it=it):
                v.tensor_copy(LNT[:].rearrange("p (c t) -> p c t", c=8)
                              [:, :, it * 128:(it + 1) * 128],
                              TMPD[:].rearrange("p (c t) -> p c t", c=8)
                              ).then_inc(sv.h, 1)
            p.op("vector", fcst)
            sv.val += 1
            gps = bank(2)
            def fgm(t, gps=gps):
                for dc in range(8):
                    mmi = t.matmul(gps[0:128, 0:16], TMPD[:, dc * 128:(dc + 1) * 128],
                                   GWT[:, dc * NE:(dc + 1) * NE],
                                   start=(dc == 0), stop=(dc == 7))
                    if dc == 7:
                        mmi.then_inc(smm.h, 1)
            p.op("tensor", fgm)
            smm.val += 1
            p.wait("scalar", smm, smm.val)
            def fsig(s, it=it, gps=gps):
                s.activation(GVAL[:, it * NE:(it + 1) * NE], gps[0:128, 0:16],
                             AF.Sigmoid, bias=0.0, scale=1.0).then_inc(ssc.h, 1)
            p.op("scalar", fsig)
            ssc.val += 1

            p.wait("vector", ssc, ssc.val)
            def ftk(v, it=it):
                g0 = GVAL[:, it * NE:(it + 1) * NE]
                v.tensor_copy(GWRK[:], g0)
                v.drain()
                for _ in range(4):
                    v.reduce_max(GMAX[:], GWRK[:], AX.X)
                    v.drain()
                    v.tensor_scalar(GEQ[:], GWRK[:], GMAX[:], None, A.is_equal)
                    v.drain()
                    v.scalar_tensor_tensor(GWRK[:], GEQ[:], -1e30, GWRK[:],
                                           A.mult, A.add)
                    v.drain()
                v.tensor_tensor(COMB[:], GWRK[:], g0, A.not_equal)
                v.drain()
                v.tensor_tensor(COMB[:], COMB[:], g0, A.mult)
                v.drain()
                v.tensor_reduce(RS[:], COMB[:], AX.X, A.add)
                v.drain()
                v.reciprocal(RS[:], RS[:])
                v.drain()
                v.tensor_scalar(COMB[:], COMB[:], RS[:], None, A.mult
                                ).then_inc(sv.h, 1)
            p.op("vector", ftk)
            sv.val += 1
            p.wait("tensor", sv, sv.val)
            def ftc(t):
                t.transpose(bank(2)[0:16, 128:256], COMB[:], IDF[:]).then_inc(smm.h, 1)
            p.op("tensor", ftc)
            smm.val += 1
            p.wait("vector", smm, smm.val)
            def fcc(v, it=it):
                v.tensor_copy(COMBT[:, it * 128:(it + 1) * 128],
                              bank(2)[0:16, 128:256]).then_inc(sv.h, 1)
            p.op("vector", fcc)
            sv.val += 1
        post_v, post_sc = sv.val, ssc.val
        tap("x2", XRES[:], sv, post_v)
        tap("tme", TME[:], sv, post_v)
        tap("tmpa", TMPA[:], sv, post_v)
        tap("tmpd", TMPD[:], sv, post_v)
        tap("ptt", PTT[:], sv, post_v)
        tap("lnt3", LNT[:], sv, post_v)
        tap("gval", GVAL[:], sv, post_v)
        tap("combt", COMBT[:], sv, post_v)

        p.wait("sync", sv, post_v)
        dma("sync", ld, scr2[:], XRES[:])
        x2_spill = ld.val

        if STOP <= 7:
            _finish(p, sst, out_d, ACCH)
            return _close(block, p, nc)
        p.wait("gpsimd", sv, post_v)
        dma("gpsimd", wt2, WSIN, wsin_d[:])
        dma("gpsimd", wt2, VSIN, vsin_d[:])
        dma("gpsimd", wt2, WSOUT, wsout_d[:])
        JOBS = []
        for pa in range(2):
            JOBS.append((pa, "sh", 0))
            JOBS.append((pa, "sh", 1))
            for e in range(NE):
                JOBS.append((pa, "e", e))
        ejobs = [ji for ji, j in enumerate(JOBS) if j[1] == "e"]
        eidx = {ji: k for k, ji in enumerate(ejobs)}
        winv_ready = {}
        wout_ready = {}
        in_done = {}
        out_done = {}
        for k in range(2):
            e = JOBS[ejobs[k]][2]
            dma("gpsimd", wt2, WINb[k % 2], ewin_d[e])
            dma("gpsimd", wt2, VINb[k % 2], evin_d[e])
            winv_ready[ejobs[k]] = wt2.val
        wt_shared_all = wt2.val

        last_facc = {}
        final_last_v = 0
        out_stores = 0
        p.wait("vector", ld, x2_spill)

        for ji, (pa, kind, e) in enumerate(JOBS):
            t0 = pa * TP
            first = (kind == "sh" and e == 0)
            hc0 = 4 * e if kind == "sh" else 0

            if ji == 0:
                p.wait("tensor", wt2, wt_shared_all)
            if pa == 1 and kind == "sh" and e == 0:
                p.wait("tensor", sv, final_last_v)
            if ji > 0:
                p.wait("tensor", sv, last_facc[ji - 1])
            if kind == "e":
                p.wait("tensor", wt2, winv_ready[ji])
                win_s, vin_s = WINb[eidx[ji] % 2], VINb[eidx[ji] % 2]
            else:
                win_s, vin_s = WSIN, VSIN

            if kind == "e":
                p.wait("sync", smm, smm.val)
                dma("sync", ld, COMBROW[:], COMBT[e:e + 1, t0: t0 + TP])
                p.wait("tensor", ld, ld.val)

                def fcb(t):
                    t.matmul(bank(2)[:, 0:512], ONESCb[:],
                             COMBROW[0:1, 0:512], start=True, stop=True)
                    t.matmul(bank(3)[:, 0:512], ONESCb[:],
                             COMBROW[0:1, 512:1024], start=True,
                             stop=True).then_inc(smm.h, 1)
                p.op("tensor", fcb)
                smm.val += 1
                p.wait("vector", smm, smm.val)

                def fcbc(v):
                    v.tensor_copy(CBX[:, 0:512], bank(2)[:, 0:512])
                    v.tensor_copy(CBX[:, 512:1024], bank(3)[:, 0:512]
                                  ).then_inc(sv.h, 1)
                p.op("vector", fcbc)
                sv.val += 1
                p.wait("tensor", sv, sv.val)

            evac_vals = {}
            for hc in range(4):
                hcw = hc0 + hc
                hb = PBIG0[:, (hc % 2) * 1024:(hc % 2) * 1024 + 1024]
                vb = PBIG1[:, 0:1024]
                if hc >= 1:
                    p.wait("tensor", sv, evac_vals[hc - 1])

                def fip(t, win_s=win_s, vin_s=vin_s, hcw=hcw, hb=hb, vb=vb,
                        t0=t0, kind=kind):
                    for dc in range(8):
                        if kind == "sh":
                            lw_ = win_s[:, dc * SHID + hcw * 128:
                                        dc * SHID + hcw * 128 + 128]
                        else:
                            lw_ = win_s[:, dc * EH + hcw * 128:
                                        dc * EH + hcw * 128 + 128]
                        for j in range(2):
                            t.matmul(hb[:, j * 512:(j + 1) * 512], lw_,
                                     LNT[:, dc * T + t0 + j * 512:
                                         dc * T + t0 + j * 512 + 512],
                                     start=(dc == 0), stop=(dc == 7))
                    for dc in range(8):
                        if kind == "sh":
                            lv_ = vin_s[:, dc * SHID + hcw * 128:
                                        dc * SHID + hcw * 128 + 128]
                        else:
                            lv_ = vin_s[:, dc * EH + hcw * 128:
                                        dc * EH + hcw * 128 + 128]
                        for j in range(2):
                            mmi = t.matmul(vb[:, j * 512:(j + 1) * 512], lv_,
                                           LNT[:, dc * T + t0 + j * 512:
                                               dc * T + t0 + j * 512 + 512],
                                           start=(dc == 0), stop=(dc == 7))
                            if dc == 7 and j == 1:
                                mmi.then_inc(smm.h, 1)
                p.op("tensor", fip)
                smm.val += 1
                p.wait("scalar", smm, smm.val)

                def fsi(s, hb=hb):
                    s.activation(MTMP[:], hb, AF.Silu, bias=0.0, scale=1.0
                                 ).then_inc(ssc.h, 1)
                p.op("scalar", fsi)
                ssc.val += 1
                p.wait("vector", ssc, ssc.val)

                def fhm(v, vb=vb, hc=hc, kind=kind):
                    if kind == "e":
                        v.tensor_tensor(HE[:, hc * TP:(hc + 1) * TP], MTMP[:], vb,
                                        A.mult)
                        v.tensor_tensor(HE[:, hc * TP:(hc + 1) * TP],
                                        HE[:, hc * TP:(hc + 1) * TP],
                                        CBX[:, 0:TP], A.mult).then_inc(sv.h, 1)
                    else:
                        v.tensor_tensor(HE[:, hc * TP:(hc + 1) * TP], MTMP[:], vb,
                                        A.mult).then_inc(sv.h, 1)
                p.op("vector", fhm)
                sv.val += 1
                evac_vals[hc] = sv.val
            in_done[ji] = smm.val
            h_ready = sv.val
            p.wait("tensor", sv, h_ready)

            if kind == "e":
                k = eidx[ji]
                if ji - 1 in out_done:
                    p.wait("gpsimd", smm, out_done[ji - 1])
                dma("gpsimd", wt2, WOUTb, ewout_d[e])
                wout_ready[ji] = wt2.val
                if k + 2 < len(ejobs):
                    p.wait("gpsimd", smm, in_done[ji])
                    e2 = JOBS[ejobs[k + 2]][2]
                    dma("gpsimd", wt2, WINb[(k + 2) % 2], ewin_d[e2])
                    dma("gpsimd", wt2, VINb[(k + 2) % 2], evin_d[e2])
                    winv_ready[ejobs[k + 2]] = wt2.val

            if kind == "e":
                p.wait("tensor", wt2, wout_ready[ji])
            facc_vals = {}
            for dc in range(8):
                ob = PBIG0[:, 0:1024] if dc % 2 == 0 else PBIG1[:, 0:1024]
                if dc >= 2:
                    p.wait("tensor", sv, facc_vals[dc - 2])

                def fop(t, dc=dc, ob=ob, hc0=hc0, kind=kind):
                    for hc in range(4):
                        hcw = hc0 + hc
                        if kind == "sh":
                            lhs = WSOUT[:, hcw * D + dc * 128:
                                        hcw * D + dc * 128 + 128]
                        else:
                            lhs = WOUTb[:, hc * D + dc * 128:
                                        hc * D + dc * 128 + 128]
                        for j in range(2):
                            mmi = t.matmul(ob[:, j * 512:(j + 1) * 512], lhs,
                                           HE[:, hc * TP + j * 512:
                                               hc * TP + j * 512 + 512],
                                           start=(hc == 0), stop=(hc == 3))
                            if hc == 3 and j == 1:
                                mmi.then_inc(smm.h, 1)
                p.op("tensor", fop)
                smm.val += 1
                p.wait("vector", smm, smm.val)

                def facc(v, dc=dc, ob=ob, first=first):
                    dst = ACCH[:, dc * TP:(dc + 1) * TP]
                    if first:
                        v.tensor_copy(dst, ob).then_inc(sv.h, 1)
                    else:
                        v.tensor_tensor(dst, dst, ob, A.add).then_inc(sv.h, 1)
                p.op("vector", facc)
                sv.val += 1
                facc_vals[dc] = sv.val
            out_done[ji] = smm.val
            last_facc[ji] = sv.val

            if kind == "e" and e == NE - 1:
                moe_done_v = sv.val
                if pa == 0:
                    p.wait("sync", ld, x2_spill)
                for itl, fdummy in [(0, True)] + [(i, False) for i in range(8)]:
                    itg = pa * 8 + itl
                    pi = itl % 2
                    p.wait("sync", sv, sv.val)
                    dma("sync", ld, X2R[:], scr2[:, itg * D:(itg + 1) * D])
                    x2r_done = ld.val
                    p.wait("tensor", sv, sv.val)
                    tbf = PBIG0[:, pi * 1024:(pi + 1) * 1024]

                    def ftf(t, itl=itl, tbf=tbf):
                        for c in range(8):
                            mmi = t.transpose(tbf[:, c * 128:(c + 1) * 128],
                                              ACCH[:, c * TP + itl * 128:
                                                   c * TP + itl * 128 + 128],
                                              IDF[:])
                            if c == 7:
                                mmi.then_inc(smm.h, 1)
                    p.op("tensor", ftf)
                    smm.val += 1
                    p.wait("vector", smm, smm.val)
                    ln_stats(tbf, D, 1, TMPA[:])
                    p.wait("vector", ld, x2r_done)
                    if out_stores > 0:
                        p.wait("vector", sst, out_stores)

                    def ffin(v, tbf=tbf):
                        v.reciprocal(STT[:, 15:16], SQS[:, 2:3])
                        v.drain()
                        v.tensor_scalar(TMPA[:], tbf, STT[:, 12:13], STT[:, 15:16],
                                        A.subtract, A.mult)
                        v.tensor_tensor(OUTT[:], X2R[:], TMPA[:], A.add
                                        ).then_inc(sv.h, 1)
                    p.op("vector", ffin)
                    sv.val += 1
                    if not fdummy:
                        p.wait("sync", sv, sv.val)
                        dma("sync", sst, out_d[:, itg * D:(itg + 1) * D], OUTT[:])
                        out_stores = sst.val
                final_last_v = sv.val

        tap("acc0", ACCH[:], sv, sv.val)

        p.wait("gpsimd", sst, sst.val)
        return _close(block, p, nc)


def _finish(p, sst, out_d, ACCH):
    p.wait("gpsimd", sst, sst.val)


def _close(block, p, nc):
    if True:
        @block.sync
        def _(e):
            for fn in p.q["sync"]:
                fn(e)

        @block.gpsimd
        def _(e):
            for fn in p.q["gpsimd"]:
                fn(e)

        @block.tensor
        def _(e):
            for fn in p.q["tensor"]:
                fn(e)

        @block.vector
        def _(e):
            for fn in p.q["vector"]:
                fn(e)

        @block.scalar
        def _(e):
            for fn in p.q["scalar"]:
                fn(e)

    return nc


def _layouts(inputs):
    f = np.ascontiguousarray
    r = {}
    r["dw"] = f(inputs["L0_attn_down_w"].reshape(8, 128, dA).transpose(1, 0, 2)
                .reshape(128, 8 * dA))
    r["db"] = f(inputs["L0_attn_down_b"].reshape(2, 128).T)
    for nm, kk in [("wq", "L0_attn_wq"), ("wk", "L0_attn_wk"),
                   ("wv", "L0_attn_wv"), ("wo", "L0_attn_wo")]:
        r[nm] = f(inputs[kk].reshape(2, 128, dA).transpose(1, 0, 2).reshape(128, 2 * dA))
    r["uw"] = f(inputs["L0_attn_up_w"].reshape(2, 128, D).transpose(1, 0, 2)
                .reshape(128, 2 * D))
    r["upb"] = f(inputs["L0_attn_up_b"].reshape(1, D))
    r["w1"] = f(np.concatenate([inputs["L0_attn_W1"][0], inputs["L0_attn_W1"][1]], 1))
    r["b1"] = f(inputs["L0_attn_b1"].reshape(1, 2 * FH))
    r["glw"] = f(inputs["L0_attn_ttt_ln_w"].reshape(1, 2 * FH))
    r["glb"] = f(inputs["L0_attn_ttt_ln_b"].reshape(1, 2 * FH))
    r["lrw"] = f(inputs["L0_attn_lr_w"].reshape(2, 2, 128).transpose(2, 1, 0)
                 .reshape(128, 4))
    r["lrb"] = f(inputs["L0_attn_lr_b"].reshape(2, 1))
    r["tki"] = f(inputs["L0_attn_tok_idx"].reshape(1, K))
    r["pw"] = f(inputs["L0_attn_post_w"].reshape(1, dA))
    r["pb"] = f(inputs["L0_attn_post_b"].reshape(1, dA))
    r["lw"] = f(inputs["L0_attn_ln_w"].reshape(1, dA))
    r["lb"] = f(inputs["L0_attn_ln_b"].reshape(1, dA))
    r["gwt"] = f(inputs["L0_ffn_gate_w"].T.reshape(8, 128, NE).transpose(1, 0, 2)
                 .reshape(128, 8 * NE))
    r["wsin"] = f(inputs["L0_ffn_ws_in"].reshape(8, 128, SHID).transpose(1, 0, 2)
                  .reshape(128, 8 * SHID))
    r["vsin"] = f(inputs["L0_ffn_vs_in"].reshape(8, 128, SHID).transpose(1, 0, 2)
                  .reshape(128, 8 * SHID))
    r["wsout"] = f(inputs["L0_ffn_ws_out"].reshape(8, 128, D).transpose(1, 0, 2)
                   .reshape(128, 8 * D))
    r["ewin"] = f(inputs["L0_ffn_e_win"].reshape(NE, 8, 128, EH)
                  .transpose(0, 2, 1, 3).reshape(NE, 128, 8 * EH))
    r["evin"] = f(inputs["L0_ffn_e_vin"].reshape(NE, 8, 128, EH)
                  .transpose(0, 2, 1, 3).reshape(NE, 128, 8 * EH))
    r["ewout"] = f(inputs["L0_ffn_e_wout"].reshape(NE, 4, 128, D)
                   .transpose(0, 2, 1, 3).reshape(NE, 128, 4 * D))
    cstv = np.zeros((128, 8), np.float32)
    cstv[:, 0] = 1e-6
    cstv[:, 1] = 1e-5
    cstv[:, 2] = 1.0 / 128.0
    cstv[:, 3] = 1.0
    r["cst"] = cstv
    r["idf"] = np.eye(128, dtype=np.float32)
    r["ones16"] = np.ones((1, 16), np.float32)
    r["onesc"] = np.ones((1, 128), np.float32)
    r["negu"] = f(-np.triu(np.ones((16, 16), np.float32)) / 128.0)
    r["rk"] = (1.0 / np.arange(1, K + 1, dtype=np.float32)).reshape(1, K)
    return r


def kernel(**inputs):
    inputs = {k: np.asarray(v) for k, v in inputs.items()}
    shared = _layouts(inputs)
    x = inputs["L0_attn_x"]
    nc = build_program()
    in_maps = []
    for b in range(B):
        m = dict(shared)
        m["xin"] = np.ascontiguousarray(
            x[b].reshape(NT, 128, D).transpose(1, 0, 2).reshape(128, NT * D))
        in_maps.append(m)
    res = run_bass_kernel_spmd(nc, in_maps, core_ids=list(range(B)))
    outs = []
    for b in range(B):
        o = np.asarray(res.results[b]["out"])
        outs.append(o.reshape(128, NT, D).transpose(1, 0, 2).reshape(T, D))
    return np.stack(outs).astype(np.float32)


# version 02
# speedup vs baseline: 1.0806x; 1.0806x over previous
import numpy as np
import concourse.bass as bass
import concourse.mybir as mybir
from concourse.bass_utils import run_bass_kernel_spmd

F32 = mybir.dt.float32
BF16 = mybir.dt.bfloat16
A = mybir.AluOpType
AF = mybir.ActivationFunctionType
AX = mybir.AxisListType

B, T, D = 8, 2048, 1024
dA = 256
H, FH, K = 2, 128, 16
NSTEP = T // K
NT = T // 128
NE, EH, SHID = 16, 512, 1024
TP = T // 2
DEV_TAPS = []
STOP = 99
DEV_LN0_TILES = None
DEV_SCAN_STEPS = None


class Sem:
    def __init__(self, handle):
        self.h = handle
        self.val = 0


class P:
    def __init__(self):
        self.q = {e: [] for e in ("tensor", "vector", "scalar", "gpsimd", "sync")}

    def op(self, eng, fn):
        self.q[eng].append(fn)

    def wait(self, eng, sem, val):
        if val > 0:
            self.q[eng].append(lambda e, s=sem.h, v=val: e.wait_ge(s, v))


def build_program():
    nc = bass.Bass("TRN2", target_bir_lowering=False)

    def din(name, shape, dtype=F32):
        return nc.dram_tensor(name, list(shape), dtype, kind="ExternalInput").ap()

    xin = din("xin", (128, NT * D))
    dw_d = din("dw", (128, 8 * dA)); db_d = din("db", (128, 2))
    wq_d = din("wq", (128, 2 * dA)); wk_d = din("wk", (128, 2 * dA))
    wv_d = din("wv", (128, 2 * dA)); wo_d = din("wo", (128, 2 * dA))
    uw_d = din("uw", (128, 2 * D)); upb_d = din("upb", (1, D))
    w1_d = din("w1", (128, 2 * FH)); b1_d = din("b1", (1, 2 * FH))
    glw_d = din("glw", (1, 2 * FH)); glb_d = din("glb", (1, 2 * FH))
    lrw_d = din("lrw", (128, 4)); lrb_d = din("lrb", (2, 1))
    tki_d = din("tki", (1, K))
    pw_d = din("pw", (1, dA)); pb_d = din("pb", (1, dA))
    lw_d = din("lw", (1, dA)); lb_d = din("lb", (1, dA))
    gwt_d = din("gwt", (128, 8 * NE))
    wsin_d = din("wsin", (128, 8 * SHID)); vsin_d = din("vsin", (128, 8 * SHID))
    wsout_d = din("wsout", (128, 8 * D))
    ewin_d = din("ewin", (NE, 128, 8 * EH))
    evin_d = din("evin", (NE, 128, 8 * EH))
    ewout_d = din("ewout", (NE, 128, 4 * D))
    cst_d = din("cst", (128, 8))
    idf_d = din("idf", (128, 128))
    ones16_d = din("ones16", (1, 16))
    onesc_d = din("onesc", (1, 128))
    negu_d = din("negu", (16, 16))
    rk_d = din("rk", (1, K))

    out_d = nc.dram_tensor("out", [128, NT * D], F32, kind="ExternalOutput").ap()
    scr = nc.dram_tensor("scr", [2 * T], F32, kind="Internal").ap()
    scr2 = nc.dram_tensor("scr2", [128, NT * D], BF16, kind="Internal").ap()

    taps = {}
    tap_specs = {
        "lnt": [128, 8 * T], "ht": [128, 2 * T], "xqt": [128, 2 * T],
        "xkt": [128, 2 * T], "siglr": [2, T], "xkle": [128, NT * dA],
        "tgt2": [128, NT * dA], "zbar": [128, NT * dA], "x2": [128, NT * D],
        "lnt3": [128, 8 * T], "gval": [128, NT * NE], "combt": [16, T],
        "w1f": [128, 2 * FH], "b1f": [1, 2 * FH], "sigt": [128, 2 * K],
        "lrc": [16, 2 * NSTEP], "acc0": [128, 8 * TP],
        "he": [128, D], "stt": [128, 20], "xktmp": [128, dA],
        "mv": [16, 4], "rstd": [16, 2], "grb": [16, 2 * FH], "xhs": [16, 2 * FH],
        "gys": [16, 2 * FH], "ctb": [16, 2 * K], "m1n": [16, 2], "m2": [16, 2], "std": [16, 2], "sqs": [128, 4],
        "tme": [128, dA], "tmpa": [128, D], "tmpd": [128, D], "ptt": [128, dA],
    }
    for name in DEV_TAPS:
        dt_ = BF16 if name in ("combt",) else F32
        taps[name] = nc.dram_tensor("tap_" + name, tap_specs[name], dt_,
                                    kind="ExternalOutput").ap()

    p = P()

    from contextlib import ExitStack
    with ExitStack() as _es:
        XRES = _es.enter_context(nc.sbuf_tensor("XRES", [128, NT * D], BF16))
        LNT = _es.enter_context(nc.sbuf_tensor("LNT", [128, 8 * T], BF16))
        BIGP = _es.enter_context(nc.sbuf_tensor("BIGP", [128, 4 * T], BF16))
        BIGQ = _es.enter_context(nc.sbuf_tensor("BIGQ", [128, 4 * T], BF16))
        TGT2F = _es.enter_context(nc.sbuf_tensor("TGT2F", [128, 2 * NT * dA], BF16))
        HE = _es.enter_context(nc.sbuf_tensor("HE", [128, 4 * TP], BF16))
        WBIGA = _es.enter_context(nc.sbuf_tensor("WBIGA", [128, 8 * dA + 2 * D], BF16))
        WINbT = _es.enter_context(nc.sbuf_tensor("WINb", [128, 2 * 8 * EH], BF16))
        VINbT = _es.enter_context(nc.sbuf_tensor("VINb", [128, 2 * 8 * EH], BF16))
        CBX = _es.enter_context(nc.sbuf_tensor("CBX", [128, TP], BF16))
        TGT16R = _es.enter_context(nc.sbuf_tensor("TGT16R", [16, 4 * dA], BF16))
        XKLE16R = _es.enter_context(nc.sbuf_tensor("XKLE16R", [16, 4 * dA], BF16))
        ZB16 = _es.enter_context(nc.sbuf_tensor("ZB16", [16, 2 * dA], BF16))
        COMBROW = _es.enter_context(nc.sbuf_tensor("COMBROW", [1, 2 * TP], BF16))
        MTMP = _es.enter_context(nc.sbuf_tensor("MTMP", [128, TP], BF16))
        COMBT = _es.enter_context(nc.sbuf_tensor("COMBT", [16, T], BF16))
        DB = _es.enter_context(nc.sbuf_tensor("DB", [128, 2], F32))
        WQ = _es.enter_context(nc.sbuf_tensor("WQ", [128, 2 * dA], BF16))
        WK = _es.enter_context(nc.sbuf_tensor("WK", [128, 2 * dA], BF16))
        WV = _es.enter_context(nc.sbuf_tensor("WV", [128, 2 * dA], BF16))
        WO = _es.enter_context(nc.sbuf_tensor("WO", [128, 2 * dA], BF16))
        UPB = _es.enter_context(nc.sbuf_tensor("UPB", [1, D], BF16))
        W1f = _es.enter_context(nc.sbuf_tensor("W1f", [128, 2 * FH], F32))
        W1b = _es.enter_context(nc.sbuf_tensor("W1b", [128, 2 * FH], BF16))
        B1f = _es.enter_context(nc.sbuf_tensor("B1f", [1, 2 * FH], F32))
        LRWT = _es.enter_context(nc.sbuf_tensor("LRWT", [128, 4], BF16))
        LRB = _es.enter_context(nc.sbuf_tensor("LRB", [2, 1], F32))
        GWT = _es.enter_context(nc.sbuf_tensor("GWT", [128, 8 * NE], F32))
        CST = _es.enter_context(nc.sbuf_tensor("CST", [128, 8], F32))
        IDF = _es.enter_context(nc.sbuf_tensor("IDF", [128, 128], F32))
        IDB = _es.enter_context(nc.sbuf_tensor("IDB", [128, 128], BF16))
        ONES16f = _es.enter_context(nc.sbuf_tensor("ONES16f", [1, 16], F32))
        ONESCb = _es.enter_context(nc.sbuf_tensor("ONESCb", [1, 128], BF16))
        NEGU = _es.enter_context(nc.sbuf_tensor("NEGU", [16, 16], F32))
        RK = _es.enter_context(nc.sbuf_tensor("RK", [1, K], F32))
        TI = _es.enter_context(nc.sbuf_tensor("TI", [1, K], F32))
        TI15b = _es.enter_context(nc.sbuf_tensor("TI15b", [128, 1], F32))
        GB = _es.enter_context(nc.sbuf_tensor("GB", [128, 2 * FH], F32))
        BB = _es.enter_context(nc.sbuf_tensor("BB", [128, 2 * FH], F32))
        G2B = _es.enter_context(nc.sbuf_tensor("G2B", [128, 2 * FH], F32))
        PWB = _es.enter_context(nc.sbuf_tensor("PWB", [128, dA], F32))
        PBB = _es.enter_context(nc.sbuf_tensor("PBB", [128, dA], F32))
        LWB = _es.enter_context(nc.sbuf_tensor("LWB", [128, dA], F32))
        LBB = _es.enter_context(nc.sbuf_tensor("LBB", [128, dA], F32))
        SIGT = _es.enter_context(nc.sbuf_tensor("SIGT", [128, 2 * NT], F32))
        LETOK = _es.enter_context(nc.sbuf_tensor("LETOK", [128, 2 * NT], F32))
        LECB = _es.enter_context(nc.sbuf_tensor("LECB", [16, 2 * NSTEP], BF16))
        TOKLN = _es.enter_context(nc.sbuf_tensor("TOKLN", [16, 16], F32))
        BN6 = _es.enter_context(nc.sbuf_tensor("BN6", [16, 12], F32))
        MV = _es.enter_context(nc.sbuf_tensor("MV", [16, 4], F32))
        STD = _es.enter_context(nc.sbuf_tensor("STD", [16, 2], F32))
        RSTD = _es.enter_context(nc.sbuf_tensor("RSTD", [16, 2], F32))
        M1N = _es.enter_context(nc.sbuf_tensor("M1N", [16, 2], F32))
        M2 = _es.enter_context(nc.sbuf_tensor("M2", [16, 2], F32))
        CT1 = _es.enter_context(nc.sbuf_tensor("CT1", [16, 2 * K], F32))
        CTB = _es.enter_context(nc.sbuf_tensor("CTB", [16, 2 * K], BF16))
        TMPA = _es.enter_context(nc.sbuf_tensor("TMPA", [128, D], F32))
        TMPD = _es.enter_context(nc.sbuf_tensor("TMPD", [128, D], F32))
        TME = _es.enter_context(nc.sbuf_tensor("TME", [128, dA], F32))
        TME2 = _es.enter_context(nc.sbuf_tensor("TME2", [128, dA], F32))
        XKtmp = _es.enter_context(nc.sbuf_tensor("XKtmp", [128, dA], F32))
        TMB = _es.enter_context(nc.sbuf_tensor("TMB", [128, dA], BF16))
        PTT = _es.enter_context(nc.sbuf_tensor("PTT", [128, dA], BF16))
        STT = _es.enter_context(nc.sbuf_tensor("STT", [128, 20], F32))
        SQS = _es.enter_context(nc.sbuf_tensor("SQS", [128, 4], F32))
        PBIG0 = _es.enter_context(nc.psum_tensor("PBIG0", [128, 2048], F32))
        PBIG1 = _es.enter_context(nc.psum_tensor("PBIG1", [128, 1024], F32))
        PTRA = _es.enter_context(nc.psum_tensor("PTRA", [128, 1024], BF16))
        PTRB = _es.enter_context(nc.psum_tensor("PTRB", [128, 1024], BF16))
        _ld = _es.enter_context(nc.semaphore("s_ld"))
        _wt = _es.enter_context(nc.semaphore("s_wt"))
        _mm = _es.enter_context(nc.semaphore("s_mm"))
        _v = _es.enter_context(nc.semaphore("s_v"))
        _sc = _es.enter_context(nc.semaphore("s_sc"))
        _g = _es.enter_context(nc.semaphore("s_g"))
        _st = _es.enter_context(nc.semaphore("s_st"))
        block = _es.enter_context(nc.Block())

        ld, wt2, smm, sv, ssc, sg, sst = (Sem(_ld), Sem(_wt), Sem(_mm), Sem(_v),
                                          Sem(_sc), Sem(_g), Sem(_st))

        OUTT = TMPD
        SIGLR = WINbT[:].bitcast(F32)[0:2, 0:T]
        ROWS = VINbT[:].bitcast(F32)[0:1, 0:6 * dA]
        LRC = WINbT[:].bitcast(F32)[0:16, 2048:2048 + 2 * NSTEP]
        GVAL = VINbT[:].bitcast(F32)[:, 1536:1536 + NT * NE]
        SCR_SC = MTMP[:].bitcast(F32)[0:16, 0:128]
        X2R = MTMP
        TI15 = TI15b[0:16, :]
        XHs = TMPD[0:16, 0:256]
        GYs = TMPD[0:16, 256:512]
        PRD = TMPD[0:16, 512:768]
        GRF = TMPD[0:16, 768:1024]
        GRB = TMPA[:].bitcast(BF16)[0:16, 0:256]
        Z1S = TMPA[0:16, 256:512]
        GWRK = TME2[:, 0:16]
        GEQ = TME2[:, 16:32]
        COMB = TME2[:, 32:48]
        GMAX = TME2[:, 48:49]
        RS = TME2[:, 49:50]
        WINb = [WINbT[:, 0:4096], WINbT[:, 4096:8192]]
        VINb = [VINbT[:, 0:4096], VINbT[:, 4096:8192]]
        XQT = BIGP[:, 0:2 * T]
        XKT = BIGP[:, 2 * T:4 * T]
        HT = BIGQ[:, 0:2 * T]
        XKLE = BIGQ[:, 2 * T:2 * T + NT * dA]
        WSIN = BIGP[:]
        VSIN = BIGQ[:]
        TGT2 = TGT2F[:, 0:NT * dA]
        WSOUT = TGT2F[:]
        DW = WBIGA[:, 0:8 * dA]
        UW = WBIGA[:, 8 * dA:8 * dA + 2 * D]
        WOUTb = WBIGA[:]
        ACCH = XRES[:].bitcast(F32)

        def bank(i):
            assert i < 6
            if i < 4:
                return PBIG0[:, i * 512:(i + 1) * 512]
            return PBIG1[:, (i - 4) * 512:(i - 4 + 1) * 512]

        def tap(name, src, sem=None, val=0):
            if name in taps:
                if sem is not None:
                    p.wait("gpsimd", sem, val)
                dst = taps[name]
                def fn(g, dst=dst, s=src):
                    g.dma_start(out=dst, in_=s).then_inc(sst.h, 16)
                p.op("gpsimd", fn)
                sst.val += 16

        def dma(eng, sem, dst, src, **kw):
            def fn(e, dst=dst, src=src, sem=sem, kw=kw):
                e.dma_start(out=dst, in_=src, **kw).then_inc(sem.h, 16)
            p.op(eng, fn)
            sem.val += 16

        for dst, src in [(CST[:], cst_d[:]), (IDF[:], idf_d[:]),
                         (ONES16f[:], ones16_d[:]), (NEGU[:], negu_d[:]),
                         (RK[:], rk_d[:]), (W1f[:], w1_d[:]), (B1f[:], b1_d[:]),
                         (TI[:], tki_d[:]), (DB[:], db_d[:]), (GWT[:], gwt_d[:])]:
            dma("sync", ld, dst, src)
        dma("sync", ld, LRB[:], lrb_d[:], allow_slow_non_contiguous=True)
        for i, src in enumerate([glw_d, glb_d, pw_d, pb_d, lw_d, lb_d]):
            dma("sync", ld, ROWS[:, i * dA:(i + 1) * dA][:, 0:src.shape[1]], src)
        ld_small = ld.val

        for dst, src in [(XRES[:], xin[:]), (DW, dw_d[:]), (WQ[:], wq_d[:]),
                         (WK[:], wk_d[:]), (WV[:], wv_d[:]), (WO[:], wo_d[:]),
                         (UW, uw_d[:]), (UPB[:], upb_d[:]), (LRWT[:], lrw_d[:]),
                         (IDB[:], idf_d[:]), (ONESCb[:], onesc_d[:]),
                         (W1b[:], w1_d[:])]:
            dma("gpsimd", wt2, dst, src)
        wt_attn = wt2.val

        p.wait("tensor", ld, ld_small)
        p.wait("tensor", wt2, wt_attn)
        ONES128f = ONES16f[0:1, 0:1].broadcast_to([1, 128])

        bc_dst = [GB, BB, PWB, PBB, LWB, LBB]
        bc_w = [2 * FH, 2 * FH, dA, dA, dA, dA]

        def f_bc_mm(t):
            for i in range(6):
                t.matmul(bank(i)[:, 0:bc_w[i]], ONES128f,
                         ROWS[0:1, i * dA: i * dA + bc_w[i]], start=True,
                         stop=True).then_inc(smm.h, 1)
        p.op("tensor", f_bc_mm)
        smm.val += 6
        p.wait("vector", smm, smm.val)

        def f_bc_cp(v):
            for i in range(6):
                v.tensor_copy(bc_dst[i][:, 0:bc_w[i]], bank(i)[:, 0:bc_w[i]])
            v.tensor_tensor(G2B[:], GB[:], GB[:], A.mult)
            v.tensor_tensor(TI[:], TI[:], RK[:], A.add)
            v.drain()
            v.tensor_relu(TI[:], TI[:])
            v.drain()
            v.tensor_copy(STT[0:1, 0:1], TI[0:1, 0:1]).then_inc(sv.h, 1)
        p.op("vector", f_bc_cp)
        sv.val += 1
        p.wait("tensor", sv, sv.val)

        def f_tok(t):
            t.matmul(bank(5)[0:16, 0:16], ONES16f[:], TI[:],
                     start=True, stop=True).then_inc(smm.h, 1)
        p.op("tensor", f_tok)
        smm.val += 1
        p.wait("vector", smm, smm.val)

        def f_tok2(v):
            v.tensor_tensor(TOKLN[:], bank(5)[0:16, 0:16], NEGU[:],
                            A.mult).then_inc(sv.h, 1)
        p.op("vector", f_tok2)
        sv.val += 1
        p.wait("tensor", sv, sv.val)

        def f_ti15(t):
            t.matmul(bank(5)[:, 16:17], ONES128f, TI[0:1, 15:16],
                     start=True, stop=True).then_inc(smm.h, 1)
        p.op("tensor", f_ti15)
        smm.val += 1
        p.wait("vector", smm, smm.val)

        def f_ti15c(v):
            v.tensor_copy(TI15b[:], bank(5)[:, 16:17]).then_inc(sv.h, 1)
        p.op("vector", f_ti15c)
        sv.val += 1
        ti15_done = sv.val

        def ln_stats(eng_src, width, eps_col, scrap):
            w = width

            def f1(v, src=eng_src, scrap=scrap, w=w):
                v.tensor_reduce(STT[:, 9:10], src, AX.X, A.add)
                v.tensor_copy(scrap[:, 0:w], src)
                v.tensor_tensor(scrap[:, 0:w], scrap[:, 0:w], scrap[:, 0:w],
                                A.mult)
                v.tensor_reduce(STT[:, 10:11], scrap[:, 0:w], AX.X, A.add)
                v.drain()
                v.tensor_scalar(STT[:, 12:13], STT[:, 9:10], 1.0 / w, None, A.mult)
                v.tensor_scalar(STT[:, 13:14], STT[:, 10:11], 1.0 / w, None, A.mult)
                v.drain()
                v.tensor_tensor(STT[:, 16:17], STT[:, 12:13], STT[:, 12:13], A.mult)
                v.drain()
                v.tensor_tensor(STT[:, 13:14], STT[:, 13:14], STT[:, 16:17],
                                A.subtract)
                v.drain()
                v.tensor_copy(STT[:, 8:9], STT[:, 13:14]).then_inc(sv.h, 1)
            p.op("vector", f1)
            sv.val += 1
            p.wait("scalar", sv, sv.val)

            def f2(sc, eps_col=eps_col):
                sc.activation(SQS[:, 2:3], STT[:, 13:14], AF.Sqrt,
                              bias=CST[:, eps_col:eps_col + 1], scale=1.0
                              ).then_inc(ssc.h, 1)
            p.op("scalar", f2)
            ssc.val += 1
            p.wait("vector", ssc, ssc.val)

        p.wait("vector", wt2, wt_attn)
        tr_done = {}
        p.wait("scalar", wt2, wt_attn)
        for it in range(NT if DEV_LN0_TILES is None else DEV_LN0_TILES):
            xsrc = XRES[:, it * D:(it + 1) * D]
            ln_stats(xsrc, D, 1, TMPA[:])
            if it >= 1:
                p.wait("vector", smm, tr_done[it - 1])

            def f3(v, src=xsrc):
                v.reciprocal(STT[:, 15:16], SQS[:, 2:3])
                v.drain()
                v.tensor_scalar(HE[:, 0:D], src, STT[:, 12:13], STT[:, 15:16],
                                A.subtract, A.mult).then_inc(sv.h, 1)
            p.op("vector", f3)
            sv.val += 1
            p.wait("tensor", sv, sv.val)
            if it >= 2:
                p.wait("tensor", ssc, ssc.val - 1)

            def ftr(t, it=it):
                ptr = PTRA if it % 2 == 0 else PTRB
                for c in range(8):
                    mmi = t.transpose(ptr[:, c * 128:(c + 1) * 128],
                                      HE[:, c * 128:(c + 1) * 128], IDB[:])
                    if c == 7:
                        mmi.then_inc(smm.h, 1)
            p.op("tensor", ftr)
            smm.val += 1
            tr_done[it] = smm.val
            p.wait("scalar", smm, smm.val)

            def fcp(s, it=it):
                ptr = PTRA if it % 2 == 0 else PTRB
                for c in range(8):
                    s.copy(LNT[:, c * T + it * 128: c * T + it * 128 + 128],
                           ptr[:, c * 128:(c + 1) * 128])
                s.copy(SQS[0:1, 3:4], CST[0:1, 3:4]).then_inc(ssc.h, 1)
            p.op("scalar", fcp)
            ssc.val += 1
        ln0_done_sc = ssc.val
        tap("lnt", LNT[:], ssc, ln0_done_sc)
        tap("he", HE[:, 0:D], ssc, ln0_done_sc)
        tap("stt", STT[:], ssc, ln0_done_sc)

        if STOP <= 1:
            _finish(p, sst, out_d, ACCH)
            return _close(block, p, nc)
        p.wait("tensor", ssc, ln0_done_sc)
        dp_evac = {}
        for fc in range(2):
            for jt in range(4):
                gi = fc * 4 + jt
                bk = gi % 6
                if gi >= 6:
                    p.wait("tensor", ssc, dp_evac[gi - 6])

                def fdp(t, fc=fc, jt=jt, bk=bk):
                    for dc in range(8):
                        mmi = t.matmul(
                            bank(bk),
                            DW[:, dc * dA + fc * 128: dc * dA + fc * 128 + 128],
                            LNT[:, dc * T + jt * 512: dc * T + jt * 512 + 512],
                            start=(dc == 0), stop=(dc == 7))
                        if dc == 7:
                            mmi.then_inc(smm.h, 1)
                p.op("tensor", fdp)
                smm.val += 1
                p.wait("scalar", smm, smm.val)

                def fsl(s, fc=fc, jt=jt, bk=bk):
                    s.activation(HT[:, fc * T + jt * 512: fc * T + jt * 512 + 512],
                                 bank(bk), AF.Silu, bias=DB[:, fc:fc + 1],
                                 scale=1.0).then_inc(ssc.h, 1)
                p.op("scalar", fsl)
                ssc.val += 1
                dp_evac[gi] = ssc.val
        ht_done = ssc.val
        tap("ht", HT, ssc, ht_done)

        if STOP <= 2:
            _finish(p, sst, out_d, ACCH)
            return _close(block, p, nc)
        p.wait("tensor", ssc, ht_done)
        for j in range(4):
            def flr(t, j=j):
                for dc in range(2):
                    mmi = t.matmul(bank(j)[0:2, 0:512],
                                   LRWT[:, dc * 2: dc * 2 + 2],
                                   HT[:, dc * T + j * 512: dc * T + j * 512 + 512],
                                   start=(dc == 0), stop=(dc == 1))
                    if dc == 1:
                        mmi.then_inc(smm.h, 1)
            p.op("tensor", flr)
            smm.val += 1
            p.wait("scalar", smm, smm.val)

            def fsg(s, j=j):
                s.activation(SIGLR[:, j * 512:(j + 1) * 512], bank(j)[0:2, 0:512],
                             AF.Sigmoid, bias=LRB[:], scale=1.0).then_inc(ssc.h, 1)
            p.op("scalar", fsg)
            ssc.val += 1
        siglr_done = ssc.val
        tap("siglr", SIGLR[:], ssc, siglr_done)

        p.wait("sync", ssc, siglr_done)
        dma("sync", ld, scr[:], SIGLR[:])
        p.wait("sync", ld, ld.val)
        for h in range(H):
            dma("sync", ld, SIGT[:, h * NT:(h + 1) * NT],
                scr[h * T:(h + 1) * T].rearrange("(w q) -> q w", q=128),
                allow_slow_non_contiguous=True)
            dma("sync", ld, LRC[:, h * NSTEP:(h + 1) * NSTEP],
                scr[h * T:(h + 1) * T].rearrange("(n k) -> k n", k=K),
                allow_slow_non_contiguous=True)
        sig_resh = ld.val
        tap("sigt", SIGT[:], ld, sig_resh)
        tap("lrc", LRC[:], ld, sig_resh)

        p.wait("vector", ld, sig_resh)
        p.wait("vector", sv, ti15_done)

        def fle(v):
            v.tensor_scalar(LETOK[:], SIGT[:], TI15b[:], 1.0 / 128.0, A.mult, A.mult)
            v.tensor_scalar(LECB[:], LRC[:], TI15[:], 1.0 / 128.0, A.mult,
                            A.mult).then_inc(sv.h, 1)
        p.op("vector", fle)
        sv.val += 1
        letok_done = sv.val

        if STOP <= 3:
            _finish(p, sst, out_d, ACCH)
            return _close(block, p, nc)
        p.wait("tensor", ssc, siglr_done)
        pj_evac = {}
        for wi_, (wsrc, dstt) in enumerate(((WQ, XQT), (WK, XKT))):
            for fc in range(2):
                for jt in range(4):
                    gi = wi_ * 8 + fc * 4 + jt
                    bk = gi % 6
                    if gi >= 6:
                        p.wait("tensor", sv, pj_evac[gi - 6])

                    def fpj(t, wsrc=wsrc, fc=fc, jt=jt, bk=bk):
                        for dc in range(2):
                            mmi = t.matmul(
                                bank(bk),
                                wsrc[:, dc * dA + fc * 128: dc * dA + fc * 128 + 128],
                                HT[:, dc * T + jt * 512: dc * T + jt * 512 + 512],
                                start=(dc == 0), stop=(dc == 1))
                            if dc == 1:
                                mmi.then_inc(smm.h, 1)
                    p.op("tensor", fpj)
                    smm.val += 1
                    p.wait("vector", smm, smm.val)

                    def fev(v, dstt=dstt, fc=fc, jt=jt, bk=bk):
                        v.tensor_copy(dstt[:, fc * T + jt * 512: fc * T + jt * 512 + 512],
                                      bank(bk)).then_inc(sv.h, 1)
                    p.op("vector", fev)
                    sv.val += 1
                    pj_evac[gi] = sv.val
        proj_fmaj_done = sv.val
        tap("xqt", XQT, sv, proj_fmaj_done)
        tap("xkt", XKT, sv, proj_fmaj_done)

        if STOP <= 4:
            _finish(p, sst, out_d, ACCH)
            return _close(block, p, nc)
        p.wait("vector", sv, letok_done)
        p.wait("tensor", sv, proj_fmaj_done)
        tokmaj_evac = {}
        for it in range(NT):
            bkk = it % 2
            bkv = 2 + it % 2
            if it >= 2:
                p.wait("tensor", sv, tokmaj_evac[it - 2])

            def fkv(t, it=it, bkk=bkk, bkv=bkv):
                for dc in range(2):
                    t.matmul(bank(bkk)[:, 0:256],
                             HT[:, dc * T + it * 128: dc * T + it * 128 + 128],
                             WK[:, dc * dA: dc * dA + dA],
                             start=(dc == 0), stop=(dc == 1))
                for dc in range(2):
                    mmi = t.matmul(bank(bkv)[:, 0:256],
                                   HT[:, dc * T + it * 128: dc * T + it * 128 + 128],
                                   WV[:, dc * dA: dc * dA + dA],
                                   start=(dc == 0), stop=(dc == 1))
                    if dc == 1:
                        mmi.then_inc(smm.h, 1)
            p.op("tensor", fkv)
            smm.val += 1
            p.wait("scalar", smm, smm.val)

            def fxk(s, bkk=bkk):
                s.copy(XKtmp[:], bank(bkk)[:, 0:256]).then_inc(ssc.h, 1)
            p.op("scalar", fxk)
            ssc.val += 1
            p.wait("vector", ssc, ssc.val)

            def fxv(v, it=it, bkv=bkv):
                for h in range(H):
                    v.tensor_scalar(XKLE[:, it * dA + h * FH: it * dA + h * FH + FH],
                                    XKtmp[:, h * FH:(h + 1) * FH],
                                    LETOK[:, h * NT + it: h * NT + it + 1], None,
                                    A.mult)
                v.tensor_tensor(TME[:], XKtmp[:], bank(bkv)[:, 0:256], A.subtract)
                v.tensor_tensor(TME[:], TME[:], BB[:], A.add)
                v.tensor_tensor(TGT2[:, it * dA:(it + 1) * dA], TME[:], GB[:],
                                A.mult).then_inc(sv.h, 1)
            p.op("vector", fxv)
            sv.val += 1
            tokmaj_evac[it] = sv.val
        tokmaj_done = sv.val
        tap("xkle", XKLE, sv, tokmaj_done)
        tap("xktmp", XKtmp[:], sv, tokmaj_done)
        tap("tgt2", TGT2[:], sv, tokmaj_done)

        if STOP <= 5:
            _finish(p, sst, out_d, ACCH)
            return _close(block, p, nc)
        p.wait("tensor", sv, tokmaj_done)
        p.wait("tensor", sv, proj_fmaj_done)

        zbar_done = {}
        cast_done = {}
        grad_ready = {}
        upd_done = {}
        mm_all = {}

        tgt_pf = {}
        xkle_pf = {}
        stash_ld = {}
        NS_ = NSTEP if DEV_SCAN_STEPS is None else DEV_SCAN_STEPS
        for n in range(NS_):
            q = n % 2
            bA, bB, bC = bank(q), bank(2 + q), bank(4 + q)
            itile, r0 = n // 8, (n % 8) * 16
            slot = n % 4

            p.wait("sync", sv, tokmaj_evac[n // 8])
            if n >= 4:
                p.wait("sync", sv, grad_ready[n - 4])
                p.wait("sync", smm, mm_all[n - 4])
            dma("sync", ld, TGT16R[:, slot * dA:(slot + 1) * dA],
                TGT2[r0:r0 + 16, itile * dA:(itile + 1) * dA])
            tgt_pf[n] = ld.val
            dma("sync", ld, XKLE16R[:, slot * dA:(slot + 1) * dA],
                XKLE[r0:r0 + 16, itile * dA:(itile + 1) * dA])
            xkle_pf[n] = ld.val

            if n >= 1:
                p.wait("tensor", ssc, cast_done[n - 1])
            if n >= 2:
                p.wait("tensor", sv, upd_done[n - 2])
                p.wait("tensor", ssc, zbar_done[n - 2])
            p.wait("tensor", ld, xkle_pf[n])

            def f_z1(t, n=n, bA=bA):
                for h in range(H):
                    t.matmul(bA[0:16, h * FH:(h + 1) * FH],
                             XKT[:, h * T + n * K: h * T + n * K + K],
                             W1b[:, h * FH:(h + 1) * FH], start=True, stop=False)
                    t.matmul(bA[0:16, h * FH:(h + 1) * FH], ONES16f[:],
                             B1f[0:1, h * FH:(h + 1) * FH], start=False, stop=True)
                for h in range(H):
                    mmi = t.matmul(bA[32:48, h * K:(h + 1) * K],
                                   XKT[:, h * T + n * K: h * T + n * K + K],
                                   XQT[:, h * T + n * K: h * T + n * K + K],
                                   start=True, stop=True)
                    if h == 1:
                        mmi.then_inc(smm.h, 1)
            p.op("tensor", f_z1)
            smm.val += 1

            mmz = smm.val
            p.wait("vector", smm, mmz)
            p.wait("scalar", smm, mmz)

            def f_sums(s, bA=bA):
                for h in range(H):
                    s.activation(SCR_SC, bA[0:16, h * FH:(h + 1) * FH], AF.Identity,
                                 bias=0.0, scale=1.0, accum_out=SQS[0:16, h:h + 1])
                for h in range(H):
                    s.activation(SCR_SC, bA[0:16, h * FH:(h + 1) * FH], AF.Square,
                                 bias=0.0, scale=1.0,
                                 accum_out=SQS[0:16, 2 + h:3 + h])
                s.copy(BN6[0:1, 0:1], CST[0:1, 3:4]).then_inc(ssc.h, 1)
            p.op("scalar", f_sums)
            ssc.val += 1
            p.wait("vector", ssc, ssc.val)

            def f_var(v):
                v.tensor_scalar(MV[:, 0:2], SQS[0:16, 0:2], 1.0 / 128.0, None,
                                A.mult)
                v.tensor_scalar(MV[:, 2:4], SQS[0:16, 2:4], 1.0 / 128.0, None,
                                A.mult)
                v.drain()
                for h in range(H):
                    v.scalar_tensor_tensor(STD[:, h:h + 1], MV[:, h:h + 1],
                                           MV[:, h:h + 1], MV[:, 2 + h:3 + h],
                                           A.mult, A.subtract)
                v.drain()
                v.tensor_copy(STT[0:1, 7:8], CST[0:1, 3:4]).then_inc(sv.h, 1)
            p.op("vector", f_var)
            sv.val += 1
            p.wait("scalar", sv, sv.val)

            def f_sq(s):
                s.activation(SQS[0:16, 2:4], STD[:], AF.Sqrt, bias=CST[0:16, 0:1],
                             scale=-1.0).then_inc(ssc.h, 1)
            p.op("scalar", f_sq)
            ssc.val += 1
            p.wait("vector", ssc, ssc.val)
            p.wait("vector", ld, tgt_pf[n])

            def f_grad(v, n=n, bA=bA, slot=slot):
                v.reciprocal(RSTD[:], SQS[0:16, 2:4])
                v.drain()
                for h in range(H):
                    v.tensor_scalar(XHs[:, h * FH:(h + 1) * FH],
                                    bA[0:16, h * FH:(h + 1) * FH],
                                    MV[:, h: h + 1], RSTD[:, h: h + 1],
                                    A.subtract, A.mult)
                for h in range(H):
                    v.tensor_scalar(CT1[:, h * K:(h + 1) * K],
                                    bA[32:48, h * K:(h + 1) * K], 1.0, None, A.add)
                v.drain()
                v.tensor_tensor(CT1[:, 0:K], CT1[:, 0:K], TOKLN[:], A.mult)
                v.tensor_tensor(CT1[:, K:2 * K], CT1[:, K:2 * K], TOKLN[:], A.mult)
                v.drain()
                for h in range(H):
                    v.tensor_scalar(CTB[:, h * K:(h + 1) * K],
                                    CT1[:, h * K:(h + 1) * K],
                                    LRC[:, h * NSTEP + n: h * NSTEP + n + 1], None,
                                    A.mult)
                v.tensor_tensor(GYs[:], XHs[:], G2B[0:16, :], A.mult)
                v.tensor_tensor(GYs[:], GYs[:],
                                TGT16R[:, slot * dA:(slot + 1) * dA], A.add)
                v.tensor_tensor(PRD[:], GYs[:], XHs[:], A.mult)
                for h in range(H):
                    v.tensor_reduce(M1N[:, h:h + 1], PRD[:, h * FH:(h + 1) * FH],
                                    AX.X, A.add, negate=True)
                v.drain()
                v.tensor_scalar(M1N[:], M1N[:], 1.0 / 128.0, None, A.mult)
                v.drain()
                for h in range(H):
                    v.tensor_scalar(PRD[:, h * FH:(h + 1) * FH],
                                    GYs[:, h * FH:(h + 1) * FH],
                                    1.0 / 128.0, 0.0, A.mult, A.add,
                                    accum_out=M2[:, h:h + 1])
                for h in range(H):
                    v.scalar_tensor_tensor(GRF[:, h * FH:(h + 1) * FH],
                                           XHs[:, h * FH:(h + 1) * FH],
                                           M1N[:, h:h + 1],
                                           GYs[:, h * FH:(h + 1) * FH],
                                           A.mult, A.add)
                v.drain()
                for h in range(H):
                    mmo = v.tensor_scalar(GRB[:, h * FH:(h + 1) * FH],
                                          GRF[:, h * FH:(h + 1) * FH],
                                          M2[:, h:h + 1], RSTD[:, h:h + 1],
                                          A.subtract, A.mult)
                    if h == 1:
                        mmo.then_inc(sv.h, 1)
            p.op("vector", f_grad)
            sv.val += 1
            grad_ready[n] = sv.val

            p.wait("tensor", sv, grad_ready[n])

            def f_zb(t, n=n, bA=bA, bB=bB, bC=bC, slot=slot):
                for h in range(H):
                    t.matmul(bB[0:16, h * FH:(h + 1) * FH],
                             XQT[:, h * T + n * K: h * T + n * K + K],
                             W1b[:, h * FH:(h + 1) * FH], start=True, stop=False)
                    t.matmul(bB[0:16, h * FH:(h + 1) * FH], ONES16f[:],
                             B1f[0:1, h * FH:(h + 1) * FH], start=False, stop=False)
                    t.matmul(bB[0:16, h * FH:(h + 1) * FH],
                             CTB[:, h * K:(h + 1) * K],
                             GRB[:, h * FH:(h + 1) * FH], start=False, stop=True)
                for h in range(H):
                    t.matmul(bC[0:128, h * FH:(h + 1) * FH],
                             XKLE16R[:, slot * dA + h * FH: slot * dA + (h + 1) * FH],
                             GRB[:, h * FH:(h + 1) * FH], start=True, stop=True)
                for h in range(H):
                    mmi = t.matmul(bA[64:65, 256 + h * FH: 256 + (h + 1) * FH],
                                   LECB[:, h * NSTEP + n: h * NSTEP + n + 1],
                                   GRB[:, h * FH:(h + 1) * FH], start=True, stop=True)
                    if h == 1:
                        mmi.then_inc(smm.h, 1)
            p.op("tensor", f_zb)
            smm.val += 1
            mm_all[n] = smm.val

            p.wait("scalar", smm, mm_all[n])
            if n >= 2:
                p.wait("scalar", ld, stash_ld[n - 2])

            def f_zc(s, bB=bB, n=n):
                s.copy(ZB16[:, (n % 2) * dA:(n % 2 + 1) * dA],
                       bB[0:16, 0:256]).then_inc(ssc.h, 1)
            p.op("scalar", f_zc)
            ssc.val += 1
            zbar_done[n] = ssc.val
            p.wait("sync", ssc, zbar_done[n])
            dma("sync", ld, TGT2[r0:r0 + 16, itile * dA:(itile + 1) * dA],
                ZB16[:, (n % 2) * dA:(n % 2 + 1) * dA])
            stash_ld[n] = ld.val

            p.wait("vector", smm, mm_all[n])

            def f_upd(v, bA=bA, bC=bC):
                v.tensor_tensor(W1f[:], W1f[:], bC[0:128, 0:256], A.subtract)
                v.tensor_tensor(B1f[:], B1f[:], bA[64:65, 256:512],
                                A.subtract).then_inc(sv.h, 1)
            p.op("vector", f_upd)
            sv.val += 1
            upd_done[n] = sv.val

            p.wait("scalar", sv, upd_done[n])

            def f_cast(s):
                s.copy(W1b[:], W1f[:]).then_inc(ssc.h, 1)
            p.op("scalar", f_cast)
            ssc.val += 1
            cast_done[n] = ssc.val

        scan_v, scan_sc = sv.val, ssc.val
        p.wait("gpsimd", ld, stash_ld[NS_ - 1])
        tap("zbar", TGT2[:], ssc, scan_sc)
        tap("w1f", W1f[:], ssc, scan_sc)
        tap("mv", MV[:], ssc, scan_sc)
        tap("rstd", RSTD[:], ssc, scan_sc)
        tap("grb", GRB, ssc, scan_sc)
        tap("xhs", XHs, ssc, scan_sc)
        tap("gys", GYs, ssc, scan_sc)
        tap("ctb", CTB[:], ssc, scan_sc)
        tap("m1n", M1N[:], ssc, scan_sc)
        tap("m2", M2[:], ssc, scan_sc)
        tap("std", STD[:], ssc, scan_sc)
        tap("sqs", SQS[:], ssc, scan_sc)
        tap("b1f", B1f[:], ssc, scan_sc)

        if STOP <= 6:
            _finish(p, sst, out_d, ACCH)
            return _close(block, p, nc)
        for eng in ("tensor", "vector", "scalar"):
            p.wait(eng, ssc, ssc.val)
            p.wait(eng, sv, sv.val)
            p.wait(eng, smm, smm.val)
            p.wait(eng, ld, ld.val)
            p.wait(eng, wt2, wt2.val)
        for it, dummy in [(0, True)] + [(i, False) for i in range(NT)]:
            b0, b1_, b3 = 0, 0, 1
            p.wait("tensor", sv, sv.val)

            def fq(t, it=it, b0=b0):
                for dc in range(2):
                    mmi = t.matmul(bank(b0)[:, 0:256],
                                   HT[:, dc * T + it * 128: dc * T + it * 128 + 128],
                                   WQ[:, dc * dA: dc * dA + dA],
                                   start=(dc == 0), stop=(dc == 1))
                    if dc == 1:
                        mmi.then_inc(smm.h, 1)
            p.op("tensor", fq)
            smm.val += 1

            zsrc = TGT2[:, it * dA:(it + 1) * dA]

            def fz1(v, zsrc=zsrc):
                v.tensor_reduce(STT[:, 0:1], zsrc[:, 0:128], AX.X, A.add)
                v.tensor_reduce(STT[:, 1:2], zsrc[:, 128:256], AX.X, A.add)
                v.tensor_tensor(TME2[:], zsrc, zsrc, A.mult)
                v.tensor_reduce(STT[:, 2:3], TME2[:, 0:128], AX.X, A.add)
                v.tensor_reduce(STT[:, 3:4], TME2[:, 128:256], AX.X, A.add)
                v.tensor_scalar(STT[:, 12:14], STT[:, 0:2], 1.0 / 128.0, None,
                                A.mult)
                v.tensor_scalar(STT[:, 2:4], STT[:, 2:4], 1.0 / 128.0, None,
                                A.mult)
                v.tensor_tensor(STT[:, 4:6], STT[:, 12:14], STT[:, 12:14], A.mult)
                v.tensor_tensor(STT[:, 2:4], STT[:, 2:4], STT[:, 4:6],
                                A.subtract).then_inc(sv.h, 1)
            p.op("vector", fz1)
            sv.val += 1
            p.wait("scalar", sv, sv.val)

            def fz2(s):
                s.activation(SQS[:, 2:4], STT[:, 2:4], AF.Sqrt,
                             bias=CST[:, 0:1], scale=1.0).then_inc(ssc.h, 1)
            p.op("scalar", fz2)
            ssc.val += 1
            p.wait("vector", ssc, ssc.val)
            p.wait("vector", smm, smm.val)
            def fz3(v, zsrc=zsrc, b0=b0):
                v.reciprocal(STT[:, 16:18], SQS[:, 2:4])
                v.drain()
                v.tensor_scalar(TME[:, 0:128], zsrc[:, 0:128], STT[:, 12:13],
                                STT[:, 16:17], A.subtract, A.mult)
                v.tensor_scalar(TME[:, 128:256], zsrc[:, 128:256], STT[:, 13:14],
                                STT[:, 17:18], A.subtract, A.mult)
                v.tensor_tensor(TME[:], TME[:], GB[:], A.mult)
                v.tensor_tensor(TME[:], TME[:], BB[:], A.add)
                v.tensor_tensor(TME[:], TME[:], bank(b0)[:, 0:256], A.add
                                ).then_inc(sv.h, 1)
            p.op("vector", fz3)
            sv.val += 1

            ln_stats(TME[:], dA, 1, TME2[:])
            def fp3(v):
                v.reciprocal(STT[:, 15:16], SQS[:, 2:3])
                v.drain()
                v.tensor_scalar(TME2[:], TME[:], STT[:, 12:13], STT[:, 15:16],
                                A.subtract, A.mult)
                v.tensor_tensor(TME2[:], TME2[:], PWB[:], A.mult)
                v.tensor_tensor(TMB[:], TME2[:], PBB[:], A.add).then_inc(sv.h, 1)
            p.op("vector", fp3)
            sv.val += 1
            p.wait("tensor", sv, sv.val)

            def ftw(t):
                t.transpose(PTRA[:, 0:128], TMB[:, 0:128], IDB[:])
                t.transpose(PTRA[:, 128:256], TMB[:, 128:256], IDB[:]
                            ).then_inc(smm.h, 1)
            p.op("tensor", ftw)
            smm.val += 1
            p.wait("scalar", smm, smm.val)
            def fpc(s):
                s.copy(PTT[:, 0:128], PTRA[:, 0:128])
                s.copy(PTT[:, 128:256], PTRA[:, 128:256]).then_inc(ssc.h, 1)
            p.op("scalar", fpc)
            ssc.val += 1
            p.wait("tensor", ssc, ssc.val)
            def fwo(t):
                for dc in range(2):
                    mmi = t.matmul(bank(0)[:, 256:512], PTT[:, dc * 128:(dc + 1) * 128],
                                   WO[:, dc * dA: dc * dA + dA],
                                   start=(dc == 0), stop=(dc == 1))
                    if dc == 1:
                        mmi.then_inc(smm.h, 1)
            p.op("tensor", fwo)
            smm.val += 1
            p.wait("vector", smm, smm.val)

            ln_stats(bank(0)[:, 256:512], dA, 1, TME2[:])
            def fl3(v):
                v.reciprocal(STT[:, 15:16], SQS[:, 2:3])
                v.drain()
                v.tensor_scalar(TME2[:], bank(0)[:, 256:512], STT[:, 12:13],
                                STT[:, 15:16], A.subtract, A.mult)
                v.tensor_tensor(TME2[:], TME2[:], LWB[:], A.mult)
                v.tensor_tensor(TMB[:], TME2[:], LBB[:], A.add).then_inc(sv.h, 1)
            p.op("vector", fl3)
            sv.val += 1
            p.wait("tensor", sv, sv.val)

            def ftu(t):
                t.transpose(PTRB[:, 0:128], TMB[:, 0:128], IDB[:])
                t.transpose(PTRB[:, 128:256], TMB[:, 128:256], IDB[:]
                            ).then_inc(smm.h, 1)
            p.op("tensor", ftu)
            smm.val += 1
            p.wait("scalar", smm, smm.val)
            def fpc2(s):
                s.copy(PTT[:, 0:128], PTRB[:, 0:128])
                s.copy(PTT[:, 128:256], PTRB[:, 128:256]).then_inc(ssc.h, 1)
            p.op("scalar", fpc2)
            ssc.val += 1
            p.wait("tensor", ssc, ssc.val)
            upps = PBIG1[:, 0:1024]
            def fup(t, upps=upps):
                for j in range(2):
                    for dc in range(2):
                        t.matmul(upps[:, j * 512:(j + 1) * 512],
                                 PTT[:, dc * 128:(dc + 1) * 128],
                                 UW[:, dc * D + j * 512: dc * D + j * 512 + 512],
                                 start=(dc == 0), stop=False)
                    mmi = t.matmul(upps[:, j * 512:(j + 1) * 512], ONESCb[:],
                                   UPB[0:1, j * 512:(j + 1) * 512],
                                   start=False, stop=True)
                    if j == 1:
                        mmi.then_inc(smm.h, 1)
            p.op("tensor", fup)
            smm.val += 1
            p.wait("vector", smm, smm.val)

            ln_stats(upps, D, 1, TMPA[:])
            xres_t = XRES[:, it * D:(it + 1) * D]
            def fx2(v, upps=upps, xres_t=xres_t, dummy=dummy):
                v.reciprocal(STT[:, 15:16], SQS[:, 2:3])
                v.drain()
                v.tensor_scalar(TMPA[:], upps, STT[:, 12:13], STT[:, 15:16],
                                A.subtract, A.mult)
                v.tensor_tensor(TMPD[:], xres_t, TMPA[:], A.add)
                if dummy:
                    v.tensor_copy(STT[:, 9:10], TMPD[:, 0:1]).then_inc(sv.h, 1)
                else:
                    v.tensor_copy(xres_t, TMPD[:]).then_inc(sv.h, 1)
            p.op("vector", fx2)
            sv.val += 1

            ln_stats(TMPD[:], D, 1, TMPA[:])
            def fl33(v):
                v.reciprocal(STT[:, 15:16], SQS[:, 2:3])
                v.drain()
                v.tensor_scalar(TMPA[:], TMPD[:], STT[:, 12:13], STT[:, 15:16],
                                A.subtract, A.mult).then_inc(sv.h, 1)
            p.op("vector", fl33)
            sv.val += 1
            p.wait("tensor", sv, sv.val)
            tb = PBIG0[:, 1024:2048]
            def ftr3(t, tb=tb):
                for c in range(8):
                    mmi = t.transpose(tb[:, c * 128:(c + 1) * 128],
                                      TMPA[:, c * 128:(c + 1) * 128], IDF[:])
                    if c == 7:
                        mmi.then_inc(smm.h, 1)
            p.op("tensor", ftr3)
            smm.val += 1
            p.wait("scalar", smm, smm.val)
            def fcp3(s, tb=tb):
                for c in range(8):
                    s.copy(TMPD[:, c * 128:(c + 1) * 128], tb[:, c * 128:(c + 1) * 128])
                s.copy(SQS[0:1, 3:4], CST[0:1, 3:4]).then_inc(ssc.h, 1)
            p.op("scalar", fcp3)
            ssc.val += 1
            p.wait("vector", ssc, ssc.val)
            p.wait("tensor", ssc, ssc.val)
            def fcst(v, it=it):
                v.tensor_copy(LNT[:].rearrange("p (c t) -> p c t", c=8)
                              [:, :, it * 128:(it + 1) * 128],
                              TMPD[:].rearrange("p (c t) -> p c t", c=8)
                              ).then_inc(sv.h, 1)
            p.op("vector", fcst)
            sv.val += 1
            gps = bank(2)
            def fgm(t, gps=gps):
                for dc in range(8):
                    mmi = t.matmul(gps[0:128, 0:16], TMPD[:, dc * 128:(dc + 1) * 128],
                                   GWT[:, dc * NE:(dc + 1) * NE],
                                   start=(dc == 0), stop=(dc == 7))
                    if dc == 7:
                        mmi.then_inc(smm.h, 1)
            p.op("tensor", fgm)
            smm.val += 1
            p.wait("scalar", smm, smm.val)
            def fsig(s, it=it, gps=gps):
                s.copy(GVAL[:, it * NE:(it + 1) * NE], gps[0:128, 0:16]
                       ).then_inc(ssc.h, 1)
            p.op("scalar", fsig)
            ssc.val += 1

        p.wait("scalar", ssc, ssc.val)

        def fsigall(s):
            s.activation(GVAL[:], GVAL[:], AF.Sigmoid, bias=0.0, scale=1.0
                         ).then_inc(ssc.h, 1)
        p.op("scalar", fsigall)
        ssc.val += 1
        p.wait("vector", ssc, ssc.val)
        for it in range(NT):
            def ftk(v, it=it):
                g0 = GVAL[:, it * NE:(it + 1) * NE]
                v.tensor_copy(GWRK[:], g0)
                v.drain()
                for _ in range(4):
                    v.reduce_max(GMAX[:], GWRK[:], AX.X)
                    v.drain()
                    v.tensor_scalar(GEQ[:], GWRK[:], GMAX[:], None, A.is_equal)
                    v.drain()
                    v.scalar_tensor_tensor(GWRK[:], GEQ[:], -1e30, GWRK[:],
                                           A.mult, A.add)
                    v.drain()
                v.tensor_tensor(COMB[:], GWRK[:], g0, A.not_equal)
                v.drain()
                v.tensor_tensor(COMB[:], COMB[:], g0, A.mult)
                v.drain()
                v.tensor_reduce(RS[:], COMB[:], AX.X, A.add)
                v.drain()
                v.reciprocal(RS[:], RS[:])
                v.drain()
                v.tensor_scalar(COMB[:], COMB[:], RS[:], None, A.mult
                                ).then_inc(sv.h, 1)
            p.op("vector", ftk)
            sv.val += 1
            p.wait("tensor", sv, sv.val)
            if it >= 1:
                p.wait("tensor", sv, sv.val)

            def ftc(t):
                t.transpose(bank(2)[0:16, 128:256], COMB[:], IDF[:]).then_inc(smm.h, 1)
            p.op("tensor", ftc)
            smm.val += 1
            p.wait("vector", smm, smm.val)

            def fcc(v, it=it):
                v.tensor_copy(COMBT[:, it * 128:(it + 1) * 128],
                              bank(2)[0:16, 128:256]).then_inc(sv.h, 1)
            p.op("vector", fcc)
            sv.val += 1
        post_v, post_sc = sv.val, ssc.val
        tap("x2", XRES[:], sv, post_v)
        tap("tme", TME[:], sv, post_v)
        tap("tmpa", TMPA[:], sv, post_v)
        tap("tmpd", TMPD[:], sv, post_v)
        tap("ptt", PTT[:], sv, post_v)
        tap("lnt3", LNT[:], sv, post_v)
        tap("gval", GVAL[:], sv, post_v)
        tap("combt", COMBT[:], sv, post_v)

        p.wait("sync", sv, post_v)
        dma("sync", ld, scr2[:], XRES[:])
        x2_spill = ld.val

        if STOP <= 7:
            _finish(p, sst, out_d, ACCH)
            return _close(block, p, nc)
        p.wait("gpsimd", sv, post_v)
        dma("gpsimd", wt2, WSIN, wsin_d[:])
        dma("gpsimd", wt2, VSIN, vsin_d[:])
        dma("gpsimd", wt2, WSOUT, wsout_d[:])
        JOBS = []
        for pa in range(2):
            JOBS.append((pa, "sh", 0))
            JOBS.append((pa, "sh", 1))
            for e in range(NE):
                JOBS.append((pa, "e", e))
        ejobs = [ji for ji, j in enumerate(JOBS) if j[1] == "e"]
        eidx = {ji: k for k, ji in enumerate(ejobs)}
        winv_ready = {}
        wout_ready = {}
        combrow_ready = {}
        in_done = {}
        out_done = {}
        for k in range(2):
            e = JOBS[ejobs[k]][2]
            dma("gpsimd", wt2, WINb[k % 2], ewin_d[e])
            dma("gpsimd", wt2, VINb[k % 2], evin_d[e])
            winv_ready[ejobs[k]] = wt2.val
        wt_shared_all = wt2.val

        last_facc = {}
        final_last_v = 0
        out_stores = 0
        p.wait("vector", ld, x2_spill)

        for ji, (pa, kind, e) in enumerate(JOBS):
            t0 = pa * TP
            first = (kind == "sh" and e == 0)
            hc0 = 4 * e if kind == "sh" else 0

            if ji == 0:
                p.wait("tensor", wt2, wt_shared_all)
            if pa == 1 and kind == "sh" and e == 0:
                p.wait("tensor", sv, final_last_v)
            if ji > 0:
                p.wait("tensor", sv, last_facc[ji - 1])
            if kind == "e":
                p.wait("tensor", wt2, winv_ready[ji])
                win_s, vin_s = WINb[eidx[ji] % 2], VINb[eidx[ji] % 2]
            else:
                win_s, vin_s = WSIN, VSIN

            if kind == "e":
                k_ = eidx[ji]
                if k_ not in combrow_ready:
                    p.wait("sync", smm, smm.val)
                    dma("sync", ld, COMBROW[0:1, (k_ % 2) * TP:(k_ % 2 + 1) * TP],
                        COMBT[e:e + 1, t0: t0 + TP])
                    combrow_ready[k_] = ld.val
                p.wait("tensor", ld, combrow_ready[k_])
                cslot = (k_ % 2) * TP

                def fcb(t, cslot=cslot):
                    t.matmul(bank(2)[:, 0:512], ONESCb[:],
                             COMBROW[0:1, cslot:cslot + 512], start=True, stop=True)
                    t.matmul(bank(3)[:, 0:512], ONESCb[:],
                             COMBROW[0:1, cslot + 512:cslot + 1024], start=True,
                             stop=True).then_inc(smm.h, 1)
                p.op("tensor", fcb)
                smm.val += 1
                p.wait("vector", smm, smm.val)

                def fcbc(v):
                    v.tensor_copy(CBX[:, 0:512], bank(2)[:, 0:512])
                    v.tensor_copy(CBX[:, 512:1024], bank(3)[:, 0:512]
                                  ).then_inc(sv.h, 1)
                p.op("vector", fcbc)
                sv.val += 1
                p.wait("tensor", sv, sv.val)
                if k_ + 1 < len(ejobs):
                    jn = ejobs[k_ + 1]
                    pa_n, _, e_n = JOBS[jn]
                    p.wait("sync", smm, smm.val)
                    dma("sync", ld,
                        COMBROW[0:1, ((k_ + 1) % 2) * TP:((k_ + 1) % 2 + 1) * TP],
                        COMBT[e_n:e_n + 1, pa_n * TP: pa_n * TP + TP])
                    combrow_ready[k_ + 1] = ld.val

            evac_vals = {}
            for hc in range(4):
                hcw = hc0 + hc
                hb = PBIG0[:, (hc % 2) * 1024:(hc % 2) * 1024 + 1024]
                vb = PBIG1[:, 0:1024]

                if hc >= 2:
                    p.wait("tensor", sv, evac_vals[hc - 2])

                def fiph(t, win_s=win_s, hcw=hcw, hb=hb, t0=t0, kind=kind):
                    for dc in range(8):
                        if kind == "sh":
                            lw_ = win_s[:, dc * SHID + hcw * 128:
                                        dc * SHID + hcw * 128 + 128]
                        else:
                            lw_ = win_s[:, dc * EH + hcw * 128:
                                        dc * EH + hcw * 128 + 128]
                        for j in range(2):
                            mmi = t.matmul(hb[:, j * 512:(j + 1) * 512], lw_,
                                           LNT[:, dc * T + t0 + j * 512:
                                               dc * T + t0 + j * 512 + 512],
                                           start=(dc == 0), stop=(dc == 7))
                            if dc == 7 and j == 1:
                                mmi.then_inc(smm.h, 1)
                p.op("tensor", fiph)
                smm.val += 1
                hin_mm = smm.val

                if hc >= 1:
                    p.wait("tensor", sv, evac_vals[hc - 1])

                def fipv(t, vin_s=vin_s, hcw=hcw, vb=vb, t0=t0, kind=kind):
                    for dc in range(8):
                        if kind == "sh":
                            lv_ = vin_s[:, dc * SHID + hcw * 128:
                                        dc * SHID + hcw * 128 + 128]
                        else:
                            lv_ = vin_s[:, dc * EH + hcw * 128:
                                        dc * EH + hcw * 128 + 128]
                        for j in range(2):
                            mmi = t.matmul(vb[:, j * 512:(j + 1) * 512], lv_,
                                           LNT[:, dc * T + t0 + j * 512:
                                               dc * T + t0 + j * 512 + 512],
                                           start=(dc == 0), stop=(dc == 7))
                            if dc == 7 and j == 1:
                                mmi.then_inc(smm.h, 1)
                p.op("tensor", fipv)
                smm.val += 1
                p.wait("scalar", smm, hin_mm)

                def fsi(s, hb=hb):
                    s.activation(MTMP[:], hb, AF.Silu, bias=0.0, scale=1.0
                                 ).then_inc(ssc.h, 1)
                p.op("scalar", fsi)
                ssc.val += 1
                p.wait("vector", ssc, ssc.val)
                p.wait("vector", smm, smm.val)

                def fhm(v, vb=vb, hc=hc, kind=kind):
                    if kind == "e":
                        v.tensor_tensor(HE[:, hc * TP:(hc + 1) * TP], MTMP[:], vb,
                                        A.mult)
                        v.tensor_tensor(HE[:, hc * TP:(hc + 1) * TP],
                                        HE[:, hc * TP:(hc + 1) * TP],
                                        CBX[:, 0:TP], A.mult).then_inc(sv.h, 1)
                    else:
                        v.tensor_tensor(HE[:, hc * TP:(hc + 1) * TP], MTMP[:], vb,
                                        A.mult).then_inc(sv.h, 1)
                p.op("vector", fhm)
                sv.val += 1
                evac_vals[hc] = sv.val
            in_done[ji] = smm.val
            h_ready = sv.val
            p.wait("tensor", sv, h_ready)

            if kind == "e":
                k = eidx[ji]
                if ji - 1 in out_done:
                    p.wait("gpsimd", smm, out_done[ji - 1])
                dma("gpsimd", wt2, WOUTb, ewout_d[e])
                wout_ready[ji] = wt2.val
                if k + 2 < len(ejobs):
                    p.wait("gpsimd", smm, in_done[ji])
                    e2 = JOBS[ejobs[k + 2]][2]
                    dma("gpsimd", wt2, WINb[(k + 2) % 2], ewin_d[e2])
                    dma("gpsimd", wt2, VINb[(k + 2) % 2], evin_d[e2])
                    winv_ready[ejobs[k + 2]] = wt2.val

            if kind == "e":
                p.wait("tensor", wt2, wout_ready[ji])
            facc_vals = {}
            for dc in range(8):
                ob = PBIG0[:, 0:1024] if dc % 2 == 0 else PBIG1[:, 0:1024]
                if dc >= 2:
                    p.wait("tensor", sv, facc_vals[dc - 2])

                def fop(t, dc=dc, ob=ob, hc0=hc0, kind=kind):
                    for hc in range(4):
                        hcw = hc0 + hc
                        if kind == "sh":
                            lhs = WSOUT[:, hcw * D + dc * 128:
                                        hcw * D + dc * 128 + 128]
                        else:
                            lhs = WOUTb[:, hc * D + dc * 128:
                                        hc * D + dc * 128 + 128]
                        for j in range(2):
                            mmi = t.matmul(ob[:, j * 512:(j + 1) * 512], lhs,
                                           HE[:, hc * TP + j * 512:
                                               hc * TP + j * 512 + 512],
                                           start=(hc == 0), stop=(hc == 3))
                            if hc == 3 and j == 1:
                                mmi.then_inc(smm.h, 1)
                p.op("tensor", fop)
                smm.val += 1
                p.wait("vector", smm, smm.val)

                def facc(v, dc=dc, ob=ob, first=first):
                    dst = ACCH[:, dc * TP:(dc + 1) * TP]
                    if first:
                        v.tensor_copy(dst, ob).then_inc(sv.h, 1)
                    else:
                        v.tensor_tensor(dst, dst, ob, A.add).then_inc(sv.h, 1)
                p.op("vector", facc)
                sv.val += 1
                facc_vals[dc] = sv.val
            out_done[ji] = smm.val
            last_facc[ji] = sv.val

            if kind == "e" and e == NE - 1:
                moe_done_v = sv.val
                if pa == 0:
                    p.wait("sync", ld, x2_spill)
                for itl, fdummy in [(0, True)] + [(i, False) for i in range(8)]:
                    itg = pa * 8 + itl
                    pi = itl % 2
                    p.wait("sync", sv, sv.val)
                    dma("sync", ld, X2R[:], scr2[:, itg * D:(itg + 1) * D])
                    x2r_done = ld.val
                    p.wait("tensor", sv, sv.val)
                    tbf = PBIG0[:, pi * 1024:(pi + 1) * 1024]

                    def ftf(t, itl=itl, tbf=tbf):
                        for c in range(8):
                            mmi = t.transpose(tbf[:, c * 128:(c + 1) * 128],
                                              ACCH[:, c * TP + itl * 128:
                                                   c * TP + itl * 128 + 128],
                                              IDF[:])
                            if c == 7:
                                mmi.then_inc(smm.h, 1)
                    p.op("tensor", ftf)
                    smm.val += 1
                    p.wait("vector", smm, smm.val)
                    ln_stats(tbf, D, 1, TMPA[:])
                    p.wait("vector", ld, x2r_done)
                    if out_stores > 0:
                        p.wait("vector", sst, out_stores)

                    def ffin(v, tbf=tbf):
                        v.reciprocal(STT[:, 15:16], SQS[:, 2:3])
                        v.drain()
                        v.tensor_scalar(TMPA[:], tbf, STT[:, 12:13], STT[:, 15:16],
                                        A.subtract, A.mult)
                        v.tensor_tensor(OUTT[:], X2R[:], TMPA[:], A.add
                                        ).then_inc(sv.h, 1)
                    p.op("vector", ffin)
                    sv.val += 1
                    if not fdummy:
                        p.wait("sync", sv, sv.val)
                        dma("sync", sst, out_d[:, itg * D:(itg + 1) * D], OUTT[:])
                        out_stores = sst.val
                final_last_v = sv.val

        tap("acc0", ACCH[:], sv, sv.val)

        p.wait("gpsimd", sst, sst.val)
        return _close(block, p, nc)


def _finish(p, sst, out_d, ACCH):
    p.wait("gpsimd", sst, sst.val)


def _close(block, p, nc):
    if True:
        @block.sync
        def _(e):
            for fn in p.q["sync"]:
                fn(e)

        @block.gpsimd
        def _(e):
            for fn in p.q["gpsimd"]:
                fn(e)

        @block.tensor
        def _(e):
            for fn in p.q["tensor"]:
                fn(e)

        @block.vector
        def _(e):
            for fn in p.q["vector"]:
                fn(e)

        @block.scalar
        def _(e):
            for fn in p.q["scalar"]:
                fn(e)

    return nc


def _layouts(inputs):
    f = np.ascontiguousarray
    r = {}
    r["dw"] = f(inputs["L0_attn_down_w"].reshape(8, 128, dA).transpose(1, 0, 2)
                .reshape(128, 8 * dA))
    r["db"] = f(inputs["L0_attn_down_b"].reshape(2, 128).T)
    for nm, kk in [("wq", "L0_attn_wq"), ("wk", "L0_attn_wk"),
                   ("wv", "L0_attn_wv"), ("wo", "L0_attn_wo")]:
        r[nm] = f(inputs[kk].reshape(2, 128, dA).transpose(1, 0, 2).reshape(128, 2 * dA))
    r["uw"] = f(inputs["L0_attn_up_w"].reshape(2, 128, D).transpose(1, 0, 2)
                .reshape(128, 2 * D))
    r["upb"] = f(inputs["L0_attn_up_b"].reshape(1, D))
    r["w1"] = f(np.concatenate([inputs["L0_attn_W1"][0], inputs["L0_attn_W1"][1]], 1))
    r["b1"] = f(inputs["L0_attn_b1"].reshape(1, 2 * FH))
    r["glw"] = f(inputs["L0_attn_ttt_ln_w"].reshape(1, 2 * FH))
    r["glb"] = f(inputs["L0_attn_ttt_ln_b"].reshape(1, 2 * FH))
    r["lrw"] = f(inputs["L0_attn_lr_w"].reshape(2, 2, 128).transpose(2, 1, 0)
                 .reshape(128, 4))
    r["lrb"] = f(inputs["L0_attn_lr_b"].reshape(2, 1))
    r["tki"] = f(inputs["L0_attn_tok_idx"].reshape(1, K))
    r["pw"] = f(inputs["L0_attn_post_w"].reshape(1, dA))
    r["pb"] = f(inputs["L0_attn_post_b"].reshape(1, dA))
    r["lw"] = f(inputs["L0_attn_ln_w"].reshape(1, dA))
    r["lb"] = f(inputs["L0_attn_ln_b"].reshape(1, dA))
    r["gwt"] = f(inputs["L0_ffn_gate_w"].T.reshape(8, 128, NE).transpose(1, 0, 2)
                 .reshape(128, 8 * NE))
    r["wsin"] = f(inputs["L0_ffn_ws_in"].reshape(8, 128, SHID).transpose(1, 0, 2)
                  .reshape(128, 8 * SHID))
    r["vsin"] = f(inputs["L0_ffn_vs_in"].reshape(8, 128, SHID).transpose(1, 0, 2)
                  .reshape(128, 8 * SHID))
    r["wsout"] = f(inputs["L0_ffn_ws_out"].reshape(8, 128, D).transpose(1, 0, 2)
                   .reshape(128, 8 * D))
    r["ewin"] = f(inputs["L0_ffn_e_win"].reshape(NE, 8, 128, EH)
                  .transpose(0, 2, 1, 3).reshape(NE, 128, 8 * EH))
    r["evin"] = f(inputs["L0_ffn_e_vin"].reshape(NE, 8, 128, EH)
                  .transpose(0, 2, 1, 3).reshape(NE, 128, 8 * EH))
    r["ewout"] = f(inputs["L0_ffn_e_wout"].reshape(NE, 4, 128, D)
                   .transpose(0, 2, 1, 3).reshape(NE, 128, 4 * D))
    cstv = np.zeros((128, 8), np.float32)
    cstv[:, 0] = 1e-6
    cstv[:, 1] = 1e-5
    cstv[:, 2] = 1.0 / 128.0
    cstv[:, 3] = 1.0
    r["cst"] = cstv
    r["idf"] = np.eye(128, dtype=np.float32)
    r["ones16"] = np.ones((1, 16), np.float32)
    r["onesc"] = np.ones((1, 128), np.float32)
    r["negu"] = f(-np.triu(np.ones((16, 16), np.float32)) / 128.0)
    r["rk"] = (1.0 / np.arange(1, K + 1, dtype=np.float32)).reshape(1, K)
    return r


def kernel(**inputs):
    inputs = {k: np.asarray(v) for k, v in inputs.items()}
    shared = _layouts(inputs)
    x = inputs["L0_attn_x"]
    nc = build_program()
    in_maps = []
    for b in range(B):
        m = dict(shared)
        m["xin"] = np.ascontiguousarray(
            x[b].reshape(NT, 128, D).transpose(1, 0, 2).reshape(128, NT * D))
        in_maps.append(m)
    res = run_bass_kernel_spmd(nc, in_maps, core_ids=list(range(B)))
    outs = []
    for b in range(B):
        o = np.asarray(res.results[b]["out"])
        outs.append(o.reshape(128, NT, D).transpose(1, 0, 2).reshape(T, D))
    return np.stack(outs).astype(np.float32)


# version 03
# speedup vs baseline: 1.0902x; 1.0090x over previous
import numpy as np
import concourse.bass as bass
import concourse.mybir as mybir
from concourse.bass_utils import run_bass_kernel_spmd

F32 = mybir.dt.float32
BF16 = mybir.dt.bfloat16
A = mybir.AluOpType
AF = mybir.ActivationFunctionType
AX = mybir.AxisListType

B, T, D = 8, 2048, 1024
dA = 256
H, FH, K = 2, 128, 16
NSTEP = T // K
NT = T // 128
NE, EH, SHID = 16, 512, 1024
TP = T // 2
DEV_TAPS = []
STOP = 99
DEV_LN0_TILES = None
DEV_SCAN_STEPS = None


class Sem:
    def __init__(self, handle):
        self.h = handle
        self.val = 0


class P:
    def __init__(self):
        self.q = {e: [] for e in ("tensor", "vector", "scalar", "gpsimd", "sync")}

    def op(self, eng, fn):
        self.q[eng].append(fn)

    def wait(self, eng, sem, val):
        if val > 0:
            self.q[eng].append(lambda e, s=sem.h, v=val: e.wait_ge(s, v))


def build_program():
    nc = bass.Bass("TRN2", target_bir_lowering=False)

    def din(name, shape, dtype=F32):
        return nc.dram_tensor(name, list(shape), dtype, kind="ExternalInput").ap()

    xin = din("xin", (128, NT * D))
    dw_d = din("dw", (128, 8 * dA)); db_d = din("db", (128, 2))
    wq_d = din("wq", (128, 2 * dA)); wk_d = din("wk", (128, 2 * dA))
    wv_d = din("wv", (128, 2 * dA)); wo_d = din("wo", (128, 2 * dA))
    uw_d = din("uw", (128, 2 * D)); upb_d = din("upb", (1, D))
    w1_d = din("w1", (128, 2 * FH)); b1_d = din("b1", (1, 2 * FH))
    glw_d = din("glw", (1, 2 * FH)); glb_d = din("glb", (1, 2 * FH))
    lrw_d = din("lrw", (128, 4)); lrb_d = din("lrb", (2, 1))
    tki_d = din("tki", (1, K))
    pw_d = din("pw", (1, dA)); pb_d = din("pb", (1, dA))
    lw_d = din("lw", (1, dA)); lb_d = din("lb", (1, dA))
    gwt_d = din("gwt", (128, 8 * NE))
    wsin_d = din("wsin", (128, 8 * SHID)); vsin_d = din("vsin", (128, 8 * SHID))
    wsout_d = din("wsout", (128, 8 * D))
    ewin_d = din("ewin", (NE, 128, 8 * EH))
    evin_d = din("evin", (NE, 128, 8 * EH))
    ewout_d = din("ewout", (NE, 128, 4 * D))
    cst_d = din("cst", (128, 8))
    idf_d = din("idf", (128, 128))
    ones16_d = din("ones16", (1, 16))
    onesc_d = din("onesc", (1, 128))
    negu_d = din("negu", (16, 16))
    rk_d = din("rk", (1, K))

    out_d = nc.dram_tensor("out", [128, NT * D], F32, kind="ExternalOutput").ap()
    scr = nc.dram_tensor("scr", [2 * T], F32, kind="Internal").ap()
    scr2 = nc.dram_tensor("scr2", [128, NT * D], BF16, kind="Internal").ap()

    taps = {}
    tap_specs = {
        "lnt": [128, 8 * T], "ht": [128, 2 * T], "xqt": [128, 2 * T],
        "xkt": [128, 2 * T], "siglr": [2, T], "xkle": [128, NT * dA],
        "tgt2": [128, NT * dA], "zbar": [128, NT * dA], "x2": [128, NT * D],
        "lnt3": [128, 8 * T], "gval": [128, NT * NE], "combt": [16, T],
        "w1f": [128, 2 * FH], "b1f": [1, 2 * FH], "sigt": [128, 2 * K],
        "lrc": [16, 2 * NSTEP], "acc0": [128, 8 * TP],
        "he": [128, D], "stt": [128, 20], "xktmp": [128, dA],
        "mv": [16, 4], "rstd": [16, 2], "grb": [16, 2 * FH], "xhs": [16, 2 * FH],
        "gys": [16, 2 * FH], "ctb": [16, 2 * K], "m1n": [16, 2], "m2": [16, 2], "std": [16, 2], "sqs": [128, 4],
        "tme": [128, dA], "tmpa": [128, D], "tmpd": [128, D], "ptt": [128, dA],
    }
    for name in DEV_TAPS:
        dt_ = BF16 if name in ("combt",) else F32
        taps[name] = nc.dram_tensor("tap_" + name, tap_specs[name], dt_,
                                    kind="ExternalOutput").ap()

    p = P()

    from contextlib import ExitStack
    with ExitStack() as _es:
        XRES = _es.enter_context(nc.sbuf_tensor("XRES", [128, NT * D], BF16))
        LNT = _es.enter_context(nc.sbuf_tensor("LNT", [128, 8 * T], BF16))
        BIGP = _es.enter_context(nc.sbuf_tensor("BIGP", [128, 4 * T], BF16))
        BIGQ = _es.enter_context(nc.sbuf_tensor("BIGQ", [128, 4 * T], BF16))
        TGT2F = _es.enter_context(nc.sbuf_tensor("TGT2F", [128, 2 * NT * dA], BF16))
        HE = _es.enter_context(nc.sbuf_tensor("HE", [128, 4 * TP], BF16))
        WBIGA = _es.enter_context(nc.sbuf_tensor("WBIGA", [128, 8 * dA + 2 * D], BF16))
        WINbT = _es.enter_context(nc.sbuf_tensor("WINb", [128, 2 * 8 * EH], BF16))
        VINbT = _es.enter_context(nc.sbuf_tensor("VINb", [128, 2 * 8 * EH], BF16))
        CBX = _es.enter_context(nc.sbuf_tensor("CBX", [128, TP], BF16))
        TGT16R = _es.enter_context(nc.sbuf_tensor("TGT16R", [16, 4 * dA], BF16))
        XKLE16R = _es.enter_context(nc.sbuf_tensor("XKLE16R", [16, 4 * dA], BF16))
        ZB16 = _es.enter_context(nc.sbuf_tensor("ZB16", [16, 2 * dA], BF16))
        COMBROW = _es.enter_context(nc.sbuf_tensor("COMBROW", [1, 2 * TP], BF16))
        MTMP = _es.enter_context(nc.sbuf_tensor("MTMP", [128, TP], BF16))
        COMBT = _es.enter_context(nc.sbuf_tensor("COMBT", [16, T], BF16))
        DB = _es.enter_context(nc.sbuf_tensor("DB", [128, 2], F32))
        WQ = _es.enter_context(nc.sbuf_tensor("WQ", [128, 2 * dA], BF16))
        WK = _es.enter_context(nc.sbuf_tensor("WK", [128, 2 * dA], BF16))
        WV = _es.enter_context(nc.sbuf_tensor("WV", [128, 2 * dA], BF16))
        WO = _es.enter_context(nc.sbuf_tensor("WO", [128, 2 * dA], BF16))
        UPB = _es.enter_context(nc.sbuf_tensor("UPB", [1, D], BF16))
        W1f = _es.enter_context(nc.sbuf_tensor("W1f", [128, 2 * FH], F32))
        W1b = _es.enter_context(nc.sbuf_tensor("W1b", [128, 2 * FH], BF16))
        B1f = _es.enter_context(nc.sbuf_tensor("B1f", [1, 2 * FH], F32))
        LRWT = _es.enter_context(nc.sbuf_tensor("LRWT", [128, 4], BF16))
        LRB = _es.enter_context(nc.sbuf_tensor("LRB", [2, 1], F32))
        GWT = _es.enter_context(nc.sbuf_tensor("GWT", [128, 8 * NE], F32))
        CST = _es.enter_context(nc.sbuf_tensor("CST", [128, 8], F32))
        IDF = _es.enter_context(nc.sbuf_tensor("IDF", [128, 128], F32))
        IDB = _es.enter_context(nc.sbuf_tensor("IDB", [128, 128], BF16))
        ONES16f = _es.enter_context(nc.sbuf_tensor("ONES16f", [1, 16], F32))
        ONESCb = _es.enter_context(nc.sbuf_tensor("ONESCb", [1, 128], BF16))
        NEGU = _es.enter_context(nc.sbuf_tensor("NEGU", [16, 16], F32))
        RK = _es.enter_context(nc.sbuf_tensor("RK", [1, K], F32))
        TI = _es.enter_context(nc.sbuf_tensor("TI", [1, K], F32))
        TI15b = _es.enter_context(nc.sbuf_tensor("TI15b", [128, 1], F32))
        GB = _es.enter_context(nc.sbuf_tensor("GB", [128, 2 * FH], F32))
        BB = _es.enter_context(nc.sbuf_tensor("BB", [128, 2 * FH], F32))
        G2B = _es.enter_context(nc.sbuf_tensor("G2B", [128, 2 * FH], F32))
        PWB = _es.enter_context(nc.sbuf_tensor("PWB", [128, dA], F32))
        PBB = _es.enter_context(nc.sbuf_tensor("PBB", [128, dA], F32))
        LWB = _es.enter_context(nc.sbuf_tensor("LWB", [128, dA], F32))
        LBB = _es.enter_context(nc.sbuf_tensor("LBB", [128, dA], F32))
        SIGT = _es.enter_context(nc.sbuf_tensor("SIGT", [128, 2 * NT], F32))
        LETOK = _es.enter_context(nc.sbuf_tensor("LETOK", [128, 2 * NT], F32))
        LECB = _es.enter_context(nc.sbuf_tensor("LECB", [16, 2 * NSTEP], BF16))
        TOKLN = _es.enter_context(nc.sbuf_tensor("TOKLN", [16, 16], F32))
        BN6 = _es.enter_context(nc.sbuf_tensor("BN6", [16, 12], F32))
        MV = _es.enter_context(nc.sbuf_tensor("MV", [16, 4], F32))
        STD = _es.enter_context(nc.sbuf_tensor("STD", [16, 2], F32))
        RSTD = _es.enter_context(nc.sbuf_tensor("RSTD", [16, 2], F32))
        M1N = _es.enter_context(nc.sbuf_tensor("M1N", [16, 2], F32))
        M2 = _es.enter_context(nc.sbuf_tensor("M2", [16, 2], F32))
        CT1 = _es.enter_context(nc.sbuf_tensor("CT1", [16, 2 * K], F32))
        CTB = _es.enter_context(nc.sbuf_tensor("CTB", [16, 2 * K], BF16))
        TMPA = _es.enter_context(nc.sbuf_tensor("TMPA", [128, D], F32))
        TMPD = _es.enter_context(nc.sbuf_tensor("TMPD", [128, D], F32))
        TME = _es.enter_context(nc.sbuf_tensor("TME", [128, dA], F32))
        TME2 = _es.enter_context(nc.sbuf_tensor("TME2", [128, dA], F32))
        XKtmp = _es.enter_context(nc.sbuf_tensor("XKtmp", [128, dA], F32))
        TMB = _es.enter_context(nc.sbuf_tensor("TMB", [128, dA], BF16))
        PTT = _es.enter_context(nc.sbuf_tensor("PTT", [128, dA], BF16))
        STT = _es.enter_context(nc.sbuf_tensor("STT", [128, 20], F32))
        SQS = _es.enter_context(nc.sbuf_tensor("SQS", [128, 4], F32))
        PBIG0 = _es.enter_context(nc.psum_tensor("PBIG0", [128, 2048], F32))
        PBIG1 = _es.enter_context(nc.psum_tensor("PBIG1", [128, 1024], F32))
        PTRA = _es.enter_context(nc.psum_tensor("PTRA", [128, 1024], BF16))
        PTRB = _es.enter_context(nc.psum_tensor("PTRB", [128, 1024], BF16))
        _ld = _es.enter_context(nc.semaphore("s_ld"))
        _wt = _es.enter_context(nc.semaphore("s_wt"))
        _mm = _es.enter_context(nc.semaphore("s_mm"))
        _v = _es.enter_context(nc.semaphore("s_v"))
        _sc = _es.enter_context(nc.semaphore("s_sc"))
        _g = _es.enter_context(nc.semaphore("s_g"))
        _st = _es.enter_context(nc.semaphore("s_st"))
        block = _es.enter_context(nc.Block())

        ld, wt2, smm, sv, ssc, sg, sst = (Sem(_ld), Sem(_wt), Sem(_mm), Sem(_v),
                                          Sem(_sc), Sem(_g), Sem(_st))

        OUTT = TMPD
        SIGLR = WINbT[:].bitcast(F32)[0:2, 0:T]
        ROWS = VINbT[:].bitcast(F32)[0:1, 0:6 * dA]
        LRC = WINbT[:].bitcast(F32)[0:16, 2048:2048 + 2 * NSTEP]
        GVAL = VINbT[:].bitcast(F32)[:, 1536:1536 + NT * NE]
        SCR_SC = MTMP[:].bitcast(F32)[0:16, 0:128]
        X2R = MTMP
        TI15 = TI15b[0:16, :]
        XHs = TMPD[0:16, 0:256]
        GYs = TMPD[0:16, 256:512]
        PRD = TMPD[0:16, 512:768]
        GRF = TMPD[0:16, 768:1024]
        GRB = TMPA[:].bitcast(BF16)[0:16, 0:256]
        Z1S = TMPA[0:16, 256:512]
        GWRK = TME2[:, 0:16]
        GEQ = TME2[:, 16:32]
        COMB = TME2[:, 32:48]
        GMAX = TME2[:, 48:49]
        RS = TME2[:, 49:50]
        WINb = [WINbT[:, 0:4096], WINbT[:, 4096:8192]]
        VINb = [VINbT[:, 0:4096], VINbT[:, 4096:8192]]
        XQT = BIGP[:, 0:2 * T]
        XKT = BIGP[:, 2 * T:4 * T]
        HT = BIGQ[:, 0:2 * T]
        XKLE = BIGQ[:, 2 * T:2 * T + NT * dA]
        WSIN = BIGP[:]
        VSIN = BIGQ[:]
        TGT2 = TGT2F[:, 0:NT * dA]
        WSOUT = TGT2F[:]
        DW = WBIGA[:, 0:8 * dA]
        UW = WBIGA[:, 8 * dA:8 * dA + 2 * D]
        WOUTb = WBIGA[:]
        ACCH = XRES[:].bitcast(F32)

        def bank(i):
            assert i < 6
            if i < 4:
                return PBIG0[:, i * 512:(i + 1) * 512]
            return PBIG1[:, (i - 4) * 512:(i - 4 + 1) * 512]

        def tap(name, src, sem=None, val=0):
            if name in taps:
                if sem is not None:
                    p.wait("gpsimd", sem, val)
                dst = taps[name]
                def fn(g, dst=dst, s=src):
                    g.dma_start(out=dst, in_=s).then_inc(sst.h, 16)
                p.op("gpsimd", fn)
                sst.val += 16

        def dma(eng, sem, dst, src, **kw):
            def fn(e, dst=dst, src=src, sem=sem, kw=kw):
                e.dma_start(out=dst, in_=src, **kw).then_inc(sem.h, 16)
            p.op(eng, fn)
            sem.val += 16

        for dst, src in [(CST[:], cst_d[:]), (IDF[:], idf_d[:]),
                         (ONES16f[:], ones16_d[:]), (NEGU[:], negu_d[:]),
                         (RK[:], rk_d[:]), (W1f[:], w1_d[:]), (B1f[:], b1_d[:]),
                         (TI[:], tki_d[:]), (DB[:], db_d[:]), (GWT[:], gwt_d[:])]:
            dma("sync", ld, dst, src)
        dma("sync", ld, LRB[:], lrb_d[:], allow_slow_non_contiguous=True)
        for i, src in enumerate([glw_d, glb_d, pw_d, pb_d, lw_d, lb_d]):
            dma("sync", ld, ROWS[:, i * dA:(i + 1) * dA][:, 0:src.shape[1]], src)
        ld_small = ld.val

        for dst, src in [(XRES[:], xin[:]), (DW, dw_d[:]), (WQ[:], wq_d[:]),
                         (WK[:], wk_d[:]), (WV[:], wv_d[:]), (WO[:], wo_d[:]),
                         (UW, uw_d[:]), (UPB[:], upb_d[:]), (LRWT[:], lrw_d[:]),
                         (IDB[:], idf_d[:]), (ONESCb[:], onesc_d[:]),
                         (W1b[:], w1_d[:])]:
            dma("gpsimd", wt2, dst, src)
        wt_attn = wt2.val

        nc.const_aps.register(F32, 1e-6, CST[:, 0:1])
        nc.const_aps.register(F32, 1e-5, CST[:, 1:2])
        nc.const_aps.register(F32, 1.0 / 128.0, CST[:, 2:3])
        nc.const_aps.register(F32, 1.0, CST[:, 3:4])
        nc.const_aps.register(F32, -1.0, CST[:, 5:6])
        p.wait("tensor", ld, ld_small)
        p.wait("tensor", wt2, wt_attn)
        ONES128f = ONES16f[0:1, 0:1].broadcast_to([1, 128])

        bc_dst = [GB, BB, PWB, PBB, LWB, LBB]
        bc_w = [2 * FH, 2 * FH, dA, dA, dA, dA]

        def f_bc_mm(t):
            for i in range(6):
                t.matmul(bank(i)[:, 0:bc_w[i]], ONES128f,
                         ROWS[0:1, i * dA: i * dA + bc_w[i]], start=True,
                         stop=True).then_inc(smm.h, 1)
        p.op("tensor", f_bc_mm)
        smm.val += 6
        p.wait("vector", smm, smm.val)

        def f_bc_cp(v):
            for i in range(6):
                v.tensor_copy(bc_dst[i][:, 0:bc_w[i]], bank(i)[:, 0:bc_w[i]])
            v.tensor_tensor(G2B[:], GB[:], GB[:], A.mult)
            v.tensor_tensor(TI[:], TI[:], RK[:], A.add)
            v.drain()
            v.tensor_relu(TI[:], TI[:])
            v.drain()
            v.tensor_copy(STT[0:1, 0:1], TI[0:1, 0:1]).then_inc(sv.h, 1)
        p.op("vector", f_bc_cp)
        sv.val += 1
        p.wait("tensor", sv, sv.val)

        def f_tok(t):
            t.matmul(bank(5)[0:16, 0:16], ONES16f[:], TI[:],
                     start=True, stop=True).then_inc(smm.h, 1)
        p.op("tensor", f_tok)
        smm.val += 1
        p.wait("vector", smm, smm.val)

        def f_tok2(v):
            v.tensor_tensor(TOKLN[:], bank(5)[0:16, 0:16], NEGU[:],
                            A.mult).then_inc(sv.h, 1)
        p.op("vector", f_tok2)
        sv.val += 1
        p.wait("tensor", sv, sv.val)

        def f_ti15(t):
            t.matmul(bank(5)[:, 16:17], ONES128f, TI[0:1, 15:16],
                     start=True, stop=True).then_inc(smm.h, 1)
        p.op("tensor", f_ti15)
        smm.val += 1
        p.wait("vector", smm, smm.val)

        def f_ti15c(v):
            v.tensor_copy(TI15b[:], bank(5)[:, 16:17]).then_inc(sv.h, 1)
        p.op("vector", f_ti15c)
        sv.val += 1
        ti15_done = sv.val

        def ln_stats(eng_src, width, eps_col, scrap):
            w = width

            def f1(v, src=eng_src, scrap=scrap, w=w):
                v.tensor_reduce(STT[:, 9:10], src, AX.X, A.add)
                v.tensor_copy(scrap[:, 0:w], src)
                v.tensor_tensor(scrap[:, 0:w], scrap[:, 0:w], scrap[:, 0:w],
                                A.mult)
                v.tensor_reduce(STT[:, 10:11], scrap[:, 0:w], AX.X, A.add)
                v.drain()
                v.tensor_scalar(STT[:, 12:13], STT[:, 9:10], 1.0 / w, None, A.mult)
                v.tensor_scalar(STT[:, 13:14], STT[:, 10:11], 1.0 / w, None, A.mult)
                v.drain()
                v.tensor_tensor(STT[:, 16:17], STT[:, 12:13], STT[:, 12:13], A.mult)
                v.drain()
                v.tensor_tensor(STT[:, 13:14], STT[:, 13:14], STT[:, 16:17],
                                A.subtract)
                v.drain()
                v.tensor_copy(STT[:, 8:9], STT[:, 13:14]).then_inc(sv.h, 1)
            p.op("vector", f1)
            sv.val += 1
            p.wait("scalar", sv, sv.val)

            def f2(sc, eps_col=eps_col):
                sc.activation(SQS[:, 2:3], STT[:, 13:14], AF.Sqrt,
                              bias=CST[:, eps_col:eps_col + 1], scale=1.0
                              ).then_inc(ssc.h, 1)
            p.op("scalar", f2)
            ssc.val += 1
            p.wait("vector", ssc, ssc.val)

        p.wait("vector", wt2, wt_attn)
        tr_done = {}
        p.wait("scalar", wt2, wt_attn)
        for it in range(NT if DEV_LN0_TILES is None else DEV_LN0_TILES):
            xsrc = XRES[:, it * D:(it + 1) * D]
            ln_stats(xsrc, D, 1, TMPA[:])
            if it >= 1:
                p.wait("vector", smm, tr_done[it - 1])

            def f3(v, src=xsrc):
                v.reciprocal(STT[:, 15:16], SQS[:, 2:3])
                v.drain()
                v.tensor_scalar(HE[:, 0:D], src, STT[:, 12:13], STT[:, 15:16],
                                A.subtract, A.mult).then_inc(sv.h, 1)
            p.op("vector", f3)
            sv.val += 1
            p.wait("tensor", sv, sv.val)
            if it >= 2:
                p.wait("tensor", ssc, ssc.val - 1)

            def ftr(t, it=it):
                ptr = PTRA if it % 2 == 0 else PTRB
                for c in range(8):
                    mmi = t.transpose(ptr[:, c * 128:(c + 1) * 128],
                                      HE[:, c * 128:(c + 1) * 128], IDB[:])
                    if c == 7:
                        mmi.then_inc(smm.h, 1)
            p.op("tensor", ftr)
            smm.val += 1
            tr_done[it] = smm.val
            p.wait("scalar", smm, smm.val)

            def fcp(s, it=it):
                ptr = PTRA if it % 2 == 0 else PTRB
                for c in range(8):
                    s.copy(LNT[:, c * T + it * 128: c * T + it * 128 + 128],
                           ptr[:, c * 128:(c + 1) * 128])
                s.copy(SQS[0:1, 3:4], CST[0:1, 3:4]).then_inc(ssc.h, 1)
            p.op("scalar", fcp)
            ssc.val += 1
        ln0_done_sc = ssc.val
        tap("lnt", LNT[:], ssc, ln0_done_sc)
        tap("he", HE[:, 0:D], ssc, ln0_done_sc)
        tap("stt", STT[:], ssc, ln0_done_sc)

        if STOP <= 1:
            _finish(p, sst, out_d, ACCH)
            return _close(block, p, nc)
        p.wait("tensor", ssc, ln0_done_sc)
        dp_evac = {}
        for fc in range(2):
            for jt in range(4):
                gi = fc * 4 + jt
                bk = gi % 6
                if gi >= 6:
                    p.wait("tensor", ssc, dp_evac[gi - 6])

                def fdp(t, fc=fc, jt=jt, bk=bk):
                    for dc in range(8):
                        mmi = t.matmul(
                            bank(bk),
                            DW[:, dc * dA + fc * 128: dc * dA + fc * 128 + 128],
                            LNT[:, dc * T + jt * 512: dc * T + jt * 512 + 512],
                            start=(dc == 0), stop=(dc == 7))
                        if dc == 7:
                            mmi.then_inc(smm.h, 1)
                p.op("tensor", fdp)
                smm.val += 1
                p.wait("scalar", smm, smm.val)

                def fsl(s, fc=fc, jt=jt, bk=bk):
                    s.activation(HT[:, fc * T + jt * 512: fc * T + jt * 512 + 512],
                                 bank(bk), AF.Silu, bias=DB[:, fc:fc + 1],
                                 scale=1.0).then_inc(ssc.h, 1)
                p.op("scalar", fsl)
                ssc.val += 1
                dp_evac[gi] = ssc.val
        ht_done = ssc.val
        tap("ht", HT, ssc, ht_done)

        if STOP <= 2:
            _finish(p, sst, out_d, ACCH)
            return _close(block, p, nc)
        p.wait("tensor", ssc, ht_done)
        for j in range(4):
            def flr(t, j=j):
                for dc in range(2):
                    mmi = t.matmul(bank(j)[0:2, 0:512],
                                   LRWT[:, dc * 2: dc * 2 + 2],
                                   HT[:, dc * T + j * 512: dc * T + j * 512 + 512],
                                   start=(dc == 0), stop=(dc == 1))
                    if dc == 1:
                        mmi.then_inc(smm.h, 1)
            p.op("tensor", flr)
            smm.val += 1
            p.wait("scalar", smm, smm.val)

            def fsg(s, j=j):
                s.activation(SIGLR[:, j * 512:(j + 1) * 512], bank(j)[0:2, 0:512],
                             AF.Sigmoid, bias=LRB[:], scale=1.0).then_inc(ssc.h, 1)
            p.op("scalar", fsg)
            ssc.val += 1
        siglr_done = ssc.val
        tap("siglr", SIGLR[:], ssc, siglr_done)

        p.wait("sync", ssc, siglr_done)
        dma("sync", ld, scr[:], SIGLR[:])
        p.wait("sync", ld, ld.val)
        for h in range(H):
            dma("sync", ld, SIGT[:, h * NT:(h + 1) * NT],
                scr[h * T:(h + 1) * T].rearrange("(w q) -> q w", q=128),
                allow_slow_non_contiguous=True)
            dma("sync", ld, LRC[:, h * NSTEP:(h + 1) * NSTEP],
                scr[h * T:(h + 1) * T].rearrange("(n k) -> k n", k=K),
                allow_slow_non_contiguous=True)
        sig_resh = ld.val
        tap("sigt", SIGT[:], ld, sig_resh)
        tap("lrc", LRC[:], ld, sig_resh)

        p.wait("vector", ld, sig_resh)
        p.wait("vector", sv, ti15_done)

        def fle(v):
            v.tensor_scalar(LETOK[:], SIGT[:], TI15b[:], 1.0 / 128.0, A.mult, A.mult)
            v.tensor_scalar(LECB[:], LRC[:], TI15[:], 1.0 / 128.0, A.mult,
                            A.mult).then_inc(sv.h, 1)
        p.op("vector", fle)
        sv.val += 1
        letok_done = sv.val

        if STOP <= 3:
            _finish(p, sst, out_d, ACCH)
            return _close(block, p, nc)
        p.wait("tensor", ssc, siglr_done)
        pj_evac = {}
        for wi_, (wsrc, dstt) in enumerate(((WQ, XQT), (WK, XKT))):
            for fc in range(2):
                for jt in range(4):
                    gi = wi_ * 8 + fc * 4 + jt
                    bk = gi % 6
                    if gi >= 6:
                        p.wait("tensor", sv, pj_evac[gi - 6])

                    def fpj(t, wsrc=wsrc, fc=fc, jt=jt, bk=bk):
                        for dc in range(2):
                            mmi = t.matmul(
                                bank(bk),
                                wsrc[:, dc * dA + fc * 128: dc * dA + fc * 128 + 128],
                                HT[:, dc * T + jt * 512: dc * T + jt * 512 + 512],
                                start=(dc == 0), stop=(dc == 1))
                            if dc == 1:
                                mmi.then_inc(smm.h, 1)
                    p.op("tensor", fpj)
                    smm.val += 1
                    p.wait("vector", smm, smm.val)

                    def fev(v, dstt=dstt, fc=fc, jt=jt, bk=bk):
                        v.tensor_copy(dstt[:, fc * T + jt * 512: fc * T + jt * 512 + 512],
                                      bank(bk)).then_inc(sv.h, 1)
                    p.op("vector", fev)
                    sv.val += 1
                    pj_evac[gi] = sv.val
        proj_fmaj_done = sv.val
        tap("xqt", XQT, sv, proj_fmaj_done)
        tap("xkt", XKT, sv, proj_fmaj_done)

        if STOP <= 4:
            _finish(p, sst, out_d, ACCH)
            return _close(block, p, nc)
        p.wait("vector", sv, letok_done)
        p.wait("tensor", sv, proj_fmaj_done)
        tokmaj_evac = {}
        for it in range(NT):
            bkk = it % 2
            bkv = 2 + it % 2
            if it >= 2:
                p.wait("tensor", sv, tokmaj_evac[it - 2])

            def fkv(t, it=it, bkk=bkk, bkv=bkv):
                for dc in range(2):
                    t.matmul(bank(bkk)[:, 0:256],
                             HT[:, dc * T + it * 128: dc * T + it * 128 + 128],
                             WK[:, dc * dA: dc * dA + dA],
                             start=(dc == 0), stop=(dc == 1))
                for dc in range(2):
                    mmi = t.matmul(bank(bkv)[:, 0:256],
                                   HT[:, dc * T + it * 128: dc * T + it * 128 + 128],
                                   WV[:, dc * dA: dc * dA + dA],
                                   start=(dc == 0), stop=(dc == 1))
                    if dc == 1:
                        mmi.then_inc(smm.h, 1)
            p.op("tensor", fkv)
            smm.val += 1
            p.wait("scalar", smm, smm.val)

            def fxk(s, bkk=bkk):
                s.copy(XKtmp[:], bank(bkk)[:, 0:256]).then_inc(ssc.h, 1)
            p.op("scalar", fxk)
            ssc.val += 1
            p.wait("vector", ssc, ssc.val)

            def fxv(v, it=it, bkv=bkv):
                for h in range(H):
                    v.tensor_scalar(XKLE[:, it * dA + h * FH: it * dA + h * FH + FH],
                                    XKtmp[:, h * FH:(h + 1) * FH],
                                    LETOK[:, h * NT + it: h * NT + it + 1], None,
                                    A.mult)
                v.tensor_tensor(TME[:], XKtmp[:], bank(bkv)[:, 0:256], A.subtract)
                v.tensor_tensor(TME[:], TME[:], BB[:], A.add)
                v.tensor_tensor(TGT2[:, it * dA:(it + 1) * dA], TME[:], GB[:],
                                A.mult).then_inc(sv.h, 1)
            p.op("vector", fxv)
            sv.val += 1
            tokmaj_evac[it] = sv.val
        tokmaj_done = sv.val
        tap("xkle", XKLE, sv, tokmaj_done)
        tap("xktmp", XKtmp[:], sv, tokmaj_done)
        tap("tgt2", TGT2[:], sv, tokmaj_done)

        if STOP <= 5:
            _finish(p, sst, out_d, ACCH)
            return _close(block, p, nc)
        p.wait("tensor", sv, tokmaj_done)
        p.wait("tensor", sv, proj_fmaj_done)

        zbar_done = {}
        cast_done = {}
        grad_ready = {}
        upd_done = {}
        mm_all = {}

        tgt_pf = {}
        xkle_pf = {}
        stash_ld = {}
        NS_ = NSTEP if DEV_SCAN_STEPS is None else DEV_SCAN_STEPS
        for n in range(NS_):
            q = n % 2
            bA, bB, bC = bank(q), bank(2 + q), bank(4 + q)
            itile, r0 = n // 8, (n % 8) * 16
            slot = n % 4

            p.wait("sync", sv, tokmaj_evac[n // 8])
            if n >= 4:
                p.wait("sync", sv, grad_ready[n - 4])
                p.wait("sync", smm, mm_all[n - 4])
            dma("sync", ld, TGT16R[:, slot * dA:(slot + 1) * dA],
                TGT2[r0:r0 + 16, itile * dA:(itile + 1) * dA])
            tgt_pf[n] = ld.val
            dma("sync", ld, XKLE16R[:, slot * dA:(slot + 1) * dA],
                XKLE[r0:r0 + 16, itile * dA:(itile + 1) * dA])
            xkle_pf[n] = ld.val

            if n >= 1:
                p.wait("tensor", ssc, cast_done[n - 1])
            if n >= 2:
                p.wait("tensor", sv, upd_done[n - 2])
                p.wait("tensor", ssc, zbar_done[n - 2])
            p.wait("tensor", ld, xkle_pf[n])

            def f_z1(t, n=n, bA=bA):
                for h in range(H):
                    t.matmul(bA[0:16, h * FH:(h + 1) * FH],
                             XKT[:, h * T + n * K: h * T + n * K + K],
                             W1b[:, h * FH:(h + 1) * FH], start=True, stop=False)
                    t.matmul(bA[0:16, h * FH:(h + 1) * FH], ONES16f[:],
                             B1f[0:1, h * FH:(h + 1) * FH], start=False, stop=True)
                for h in range(H):
                    t.matmul(bA[32:48, h * K:(h + 1) * K],
                             XKT[:, h * T + n * K: h * T + n * K + K],
                             XQT[:, h * T + n * K: h * T + n * K + K],
                             start=True, stop=False)
                    mmi = t.matmul(bA[32:48, h * K:(h + 1) * K],
                                   ONESCb[0:1, 0:16], ONESCb[0:1, 0:16],
                                   start=False, stop=True)
                    if h == 1:
                        mmi.then_inc(smm.h, 1)
            p.op("tensor", f_z1)
            smm.val += 1

            mmz = smm.val
            p.wait("vector", smm, mmz)
            p.wait("scalar", smm, mmz)

            def f_sums(s, bA=bA):
                for h in range(H):
                    s.activation(SCR_SC, bA[0:16, h * FH:(h + 1) * FH], AF.Identity,
                                 bias=0.0, scale=1.0, accum_out=SQS[0:16, h:h + 1])
                for h in range(H):
                    s.activation(SCR_SC, bA[0:16, h * FH:(h + 1) * FH], AF.Square,
                                 bias=0.0, scale=1.0,
                                 accum_out=SQS[0:16, 2 + h:3 + h])
                s.drain()
                s.mul(BN6[0:16, 0:2], SQS[0:16, 0:2], 1.0 / 128.0)
                s.activation(BN6[0:16, 2:4], SQS[0:16, 0:2], AF.Square,
                             bias=0.0, scale=1.0 / 128.0)
                s.drain()
                s.mul(BN6[0:16, 4:6], BN6[0:16, 2:4], -1.0)
                s.drain()
                s.add(BN6[0:16, 6:8], BN6[0:16, 4:6], 1e-6)
                s.drain()
                for h in range(H):
                    s.activation(SQS[0:16, 2 + h:3 + h], SQS[0:16, 2 + h:3 + h],
                                 AF.Sqrt, bias=BN6[0:16, 6 + h:7 + h],
                                 scale=1.0 / 128.0)
                s.drain()
                s.copy(BN6[0:1, 8:9], CST[0:1, 3:4]).then_inc(ssc.h, 1)
            p.op("scalar", f_sums)
            ssc.val += 1
            p.wait("vector", ssc, ssc.val)
            p.wait("vector", ld, tgt_pf[n])

            def f_grad(v, n=n, bA=bA, slot=slot):
                v.reciprocal(RSTD[:], SQS[0:16, 2:4])
                v.drain()
                for h in range(H):
                    v.tensor_scalar(XHs[:, h * FH:(h + 1) * FH],
                                    bA[0:16, h * FH:(h + 1) * FH],
                                    BN6[0:16, h: h + 1], RSTD[:, h: h + 1],
                                    A.subtract, A.mult)
                v.tensor_tensor(CT1[:, 0:K], bA[32:48, 0:K], TOKLN[:], A.mult)
                v.tensor_tensor(CT1[:, K:2 * K], bA[32:48, K:2 * K], TOKLN[:],
                                A.mult)
                v.drain()
                for h in range(H):
                    v.tensor_scalar(CTB[:, h * K:(h + 1) * K],
                                    CT1[:, h * K:(h + 1) * K],
                                    LRC[:, h * NSTEP + n: h * NSTEP + n + 1], None,
                                    A.mult)
                v.tensor_tensor(GYs[:], XHs[:], G2B[0:16, :], A.mult)
                v.tensor_tensor(GYs[:], GYs[:],
                                TGT16R[:, slot * dA:(slot + 1) * dA], A.add)
                v.tensor_tensor(PRD[:], GYs[:], XHs[:], A.mult)
                for h in range(H):
                    v.tensor_reduce(M1N[:, h:h + 1], PRD[:, h * FH:(h + 1) * FH],
                                    AX.X, A.add, negate=True)
                v.drain()
                v.tensor_scalar(M1N[:], M1N[:], 1.0 / 128.0, None, A.mult)
                v.drain()
                for h in range(H):
                    v.tensor_scalar(PRD[:, h * FH:(h + 1) * FH],
                                    GYs[:, h * FH:(h + 1) * FH],
                                    1.0 / 128.0, 0.0, A.mult, A.add,
                                    accum_out=M2[:, h:h + 1])
                for h in range(H):
                    v.scalar_tensor_tensor(GRF[:, h * FH:(h + 1) * FH],
                                           XHs[:, h * FH:(h + 1) * FH],
                                           M1N[:, h:h + 1],
                                           GYs[:, h * FH:(h + 1) * FH],
                                           A.mult, A.add)
                v.drain()
                for h in range(H):
                    mmo = v.tensor_scalar(GRB[:, h * FH:(h + 1) * FH],
                                          GRF[:, h * FH:(h + 1) * FH],
                                          M2[:, h:h + 1], RSTD[:, h:h + 1],
                                          A.subtract, A.mult)
                    if h == 1:
                        mmo.then_inc(sv.h, 1)
            p.op("vector", f_grad)
            sv.val += 1
            grad_ready[n] = sv.val

            p.wait("tensor", sv, grad_ready[n])

            def f_zb(t, n=n, bA=bA, bB=bB, bC=bC, slot=slot):
                for h in range(H):
                    t.matmul(bB[0:16, h * FH:(h + 1) * FH],
                             XQT[:, h * T + n * K: h * T + n * K + K],
                             W1b[:, h * FH:(h + 1) * FH], start=True, stop=False)
                    t.matmul(bB[0:16, h * FH:(h + 1) * FH], ONES16f[:],
                             B1f[0:1, h * FH:(h + 1) * FH], start=False, stop=False)
                    t.matmul(bB[0:16, h * FH:(h + 1) * FH],
                             CTB[:, h * K:(h + 1) * K],
                             GRB[:, h * FH:(h + 1) * FH], start=False, stop=True)
                for h in range(H):
                    t.matmul(bC[0:128, h * FH:(h + 1) * FH],
                             XKLE16R[:, slot * dA + h * FH: slot * dA + (h + 1) * FH],
                             GRB[:, h * FH:(h + 1) * FH], start=True, stop=True)
                for h in range(H):
                    mmi = t.matmul(bA[64:65, 256 + h * FH: 256 + (h + 1) * FH],
                                   LECB[:, h * NSTEP + n: h * NSTEP + n + 1],
                                   GRB[:, h * FH:(h + 1) * FH], start=True, stop=True)
                    if h == 1:
                        mmi.then_inc(smm.h, 1)
            p.op("tensor", f_zb)
            smm.val += 1
            mm_all[n] = smm.val

            p.wait("scalar", smm, mm_all[n])
            if n >= 2:
                p.wait("scalar", ld, stash_ld[n - 2])

            def f_zc(s, bB=bB, n=n):
                s.copy(ZB16[:, (n % 2) * dA:(n % 2 + 1) * dA],
                       bB[0:16, 0:256]).then_inc(ssc.h, 1)
            p.op("scalar", f_zc)
            ssc.val += 1
            zbar_done[n] = ssc.val
            p.wait("sync", ssc, zbar_done[n])
            dma("sync", ld, TGT2[r0:r0 + 16, itile * dA:(itile + 1) * dA],
                ZB16[:, (n % 2) * dA:(n % 2 + 1) * dA])
            stash_ld[n] = ld.val

            p.wait("vector", smm, mm_all[n])

            def f_upd(v, bA=bA, bC=bC):
                v.tensor_tensor(W1f[:], W1f[:], bC[0:128, 0:256], A.subtract)
                v.tensor_tensor(B1f[:], B1f[:], bA[64:65, 256:512],
                                A.subtract).then_inc(sv.h, 1)
            p.op("vector", f_upd)
            sv.val += 1
            upd_done[n] = sv.val

            p.wait("scalar", sv, upd_done[n])

            def f_cast(s):
                s.copy(W1b[:], W1f[:]).then_inc(ssc.h, 1)
            p.op("scalar", f_cast)
            ssc.val += 1
            cast_done[n] = ssc.val

        scan_v, scan_sc = sv.val, ssc.val
        p.wait("gpsimd", ld, stash_ld[NS_ - 1])
        tap("zbar", TGT2[:], ssc, scan_sc)
        tap("w1f", W1f[:], ssc, scan_sc)
        tap("mv", MV[:], ssc, scan_sc)
        tap("rstd", RSTD[:], ssc, scan_sc)
        tap("grb", GRB, ssc, scan_sc)
        tap("xhs", XHs, ssc, scan_sc)
        tap("gys", GYs, ssc, scan_sc)
        tap("ctb", CTB[:], ssc, scan_sc)
        tap("m1n", M1N[:], ssc, scan_sc)
        tap("m2", M2[:], ssc, scan_sc)
        tap("std", STD[:], ssc, scan_sc)
        tap("sqs", SQS[:], ssc, scan_sc)
        tap("b1f", B1f[:], ssc, scan_sc)

        if STOP <= 6:
            _finish(p, sst, out_d, ACCH)
            return _close(block, p, nc)
        for eng in ("tensor", "vector", "scalar"):
            p.wait(eng, ssc, ssc.val)
            p.wait(eng, sv, sv.val)
            p.wait(eng, smm, smm.val)
            p.wait(eng, ld, ld.val)
            p.wait(eng, wt2, wt2.val)
        for it, dummy in [(0, True)] + [(i, False) for i in range(NT)]:
            b0, b1_, b3 = 0, 0, 1
            p.wait("tensor", sv, sv.val)

            def fq(t, it=it, b0=b0):
                for dc in range(2):
                    mmi = t.matmul(bank(b0)[:, 0:256],
                                   HT[:, dc * T + it * 128: dc * T + it * 128 + 128],
                                   WQ[:, dc * dA: dc * dA + dA],
                                   start=(dc == 0), stop=(dc == 1))
                    if dc == 1:
                        mmi.then_inc(smm.h, 1)
            p.op("tensor", fq)
            smm.val += 1

            zsrc = TGT2[:, it * dA:(it + 1) * dA]

            def fz1(v, zsrc=zsrc):
                v.tensor_reduce(STT[:, 0:1], zsrc[:, 0:128], AX.X, A.add)
                v.tensor_reduce(STT[:, 1:2], zsrc[:, 128:256], AX.X, A.add)
                v.tensor_tensor(TME2[:], zsrc, zsrc, A.mult)
                v.tensor_reduce(STT[:, 2:3], TME2[:, 0:128], AX.X, A.add)
                v.tensor_reduce(STT[:, 3:4], TME2[:, 128:256], AX.X, A.add)
                v.tensor_scalar(STT[:, 12:14], STT[:, 0:2], 1.0 / 128.0, None,
                                A.mult)
                v.tensor_scalar(STT[:, 2:4], STT[:, 2:4], 1.0 / 128.0, None,
                                A.mult)
                v.tensor_tensor(STT[:, 4:6], STT[:, 12:14], STT[:, 12:14], A.mult)
                v.tensor_tensor(STT[:, 2:4], STT[:, 2:4], STT[:, 4:6],
                                A.subtract).then_inc(sv.h, 1)
            p.op("vector", fz1)
            sv.val += 1
            p.wait("scalar", sv, sv.val)

            def fz2(s):
                s.activation(SQS[:, 2:4], STT[:, 2:4], AF.Sqrt,
                             bias=CST[:, 0:1], scale=1.0).then_inc(ssc.h, 1)
            p.op("scalar", fz2)
            ssc.val += 1
            p.wait("vector", ssc, ssc.val)
            p.wait("vector", smm, smm.val)
            def fz3(v, zsrc=zsrc, b0=b0):
                v.reciprocal(STT[:, 16:18], SQS[:, 2:4])
                v.drain()
                v.tensor_scalar(TME[:, 0:128], zsrc[:, 0:128], STT[:, 12:13],
                                STT[:, 16:17], A.subtract, A.mult)
                v.tensor_scalar(TME[:, 128:256], zsrc[:, 128:256], STT[:, 13:14],
                                STT[:, 17:18], A.subtract, A.mult)
                v.tensor_tensor(TME[:], TME[:], GB[:], A.mult)
                v.tensor_tensor(TME[:], TME[:], BB[:], A.add)
                v.tensor_tensor(TME[:], TME[:], bank(b0)[:, 0:256], A.add
                                ).then_inc(sv.h, 1)
            p.op("vector", fz3)
            sv.val += 1

            ln_stats(TME[:], dA, 1, TME2[:])
            def fp3(v):
                v.reciprocal(STT[:, 15:16], SQS[:, 2:3])
                v.drain()
                v.tensor_scalar(TME2[:], TME[:], STT[:, 12:13], STT[:, 15:16],
                                A.subtract, A.mult)
                v.tensor_tensor(TME2[:], TME2[:], PWB[:], A.mult)
                v.tensor_tensor(TMB[:], TME2[:], PBB[:], A.add).then_inc(sv.h, 1)
            p.op("vector", fp3)
            sv.val += 1
            p.wait("tensor", sv, sv.val)

            def ftw(t):
                t.transpose(PTRA[:, 0:128], TMB[:, 0:128], IDB[:])
                t.transpose(PTRA[:, 128:256], TMB[:, 128:256], IDB[:]
                            ).then_inc(smm.h, 1)
            p.op("tensor", ftw)
            smm.val += 1
            p.wait("scalar", smm, smm.val)
            def fpc(s):
                s.copy(PTT[:, 0:128], PTRA[:, 0:128])
                s.copy(PTT[:, 128:256], PTRA[:, 128:256]).then_inc(ssc.h, 1)
            p.op("scalar", fpc)
            ssc.val += 1
            p.wait("tensor", ssc, ssc.val)
            def fwo(t):
                for dc in range(2):
                    mmi = t.matmul(bank(0)[:, 256:512], PTT[:, dc * 128:(dc + 1) * 128],
                                   WO[:, dc * dA: dc * dA + dA],
                                   start=(dc == 0), stop=(dc == 1))
                    if dc == 1:
                        mmi.then_inc(smm.h, 1)
            p.op("tensor", fwo)
            smm.val += 1
            p.wait("vector", smm, smm.val)

            ln_stats(bank(0)[:, 256:512], dA, 1, TME2[:])
            def fl3(v):
                v.reciprocal(STT[:, 15:16], SQS[:, 2:3])
                v.drain()
                v.tensor_scalar(TME2[:], bank(0)[:, 256:512], STT[:, 12:13],
                                STT[:, 15:16], A.subtract, A.mult)
                v.tensor_tensor(TME2[:], TME2[:], LWB[:], A.mult)
                v.tensor_tensor(TMB[:], TME2[:], LBB[:], A.add).then_inc(sv.h, 1)
            p.op("vector", fl3)
            sv.val += 1
            p.wait("tensor", sv, sv.val)

            def ftu(t):
                t.transpose(PTRB[:, 0:128], TMB[:, 0:128], IDB[:])
                t.transpose(PTRB[:, 128:256], TMB[:, 128:256], IDB[:]
                            ).then_inc(smm.h, 1)
            p.op("tensor", ftu)
            smm.val += 1
            p.wait("scalar", smm, smm.val)
            def fpc2(s):
                s.copy(PTT[:, 0:128], PTRB[:, 0:128])
                s.copy(PTT[:, 128:256], PTRB[:, 128:256]).then_inc(ssc.h, 1)
            p.op("scalar", fpc2)
            ssc.val += 1
            p.wait("tensor", ssc, ssc.val)
            upps = PBIG1[:, 0:1024]
            def fup(t, upps=upps):
                for j in range(2):
                    for dc in range(2):
                        t.matmul(upps[:, j * 512:(j + 1) * 512],
                                 PTT[:, dc * 128:(dc + 1) * 128],
                                 UW[:, dc * D + j * 512: dc * D + j * 512 + 512],
                                 start=(dc == 0), stop=False)
                    mmi = t.matmul(upps[:, j * 512:(j + 1) * 512], ONESCb[:],
                                   UPB[0:1, j * 512:(j + 1) * 512],
                                   start=False, stop=True)
                    if j == 1:
                        mmi.then_inc(smm.h, 1)
            p.op("tensor", fup)
            smm.val += 1
            p.wait("vector", smm, smm.val)

            ln_stats(upps, D, 1, TMPA[:])
            xres_t = XRES[:, it * D:(it + 1) * D]
            def fx2(v, upps=upps, xres_t=xres_t, dummy=dummy):
                v.reciprocal(STT[:, 15:16], SQS[:, 2:3])
                v.drain()
                v.tensor_scalar(TMPA[:], upps, STT[:, 12:13], STT[:, 15:16],
                                A.subtract, A.mult)
                v.tensor_tensor(TMPD[:], xres_t, TMPA[:], A.add)
                if dummy:
                    v.tensor_copy(STT[:, 9:10], TMPD[:, 0:1]).then_inc(sv.h, 1)
                else:
                    v.tensor_copy(xres_t, TMPD[:]).then_inc(sv.h, 1)
            p.op("vector", fx2)
            sv.val += 1

            ln_stats(TMPD[:], D, 1, TMPA[:])
            def fl33(v):
                v.reciprocal(STT[:, 15:16], SQS[:, 2:3])
                v.drain()
                v.tensor_scalar(TMPA[:], TMPD[:], STT[:, 12:13], STT[:, 15:16],
                                A.subtract, A.mult).then_inc(sv.h, 1)
            p.op("vector", fl33)
            sv.val += 1
            p.wait("tensor", sv, sv.val)
            tb = PBIG0[:, 1024:2048]
            def ftr3(t, tb=tb):
                for c in range(8):
                    mmi = t.transpose(tb[:, c * 128:(c + 1) * 128],
                                      TMPA[:, c * 128:(c + 1) * 128], IDF[:])
                    if c == 7:
                        mmi.then_inc(smm.h, 1)
            p.op("tensor", ftr3)
            smm.val += 1
            p.wait("scalar", smm, smm.val)
            def fcp3(s, tb=tb):
                for c in range(8):
                    s.copy(TMPD[:, c * 128:(c + 1) * 128], tb[:, c * 128:(c + 1) * 128])
                s.copy(SQS[0:1, 3:4], CST[0:1, 3:4]).then_inc(ssc.h, 1)
            p.op("scalar", fcp3)
            ssc.val += 1
            p.wait("vector", ssc, ssc.val)
            p.wait("tensor", ssc, ssc.val)
            def fcst(v, it=it):
                v.tensor_copy(LNT[:].rearrange("p (c t) -> p c t", c=8)
                              [:, :, it * 128:(it + 1) * 128],
                              TMPD[:].rearrange("p (c t) -> p c t", c=8)
                              ).then_inc(sv.h, 1)
            p.op("vector", fcst)
            sv.val += 1
            gps = bank(2)
            def fgm(t, gps=gps):
                for dc in range(8):
                    mmi = t.matmul(gps[0:128, 0:16], TMPD[:, dc * 128:(dc + 1) * 128],
                                   GWT[:, dc * NE:(dc + 1) * NE],
                                   start=(dc == 0), stop=(dc == 7))
                    if dc == 7:
                        mmi.then_inc(smm.h, 1)
            p.op("tensor", fgm)
            smm.val += 1
            p.wait("scalar", smm, smm.val)
            def fsig(s, it=it, gps=gps):
                s.copy(GVAL[:, it * NE:(it + 1) * NE], gps[0:128, 0:16]
                       ).then_inc(ssc.h, 1)
            p.op("scalar", fsig)
            ssc.val += 1

        p.wait("scalar", ssc, ssc.val)

        def fsigall(s):
            s.activation(GVAL[:], GVAL[:], AF.Sigmoid, bias=0.0, scale=1.0
                         ).then_inc(ssc.h, 1)
        p.op("scalar", fsigall)
        ssc.val += 1
        p.wait("vector", ssc, ssc.val)
        for it in range(NT):
            def ftk(v, it=it):
                g0 = GVAL[:, it * NE:(it + 1) * NE]
                v.tensor_copy(GWRK[:], g0)
                v.drain()
                for _ in range(4):
                    v.reduce_max(GMAX[:], GWRK[:], AX.X)
                    v.drain()
                    v.tensor_scalar(GEQ[:], GWRK[:], GMAX[:], None, A.is_equal)
                    v.drain()
                    v.scalar_tensor_tensor(GWRK[:], GEQ[:], -1e30, GWRK[:],
                                           A.mult, A.add)
                    v.drain()
                v.tensor_tensor(COMB[:], GWRK[:], g0, A.not_equal)
                v.drain()
                v.tensor_tensor(COMB[:], COMB[:], g0, A.mult)
                v.drain()
                v.tensor_reduce(RS[:], COMB[:], AX.X, A.add)
                v.drain()
                v.reciprocal(RS[:], RS[:])
                v.drain()
                v.tensor_scalar(COMB[:], COMB[:], RS[:], None, A.mult
                                ).then_inc(sv.h, 1)
            p.op("vector", ftk)
            sv.val += 1
            p.wait("tensor", sv, sv.val)
            if it >= 1:
                p.wait("tensor", sv, sv.val)

            def ftc(t):
                t.transpose(bank(2)[0:16, 128:256], COMB[:], IDF[:]).then_inc(smm.h, 1)
            p.op("tensor", ftc)
            smm.val += 1
            p.wait("vector", smm, smm.val)

            def fcc(v, it=it):
                v.tensor_copy(COMBT[:, it * 128:(it + 1) * 128],
                              bank(2)[0:16, 128:256]).then_inc(sv.h, 1)
            p.op("vector", fcc)
            sv.val += 1
        post_v, post_sc = sv.val, ssc.val
        tap("x2", XRES[:], sv, post_v)
        tap("tme", TME[:], sv, post_v)
        tap("tmpa", TMPA[:], sv, post_v)
        tap("tmpd", TMPD[:], sv, post_v)
        tap("ptt", PTT[:], sv, post_v)
        tap("lnt3", LNT[:], sv, post_v)
        tap("gval", GVAL[:], sv, post_v)
        tap("combt", COMBT[:], sv, post_v)

        p.wait("sync", sv, post_v)
        dma("sync", ld, scr2[:], XRES[:])
        x2_spill = ld.val

        if STOP <= 7:
            _finish(p, sst, out_d, ACCH)
            return _close(block, p, nc)
        p.wait("gpsimd", sv, post_v)
        dma("gpsimd", wt2, WSIN, wsin_d[:])
        dma("gpsimd", wt2, VSIN, vsin_d[:])
        dma("gpsimd", wt2, WSOUT, wsout_d[:])
        JOBS = []
        for pa in range(2):
            JOBS.append((pa, "sh", 0))
            JOBS.append((pa, "sh", 1))
            for e in range(NE):
                JOBS.append((pa, "e", e))
        ejobs = [ji for ji, j in enumerate(JOBS) if j[1] == "e"]
        eidx = {ji: k for k, ji in enumerate(ejobs)}
        winv_ready = {}
        wout_ready = {}
        combrow_ready = {}
        in_done = {}
        out_done = {}
        for k in range(2):
            e = JOBS[ejobs[k]][2]
            dma("gpsimd", wt2, WINb[k % 2], ewin_d[e])
            dma("gpsimd", wt2, VINb[k % 2], evin_d[e])
            winv_ready[ejobs[k]] = wt2.val
        wt_shared_all = wt2.val

        last_facc = {}
        final_last_v = 0
        out_stores = 0
        p.wait("vector", ld, x2_spill)

        for ji, (pa, kind, e) in enumerate(JOBS):
            t0 = pa * TP
            first = (kind == "sh" and e == 0)
            hc0 = 4 * e if kind == "sh" else 0

            if ji == 0:
                p.wait("tensor", wt2, wt_shared_all)
            if pa == 1 and kind == "sh" and e == 0:
                p.wait("tensor", sv, final_last_v)
            if ji > 0:
                p.wait("tensor", sv, last_facc[ji - 1])
            if kind == "e":
                p.wait("tensor", wt2, winv_ready[ji])
                win_s, vin_s = WINb[eidx[ji] % 2], VINb[eidx[ji] % 2]
            else:
                win_s, vin_s = WSIN, VSIN

            if kind == "e":
                k_ = eidx[ji]
                if k_ not in combrow_ready:
                    p.wait("sync", smm, smm.val)
                    dma("sync", ld, COMBROW[0:1, (k_ % 2) * TP:(k_ % 2 + 1) * TP],
                        COMBT[e:e + 1, t0: t0 + TP])
                    combrow_ready[k_] = ld.val
                p.wait("tensor", ld, combrow_ready[k_])
                cslot = (k_ % 2) * TP

                def fcb(t, cslot=cslot):
                    t.matmul(bank(2)[:, 0:512], ONESCb[:],
                             COMBROW[0:1, cslot:cslot + 512], start=True, stop=True)
                    t.matmul(bank(3)[:, 0:512], ONESCb[:],
                             COMBROW[0:1, cslot + 512:cslot + 1024], start=True,
                             stop=True).then_inc(smm.h, 1)
                p.op("tensor", fcb)
                smm.val += 1
                p.wait("vector", smm, smm.val)

                def fcbc(v):
                    v.tensor_copy(CBX[:, 0:512], bank(2)[:, 0:512])
                    v.tensor_copy(CBX[:, 512:1024], bank(3)[:, 0:512]
                                  ).then_inc(sv.h, 1)
                p.op("vector", fcbc)
                sv.val += 1
                p.wait("tensor", sv, sv.val)
                if k_ + 1 < len(ejobs):
                    jn = ejobs[k_ + 1]
                    pa_n, _, e_n = JOBS[jn]
                    p.wait("sync", smm, smm.val)
                    dma("sync", ld,
                        COMBROW[0:1, ((k_ + 1) % 2) * TP:((k_ + 1) % 2 + 1) * TP],
                        COMBT[e_n:e_n + 1, pa_n * TP: pa_n * TP + TP])
                    combrow_ready[k_ + 1] = ld.val

            evac_vals = {}
            for hc in range(4):
                hcw = hc0 + hc
                hb = PBIG0[:, (hc % 2) * 1024:(hc % 2) * 1024 + 1024]
                vb = PBIG1[:, 0:1024]

                if hc >= 2:
                    p.wait("tensor", sv, evac_vals[hc - 2])

                def fiph(t, win_s=win_s, hcw=hcw, hb=hb, t0=t0, kind=kind):
                    for dc in range(8):
                        if kind == "sh":
                            lw_ = win_s[:, dc * SHID + hcw * 128:
                                        dc * SHID + hcw * 128 + 128]
                        else:
                            lw_ = win_s[:, dc * EH + hcw * 128:
                                        dc * EH + hcw * 128 + 128]
                        for j in range(2):
                            mmi = t.matmul(hb[:, j * 512:(j + 1) * 512], lw_,
                                           LNT[:, dc * T + t0 + j * 512:
                                               dc * T + t0 + j * 512 + 512],
                                           start=(dc == 0), stop=(dc == 7))
                            if dc == 7 and j == 1:
                                mmi.then_inc(smm.h, 1)
                p.op("tensor", fiph)
                smm.val += 1
                hin_mm = smm.val

                if hc >= 1:
                    p.wait("tensor", sv, evac_vals[hc - 1])

                def fipv(t, vin_s=vin_s, hcw=hcw, vb=vb, t0=t0, kind=kind):
                    for dc in range(8):
                        if kind == "sh":
                            lv_ = vin_s[:, dc * SHID + hcw * 128:
                                        dc * SHID + hcw * 128 + 128]
                        else:
                            lv_ = vin_s[:, dc * EH + hcw * 128:
                                        dc * EH + hcw * 128 + 128]
                        for j in range(2):
                            mmi = t.matmul(vb[:, j * 512:(j + 1) * 512], lv_,
                                           LNT[:, dc * T + t0 + j * 512:
                                               dc * T + t0 + j * 512 + 512],
                                           start=(dc == 0), stop=(dc == 7))
                            if dc == 7 and j == 1:
                                mmi.then_inc(smm.h, 1)
                p.op("tensor", fipv)
                smm.val += 1
                p.wait("scalar", smm, hin_mm)

                def fsi(s, hb=hb):
                    s.activation(MTMP[:], hb, AF.Silu, bias=0.0, scale=1.0
                                 ).then_inc(ssc.h, 1)
                p.op("scalar", fsi)
                ssc.val += 1
                p.wait("vector", ssc, ssc.val)
                p.wait("vector", smm, smm.val)

                def fhm(v, vb=vb, hc=hc, kind=kind):
                    if kind == "e":
                        v.tensor_tensor(HE[:, hc * TP:(hc + 1) * TP], MTMP[:], vb,
                                        A.mult)
                        v.tensor_tensor(HE[:, hc * TP:(hc + 1) * TP],
                                        HE[:, hc * TP:(hc + 1) * TP],
                                        CBX[:, 0:TP], A.mult).then_inc(sv.h, 1)
                    else:
                        v.tensor_tensor(HE[:, hc * TP:(hc + 1) * TP], MTMP[:], vb,
                                        A.mult).then_inc(sv.h, 1)
                p.op("vector", fhm)
                sv.val += 1
                evac_vals[hc] = sv.val
            in_done[ji] = smm.val
            h_ready = sv.val
            p.wait("tensor", sv, h_ready)

            if kind == "e":
                k = eidx[ji]
                if ji - 1 in out_done:
                    p.wait("gpsimd", smm, out_done[ji - 1])
                dma("gpsimd", wt2, WOUTb, ewout_d[e])
                wout_ready[ji] = wt2.val
                if k + 2 < len(ejobs):
                    p.wait("gpsimd", smm, in_done[ji])
                    e2 = JOBS[ejobs[k + 2]][2]
                    dma("gpsimd", wt2, WINb[(k + 2) % 2], ewin_d[e2])
                    dma("gpsimd", wt2, VINb[(k + 2) % 2], evin_d[e2])
                    winv_ready[ejobs[k + 2]] = wt2.val

            if kind == "e":
                p.wait("tensor", wt2, wout_ready[ji])
            facc_vals = {}
            OBS = [PBIG0[:, 0:1024], PBIG0[:, 1024:2048], PBIG1[:, 0:1024]]
            for dc in range(8):
                ob = OBS[dc % 3]
                if dc >= 3:
                    p.wait("tensor", sv, facc_vals[dc - 3])

                def fop(t, dc=dc, ob=ob, hc0=hc0, kind=kind):
                    for hc in range(4):
                        hcw = hc0 + hc
                        if kind == "sh":
                            lhs = WSOUT[:, hcw * D + dc * 128:
                                        hcw * D + dc * 128 + 128]
                        else:
                            lhs = WOUTb[:, hc * D + dc * 128:
                                        hc * D + dc * 128 + 128]
                        for j in range(2):
                            mmi = t.matmul(ob[:, j * 512:(j + 1) * 512], lhs,
                                           HE[:, hc * TP + j * 512:
                                               hc * TP + j * 512 + 512],
                                           start=(hc == 0), stop=(hc == 3))
                            if hc == 3 and j == 1:
                                mmi.then_inc(smm.h, 1)
                p.op("tensor", fop)
                smm.val += 1
                p.wait("vector", smm, smm.val)

                def facc(v, dc=dc, ob=ob, first=first):
                    dst = ACCH[:, dc * TP:(dc + 1) * TP]
                    if first:
                        v.tensor_copy(dst, ob).then_inc(sv.h, 1)
                    else:
                        v.tensor_tensor(dst, dst, ob, A.add).then_inc(sv.h, 1)
                p.op("vector", facc)
                sv.val += 1
                facc_vals[dc] = sv.val
            out_done[ji] = smm.val
            last_facc[ji] = sv.val

            if kind == "e" and e == NE - 1:
                moe_done_v = sv.val
                if pa == 0:
                    p.wait("sync", ld, x2_spill)
                for itl, fdummy in [(0, True)] + [(i, False) for i in range(8)]:
                    itg = pa * 8 + itl
                    pi = itl % 2
                    p.wait("sync", sv, sv.val)
                    dma("sync", ld, X2R[:], scr2[:, itg * D:(itg + 1) * D])
                    x2r_done = ld.val
                    p.wait("tensor", sv, sv.val)
                    tbf = PBIG0[:, pi * 1024:(pi + 1) * 1024]

                    def ftf(t, itl=itl, tbf=tbf):
                        for c in range(8):
                            mmi = t.transpose(tbf[:, c * 128:(c + 1) * 128],
                                              ACCH[:, c * TP + itl * 128:
                                                   c * TP + itl * 128 + 128],
                                              IDF[:])
                            if c == 7:
                                mmi.then_inc(smm.h, 1)
                    p.op("tensor", ftf)
                    smm.val += 1
                    p.wait("vector", smm, smm.val)
                    ln_stats(tbf, D, 1, TMPA[:])
                    p.wait("vector", ld, x2r_done)
                    if out_stores > 0:
                        p.wait("vector", sst, out_stores)

                    def ffin(v, tbf=tbf):
                        v.reciprocal(STT[:, 15:16], SQS[:, 2:3])
                        v.drain()
                        v.tensor_scalar(TMPA[:], tbf, STT[:, 12:13], STT[:, 15:16],
                                        A.subtract, A.mult)
                        v.tensor_tensor(OUTT[:], X2R[:], TMPA[:], A.add
                                        ).then_inc(sv.h, 1)
                    p.op("vector", ffin)
                    sv.val += 1
                    if not fdummy:
                        p.wait("sync", sv, sv.val)
                        dma("sync", sst, out_d[:, itg * D:(itg + 1) * D], OUTT[:])
                        out_stores = sst.val
                final_last_v = sv.val

        tap("acc0", ACCH[:], sv, sv.val)

        p.wait("gpsimd", sst, sst.val)
        return _close(block, p, nc)


def _finish(p, sst, out_d, ACCH):
    p.wait("gpsimd", sst, sst.val)


def _close(block, p, nc):
    if True:
        @block.sync
        def _(e):
            for fn in p.q["sync"]:
                fn(e)

        @block.gpsimd
        def _(e):
            for fn in p.q["gpsimd"]:
                fn(e)

        @block.tensor
        def _(e):
            for fn in p.q["tensor"]:
                fn(e)

        @block.vector
        def _(e):
            for fn in p.q["vector"]:
                fn(e)

        @block.scalar
        def _(e):
            for fn in p.q["scalar"]:
                fn(e)

    return nc


def _layouts(inputs):
    f = np.ascontiguousarray
    r = {}
    r["dw"] = f(inputs["L0_attn_down_w"].reshape(8, 128, dA).transpose(1, 0, 2)
                .reshape(128, 8 * dA))
    r["db"] = f(inputs["L0_attn_down_b"].reshape(2, 128).T)
    for nm, kk in [("wq", "L0_attn_wq"), ("wk", "L0_attn_wk"),
                   ("wv", "L0_attn_wv"), ("wo", "L0_attn_wo")]:
        r[nm] = f(inputs[kk].reshape(2, 128, dA).transpose(1, 0, 2).reshape(128, 2 * dA))
    r["uw"] = f(inputs["L0_attn_up_w"].reshape(2, 128, D).transpose(1, 0, 2)
                .reshape(128, 2 * D))
    r["upb"] = f(inputs["L0_attn_up_b"].reshape(1, D))
    r["w1"] = f(np.concatenate([inputs["L0_attn_W1"][0], inputs["L0_attn_W1"][1]], 1))
    r["b1"] = f(inputs["L0_attn_b1"].reshape(1, 2 * FH))
    r["glw"] = f(inputs["L0_attn_ttt_ln_w"].reshape(1, 2 * FH))
    r["glb"] = f(inputs["L0_attn_ttt_ln_b"].reshape(1, 2 * FH))
    r["lrw"] = f(inputs["L0_attn_lr_w"].reshape(2, 2, 128).transpose(2, 1, 0)
                 .reshape(128, 4))
    r["lrb"] = f(inputs["L0_attn_lr_b"].reshape(2, 1))
    r["tki"] = f(inputs["L0_attn_tok_idx"].reshape(1, K))
    r["pw"] = f(inputs["L0_attn_post_w"].reshape(1, dA))
    r["pb"] = f(inputs["L0_attn_post_b"].reshape(1, dA))
    r["lw"] = f(inputs["L0_attn_ln_w"].reshape(1, dA))
    r["lb"] = f(inputs["L0_attn_ln_b"].reshape(1, dA))
    r["gwt"] = f(inputs["L0_ffn_gate_w"].T.reshape(8, 128, NE).transpose(1, 0, 2)
                 .reshape(128, 8 * NE))
    r["wsin"] = f(inputs["L0_ffn_ws_in"].reshape(8, 128, SHID).transpose(1, 0, 2)
                  .reshape(128, 8 * SHID))
    r["vsin"] = f(inputs["L0_ffn_vs_in"].reshape(8, 128, SHID).transpose(1, 0, 2)
                  .reshape(128, 8 * SHID))
    r["wsout"] = f(inputs["L0_ffn_ws_out"].reshape(8, 128, D).transpose(1, 0, 2)
                   .reshape(128, 8 * D))
    r["ewin"] = f(inputs["L0_ffn_e_win"].reshape(NE, 8, 128, EH)
                  .transpose(0, 2, 1, 3).reshape(NE, 128, 8 * EH))
    r["evin"] = f(inputs["L0_ffn_e_vin"].reshape(NE, 8, 128, EH)
                  .transpose(0, 2, 1, 3).reshape(NE, 128, 8 * EH))
    r["ewout"] = f(inputs["L0_ffn_e_wout"].reshape(NE, 4, 128, D)
                   .transpose(0, 2, 1, 3).reshape(NE, 128, 4 * D))
    cstv = np.zeros((128, 8), np.float32)
    cstv[:, 0] = 1e-6
    cstv[:, 1] = 1e-5
    cstv[:, 2] = 1.0 / 128.0
    cstv[:, 3] = 1.0
    cstv[:, 5] = -1.0
    r["cst"] = cstv
    r["idf"] = np.eye(128, dtype=np.float32)
    r["ones16"] = np.ones((1, 16), np.float32)
    r["onesc"] = np.ones((1, 128), np.float32)
    r["negu"] = f(-np.triu(np.ones((16, 16), np.float32)) / 128.0)
    r["rk"] = (1.0 / np.arange(1, K + 1, dtype=np.float32)).reshape(1, K)
    return r


def kernel(**inputs):
    inputs = {k: np.asarray(v) for k, v in inputs.items()}
    shared = _layouts(inputs)
    x = inputs["L0_attn_x"]
    nc = build_program()
    in_maps = []
    for b in range(B):
        m = dict(shared)
        m["xin"] = np.ascontiguousarray(
            x[b].reshape(NT, 128, D).transpose(1, 0, 2).reshape(128, NT * D))
        in_maps.append(m)
    res = run_bass_kernel_spmd(nc, in_maps, core_ids=list(range(B)))
    outs = []
    for b in range(B):
        o = np.asarray(res.results[b]["out"])
        outs.append(o.reshape(128, NT, D).transpose(1, 0, 2).reshape(T, D))
    return np.stack(outs).astype(np.float32)


# version 04
# speedup vs baseline: 1.1052x; 1.0137x over previous
import numpy as np
import concourse.bass as bass
import concourse.mybir as mybir
from concourse.bass_utils import run_bass_kernel_spmd

F32 = mybir.dt.float32
BF16 = mybir.dt.bfloat16
A = mybir.AluOpType
AF = mybir.ActivationFunctionType
AX = mybir.AxisListType

B, T, D = 8, 2048, 1024
dA = 256
H, FH, K = 2, 128, 16
NSTEP = T // K
NT = T // 128
NE, EH, SHID = 16, 512, 1024
TP = T // 2
DEV_TAPS = []
STOP = 99
DEV_LN0_TILES = None
DEV_SCAN_STEPS = None


class Sem:
    def __init__(self, handle):
        self.h = handle
        self.val = 0


class P:
    def __init__(self):
        self.q = {e: [] for e in ("tensor", "vector", "scalar", "gpsimd", "sync")}

    def op(self, eng, fn):
        self.q[eng].append(fn)

    def wait(self, eng, sem, val):
        if val > 0:
            self.q[eng].append(lambda e, s=sem.h, v=val: e.wait_ge(s, v))


def build_program():
    nc = bass.Bass("TRN2", target_bir_lowering=False)

    def din(name, shape, dtype=F32):
        return nc.dram_tensor(name, list(shape), dtype, kind="ExternalInput").ap()

    xin = din("xin", (128, NT * D))
    dw_d = din("dw", (128, 8 * dA)); db_d = din("db", (128, 2))
    wq_d = din("wq", (128, 2 * dA)); wk_d = din("wk", (128, 2 * dA))
    wv_d = din("wv", (128, 2 * dA)); wo_d = din("wo", (128, 2 * dA))
    uw_d = din("uw", (128, 2 * D)); upb_d = din("upb", (1, D))
    w1_d = din("w1", (128, 2 * FH)); b1_d = din("b1", (1, 2 * FH))
    glw_d = din("glw", (1, 2 * FH)); glb_d = din("glb", (1, 2 * FH))
    lrw_d = din("lrw", (128, 4)); lrb_d = din("lrb", (2, 1))
    tki_d = din("tki", (1, K))
    pw_d = din("pw", (1, dA)); pb_d = din("pb", (1, dA))
    lw_d = din("lw", (1, dA)); lb_d = din("lb", (1, dA))
    gwt_d = din("gwt", (128, 8 * NE))
    wsin_d = din("wsin", (128, 8 * SHID)); vsin_d = din("vsin", (128, 8 * SHID))
    wsout_d = din("wsout", (128, 8 * D))
    ewin_d = din("ewin", (NE, 128, 8 * EH))
    evin_d = din("evin", (NE, 128, 8 * EH))
    ewout_d = din("ewout", (NE, 128, 4 * D))
    cst_d = din("cst", (128, 8))
    idf_d = din("idf", (128, 128))
    ones16_d = din("ones16", (1, 16))
    onesc_d = din("onesc", (1, 128))
    negu_d = din("negu", (16, 16))
    rk_d = din("rk", (1, K))

    out_d = nc.dram_tensor("out", [128, NT * D], F32, kind="ExternalOutput").ap()
    scr = nc.dram_tensor("scr", [2 * T], F32, kind="Internal").ap()
    scr2 = nc.dram_tensor("scr2", [128, NT * D], BF16, kind="Internal").ap()

    taps = {}
    tap_specs = {
        "lnt": [128, 8 * T], "ht": [128, 2 * T], "xqt": [128, 2 * T],
        "xkt": [128, 2 * T], "siglr": [2, T], "xkle": [128, NT * dA],
        "tgt2": [128, NT * dA], "zbar": [128, NT * dA], "x2": [128, NT * D],
        "lnt3": [128, 8 * T], "gval": [128, NT * NE], "combt": [16, T],
        "w1f": [128, 2 * FH], "b1f": [1, 2 * FH], "sigt": [128, 2 * K],
        "lrc": [16, 2 * NSTEP], "acc0": [128, 8 * TP],
        "he": [128, D], "stt": [128, 20], "xktmp": [128, dA],
        "mv": [16, 4], "rstd": [16, 2], "grb": [16, 2 * FH], "xhs": [16, 2 * FH],
        "gys": [16, 2 * FH], "ctb": [16, 2 * K], "m1n": [16, 2], "m2": [16, 2], "std": [16, 2], "sqs": [128, 4],
        "tme": [128, dA], "tmpa": [128, D], "tmpd": [128, D], "ptt": [128, dA],
    }
    for name in DEV_TAPS:
        dt_ = BF16 if name in ("combt",) else F32
        taps[name] = nc.dram_tensor("tap_" + name, tap_specs[name], dt_,
                                    kind="ExternalOutput").ap()

    p = P()

    from contextlib import ExitStack
    with ExitStack() as _es:
        XRES = _es.enter_context(nc.sbuf_tensor("XRES", [128, NT * D], BF16))
        LNT = _es.enter_context(nc.sbuf_tensor("LNT", [128, 8 * T], BF16))
        BIGP = _es.enter_context(nc.sbuf_tensor("BIGP", [128, 4 * T], BF16))
        BIGQ = _es.enter_context(nc.sbuf_tensor("BIGQ", [128, 4 * T], BF16))
        TGT2F = _es.enter_context(nc.sbuf_tensor("TGT2F", [128, 2 * NT * dA], BF16))
        HE = _es.enter_context(nc.sbuf_tensor("HE", [128, 4 * TP], BF16))
        WBIGA = _es.enter_context(nc.sbuf_tensor("WBIGA", [128, 8 * dA + 2 * D], BF16))
        WINbT = _es.enter_context(nc.sbuf_tensor("WINb", [128, 2 * 8 * EH], BF16))
        VINbT = _es.enter_context(nc.sbuf_tensor("VINb", [128, 2 * 8 * EH], BF16))
        CBX = _es.enter_context(nc.sbuf_tensor("CBX", [128, TP], BF16))
        TGT16R = _es.enter_context(nc.sbuf_tensor("TGT16R", [16, 4 * dA], BF16))
        XKLE16R = _es.enter_context(nc.sbuf_tensor("XKLE16R", [16, 4 * dA], BF16))
        ZB16 = _es.enter_context(nc.sbuf_tensor("ZB16", [16, 2 * dA], BF16))
        COMBROW = _es.enter_context(nc.sbuf_tensor("COMBROW", [1, 2 * TP], BF16))
        MTMP = _es.enter_context(nc.sbuf_tensor("MTMP", [128, TP], BF16))
        COMBT = _es.enter_context(nc.sbuf_tensor("COMBT", [16, T], BF16))
        DB = _es.enter_context(nc.sbuf_tensor("DB", [128, 2], F32))
        WQ = _es.enter_context(nc.sbuf_tensor("WQ", [128, 2 * dA], BF16))
        WK = _es.enter_context(nc.sbuf_tensor("WK", [128, 2 * dA], BF16))
        WV = _es.enter_context(nc.sbuf_tensor("WV", [128, 2 * dA], BF16))
        WO = _es.enter_context(nc.sbuf_tensor("WO", [128, 2 * dA], BF16))
        UPB = _es.enter_context(nc.sbuf_tensor("UPB", [1, D], BF16))
        W1f = _es.enter_context(nc.sbuf_tensor("W1f", [128, 2 * FH], F32))
        W1b = _es.enter_context(nc.sbuf_tensor("W1b", [128, 2 * FH], BF16))
        B1f = _es.enter_context(nc.sbuf_tensor("B1f", [1, 2 * FH], F32))
        LRWT = _es.enter_context(nc.sbuf_tensor("LRWT", [128, 4], BF16))
        LRB = _es.enter_context(nc.sbuf_tensor("LRB", [2, 1], F32))
        GWT = _es.enter_context(nc.sbuf_tensor("GWT", [128, 8 * NE], F32))
        CST = _es.enter_context(nc.sbuf_tensor("CST", [128, 8], F32))
        IDF = _es.enter_context(nc.sbuf_tensor("IDF", [128, 128], F32))
        IDB = _es.enter_context(nc.sbuf_tensor("IDB", [128, 128], BF16))
        ONES16f = _es.enter_context(nc.sbuf_tensor("ONES16f", [1, 16], F32))
        ONESCb = _es.enter_context(nc.sbuf_tensor("ONESCb", [1, 128], BF16))
        NEGU = _es.enter_context(nc.sbuf_tensor("NEGU", [16, 16], F32))
        RK = _es.enter_context(nc.sbuf_tensor("RK", [1, K], F32))
        TI = _es.enter_context(nc.sbuf_tensor("TI", [1, K], F32))
        TI15b = _es.enter_context(nc.sbuf_tensor("TI15b", [128, 1], F32))
        GB = _es.enter_context(nc.sbuf_tensor("GB", [128, 2 * FH], F32))
        BB = _es.enter_context(nc.sbuf_tensor("BB", [128, 2 * FH], F32))
        G2B = _es.enter_context(nc.sbuf_tensor("G2B", [128, 2 * FH], F32))
        PWB = _es.enter_context(nc.sbuf_tensor("PWB", [128, dA], F32))
        PBB = _es.enter_context(nc.sbuf_tensor("PBB", [128, dA], F32))
        LWB = _es.enter_context(nc.sbuf_tensor("LWB", [128, dA], F32))
        LBB = _es.enter_context(nc.sbuf_tensor("LBB", [128, dA], F32))
        SIGT = _es.enter_context(nc.sbuf_tensor("SIGT", [128, 2 * NT], F32))
        LETOK = _es.enter_context(nc.sbuf_tensor("LETOK", [128, 2 * NT], F32))
        LECB = _es.enter_context(nc.sbuf_tensor("LECB", [16, 2 * NSTEP], BF16))
        TOKLN = _es.enter_context(nc.sbuf_tensor("TOKLN", [16, 16], F32))
        BN6 = _es.enter_context(nc.sbuf_tensor("BN6", [16, 12], F32))
        MV = _es.enter_context(nc.sbuf_tensor("MV", [16, 4], F32))
        STD = _es.enter_context(nc.sbuf_tensor("STD", [16, 2], F32))
        RSTD = _es.enter_context(nc.sbuf_tensor("RSTD", [16, 2], F32))
        M1N = _es.enter_context(nc.sbuf_tensor("M1N", [16, 2], F32))
        M2 = _es.enter_context(nc.sbuf_tensor("M2", [16, 2], F32))
        CT1 = _es.enter_context(nc.sbuf_tensor("CT1", [16, 2 * K], F32))
        CTB = _es.enter_context(nc.sbuf_tensor("CTB", [16, 2 * K], BF16))
        TMPA = _es.enter_context(nc.sbuf_tensor("TMPA", [128, D], F32))
        TMPD = _es.enter_context(nc.sbuf_tensor("TMPD", [128, D], F32))
        TME = _es.enter_context(nc.sbuf_tensor("TME", [128, dA], F32))
        TME2 = _es.enter_context(nc.sbuf_tensor("TME2", [128, dA], F32))
        XKtmp = _es.enter_context(nc.sbuf_tensor("XKtmp", [128, dA], F32))
        TMB = _es.enter_context(nc.sbuf_tensor("TMB", [128, dA], BF16))
        PTT = _es.enter_context(nc.sbuf_tensor("PTT", [128, dA], BF16))
        STT = _es.enter_context(nc.sbuf_tensor("STT", [128, 20], F32))
        SQS = _es.enter_context(nc.sbuf_tensor("SQS", [128, 4], F32))
        PBIG0 = _es.enter_context(nc.psum_tensor("PBIG0", [128, 2048], F32))
        PBIG1 = _es.enter_context(nc.psum_tensor("PBIG1", [128, 1024], F32))
        PTRA = _es.enter_context(nc.psum_tensor("PTRA", [128, 1024], BF16))
        PTRB = _es.enter_context(nc.psum_tensor("PTRB", [128, 1024], BF16))
        _ld = _es.enter_context(nc.semaphore("s_ld"))
        _wt = _es.enter_context(nc.semaphore("s_wt"))
        _mm = _es.enter_context(nc.semaphore("s_mm"))
        _v = _es.enter_context(nc.semaphore("s_v"))
        _sc = _es.enter_context(nc.semaphore("s_sc"))
        _g = _es.enter_context(nc.semaphore("s_g"))
        _st = _es.enter_context(nc.semaphore("s_st"))
        block = _es.enter_context(nc.Block())

        ld, wt2, smm, sv, ssc, sg, sst = (Sem(_ld), Sem(_wt), Sem(_mm), Sem(_v),
                                          Sem(_sc), Sem(_g), Sem(_st))

        OUTT = TMPD
        SIGLR = WINbT[:].bitcast(F32)[0:2, 0:T]
        ROWS = VINbT[:].bitcast(F32)[0:1, 0:6 * dA]
        LRC = WINbT[:].bitcast(F32)[0:16, 2048:2048 + 2 * NSTEP]
        GVAL = VINbT[:].bitcast(F32)[:, 1536:1536 + NT * NE]
        SCR_SC = MTMP[:].bitcast(F32)[0:16, 0:128]
        X2R = MTMP
        TI15 = TI15b[0:16, :]
        XHs = TMPD[0:16, 0:256]
        GYs = TMPD[0:16, 256:512]
        PRD = TMPD[0:16, 512:768]
        GRF = TMPD[0:16, 768:1024]
        GRB = TMPA[:].bitcast(BF16)[0:16, 0:256]
        Z1S = TMPA[0:16, 256:512]
        GWRK = TME2[:, 0:16]
        GEQ = TME2[:, 16:32]
        COMB = TME2[:, 32:48]
        GMAX = TME2[:, 48:49]
        RS = TME2[:, 49:50]
        WINb = [WINbT[:, 0:4096], WINbT[:, 4096:8192]]
        VINb = [VINbT[:, 0:4096], VINbT[:, 4096:8192]]
        XQT = BIGP[:, 0:2 * T]
        XKT = BIGP[:, 2 * T:4 * T]
        HT = BIGQ[:, 0:2 * T]
        XKLE = BIGQ[:, 2 * T:2 * T + NT * dA]
        WSIN = BIGP[:]
        VSIN = BIGQ[:]
        TGT2 = TGT2F[:, 0:NT * dA]
        WSOUT = TGT2F[:]
        DW = WBIGA[:, 0:8 * dA]
        UW = WBIGA[:, 8 * dA:8 * dA + 2 * D]
        WOUTb = WBIGA[:]
        ACCH = XRES[:].bitcast(F32)

        def bank(i):
            assert i < 6
            if i < 4:
                return PBIG0[:, i * 512:(i + 1) * 512]
            return PBIG1[:, (i - 4) * 512:(i - 4 + 1) * 512]

        def tap(name, src, sem=None, val=0):
            if name in taps:
                if sem is not None:
                    p.wait("gpsimd", sem, val)
                dst = taps[name]
                def fn(g, dst=dst, s=src):
                    g.dma_start(out=dst, in_=s).then_inc(sst.h, 16)
                p.op("gpsimd", fn)
                sst.val += 16

        def dma(eng, sem, dst, src, **kw):
            def fn(e, dst=dst, src=src, sem=sem, kw=kw):
                e.dma_start(out=dst, in_=src, **kw).then_inc(sem.h, 16)
            p.op(eng, fn)
            sem.val += 16

        for dst, src in [(CST[:], cst_d[:]), (IDF[:], idf_d[:]),
                         (ONES16f[:], ones16_d[:]), (NEGU[:], negu_d[:]),
                         (RK[:], rk_d[:]), (W1f[:], w1_d[:]), (B1f[:], b1_d[:]),
                         (TI[:], tki_d[:]), (DB[:], db_d[:]), (GWT[:], gwt_d[:])]:
            dma("sync", ld, dst, src)
        dma("sync", ld, LRB[:], lrb_d[:], allow_slow_non_contiguous=True)
        for i, src in enumerate([glw_d, glb_d, pw_d, pb_d, lw_d, lb_d]):
            dma("sync", ld, ROWS[:, i * dA:(i + 1) * dA][:, 0:src.shape[1]], src)
        ld_small = ld.val

        for dst, src in [(XRES[:], xin[:]), (DW, dw_d[:]), (WQ[:], wq_d[:]),
                         (WK[:], wk_d[:]), (WV[:], wv_d[:]), (WO[:], wo_d[:]),
                         (UW, uw_d[:]), (UPB[:], upb_d[:]), (LRWT[:], lrw_d[:]),
                         (IDB[:], idf_d[:]), (ONESCb[:], onesc_d[:]),
                         (W1b[:], w1_d[:])]:
            dma("gpsimd", wt2, dst, src)
        wt_attn = wt2.val

        nc.const_aps.register(F32, 1e-6, CST[:, 0:1])
        nc.const_aps.register(F32, 1e-5, CST[:, 1:2])
        nc.const_aps.register(F32, 1.0 / 128.0, CST[:, 2:3])
        nc.const_aps.register(F32, 1.0, CST[:, 3:4])
        nc.const_aps.register(F32, -1.0, CST[:, 5:6])
        p.wait("tensor", ld, ld_small)
        p.wait("tensor", wt2, wt_attn)
        ONES128f = ONES16f[0:1, 0:1].broadcast_to([1, 128])

        bc_dst = [GB, BB, PWB, PBB, LWB, LBB]
        bc_w = [2 * FH, 2 * FH, dA, dA, dA, dA]

        def f_bc_mm(t):
            for i in range(6):
                t.matmul(bank(i)[:, 0:bc_w[i]], ONES128f,
                         ROWS[0:1, i * dA: i * dA + bc_w[i]], start=True,
                         stop=True).then_inc(smm.h, 1)
        p.op("tensor", f_bc_mm)
        smm.val += 6
        p.wait("vector", smm, smm.val)

        def f_bc_cp(v):
            for i in range(6):
                v.tensor_copy(bc_dst[i][:, 0:bc_w[i]], bank(i)[:, 0:bc_w[i]])
            v.tensor_tensor(G2B[:], GB[:], GB[:], A.mult)
            v.tensor_tensor(TI[:], TI[:], RK[:], A.add)
            v.drain()
            v.tensor_relu(TI[:], TI[:])
            v.drain()
            v.tensor_copy(STT[0:1, 0:1], TI[0:1, 0:1]).then_inc(sv.h, 1)
        p.op("vector", f_bc_cp)
        sv.val += 1
        p.wait("tensor", sv, sv.val)

        def f_tok(t):
            t.matmul(bank(5)[0:16, 0:16], ONES16f[:], TI[:],
                     start=True, stop=True).then_inc(smm.h, 1)
        p.op("tensor", f_tok)
        smm.val += 1
        p.wait("vector", smm, smm.val)

        def f_tok2(v):
            v.tensor_tensor(TOKLN[:], bank(5)[0:16, 0:16], NEGU[:],
                            A.mult).then_inc(sv.h, 1)
        p.op("vector", f_tok2)
        sv.val += 1
        p.wait("tensor", sv, sv.val)

        def f_ti15(t):
            t.matmul(bank(5)[:, 16:17], ONES128f, TI[0:1, 15:16],
                     start=True, stop=True).then_inc(smm.h, 1)
        p.op("tensor", f_ti15)
        smm.val += 1
        p.wait("vector", smm, smm.val)

        def f_ti15c(v):
            v.tensor_copy(TI15b[:], bank(5)[:, 16:17]).then_inc(sv.h, 1)
        p.op("vector", f_ti15c)
        sv.val += 1
        ti15_done = sv.val

        def ln_stats(eng_src, width, eps_col, scrap):
            w = width

            def f1(v, src=eng_src, scrap=scrap, w=w):
                v.tensor_reduce(STT[:, 9:10], src, AX.X, A.add)
                v.tensor_copy(scrap[:, 0:w], src)
                v.tensor_tensor(scrap[:, 0:w], scrap[:, 0:w], scrap[:, 0:w],
                                A.mult)
                v.tensor_reduce(STT[:, 10:11], scrap[:, 0:w], AX.X, A.add)
                v.drain()
                v.tensor_scalar(STT[:, 12:13], STT[:, 9:10], 1.0 / w, None, A.mult)
                v.tensor_scalar(STT[:, 13:14], STT[:, 10:11], 1.0 / w, None, A.mult)
                v.drain()
                v.tensor_tensor(STT[:, 16:17], STT[:, 12:13], STT[:, 12:13], A.mult)
                v.drain()
                v.tensor_tensor(STT[:, 13:14], STT[:, 13:14], STT[:, 16:17],
                                A.subtract)
                v.drain()
                v.tensor_copy(STT[:, 8:9], STT[:, 13:14]).then_inc(sv.h, 1)
            p.op("vector", f1)
            sv.val += 1
            p.wait("scalar", sv, sv.val)

            def f2(sc, eps_col=eps_col):
                sc.activation(SQS[:, 2:3], STT[:, 13:14], AF.Sqrt,
                              bias=CST[:, eps_col:eps_col + 1], scale=1.0
                              ).then_inc(ssc.h, 1)
            p.op("scalar", f2)
            ssc.val += 1
            p.wait("vector", ssc, ssc.val)

        p.wait("vector", wt2, wt_attn)
        tr_done = {}
        p.wait("scalar", wt2, wt_attn)
        for it in range(NT if DEV_LN0_TILES is None else DEV_LN0_TILES):
            xsrc = XRES[:, it * D:(it + 1) * D]
            ln_stats(xsrc, D, 1, TMPA[:])
            if it >= 1:
                p.wait("vector", smm, tr_done[it - 1])

            def f3(v, src=xsrc):
                v.reciprocal(STT[:, 15:16], SQS[:, 2:3])
                v.drain()
                v.tensor_scalar(HE[:, 0:D], src, STT[:, 12:13], STT[:, 15:16],
                                A.subtract, A.mult).then_inc(sv.h, 1)
            p.op("vector", f3)
            sv.val += 1
            p.wait("tensor", sv, sv.val)
            if it >= 2:
                p.wait("tensor", ssc, ssc.val - 1)

            def ftr(t, it=it):
                ptr = PTRA if it % 2 == 0 else PTRB
                for c in range(8):
                    mmi = t.transpose(ptr[:, c * 128:(c + 1) * 128],
                                      HE[:, c * 128:(c + 1) * 128], IDB[:])
                    if c == 7:
                        mmi.then_inc(smm.h, 1)
            p.op("tensor", ftr)
            smm.val += 1
            tr_done[it] = smm.val
            p.wait("scalar", smm, smm.val)

            def fcp(s, it=it):
                ptr = PTRA if it % 2 == 0 else PTRB
                for c in range(8):
                    s.copy(LNT[:, c * T + it * 128: c * T + it * 128 + 128],
                           ptr[:, c * 128:(c + 1) * 128])
                s.copy(SQS[0:1, 3:4], CST[0:1, 3:4]).then_inc(ssc.h, 1)
            p.op("scalar", fcp)
            ssc.val += 1
        ln0_done_sc = ssc.val
        tap("lnt", LNT[:], ssc, ln0_done_sc)
        tap("he", HE[:, 0:D], ssc, ln0_done_sc)
        tap("stt", STT[:], ssc, ln0_done_sc)

        if STOP <= 1:
            _finish(p, sst, out_d, ACCH)
            return _close(block, p, nc)
        p.wait("tensor", ssc, ln0_done_sc)
        dp_evac = {}
        for fc in range(2):
            for jt in range(4):
                gi = fc * 4 + jt
                bk = gi % 6
                if gi >= 6:
                    p.wait("tensor", ssc, dp_evac[gi - 6])

                def fdp(t, fc=fc, jt=jt, bk=bk):
                    for dc in range(8):
                        mmi = t.matmul(
                            bank(bk),
                            DW[:, dc * dA + fc * 128: dc * dA + fc * 128 + 128],
                            LNT[:, dc * T + jt * 512: dc * T + jt * 512 + 512],
                            start=(dc == 0), stop=(dc == 7))
                        if dc == 7:
                            mmi.then_inc(smm.h, 1)
                p.op("tensor", fdp)
                smm.val += 1
                p.wait("scalar", smm, smm.val)

                def fsl(s, fc=fc, jt=jt, bk=bk):
                    s.activation(HT[:, fc * T + jt * 512: fc * T + jt * 512 + 512],
                                 bank(bk), AF.Silu, bias=DB[:, fc:fc + 1],
                                 scale=1.0).then_inc(ssc.h, 1)
                p.op("scalar", fsl)
                ssc.val += 1
                dp_evac[gi] = ssc.val
        ht_done = ssc.val
        tap("ht", HT, ssc, ht_done)

        if STOP <= 2:
            _finish(p, sst, out_d, ACCH)
            return _close(block, p, nc)
        p.wait("tensor", ssc, ht_done)
        for j in range(4):
            def flr(t, j=j):
                for dc in range(2):
                    mmi = t.matmul(bank(j)[0:2, 0:512],
                                   LRWT[:, dc * 2: dc * 2 + 2],
                                   HT[:, dc * T + j * 512: dc * T + j * 512 + 512],
                                   start=(dc == 0), stop=(dc == 1))
                    if dc == 1:
                        mmi.then_inc(smm.h, 1)
            p.op("tensor", flr)
            smm.val += 1
            p.wait("scalar", smm, smm.val)

            def fsg(s, j=j):
                s.activation(SIGLR[:, j * 512:(j + 1) * 512], bank(j)[0:2, 0:512],
                             AF.Sigmoid, bias=LRB[:], scale=1.0).then_inc(ssc.h, 1)
            p.op("scalar", fsg)
            ssc.val += 1
        siglr_done = ssc.val
        tap("siglr", SIGLR[:], ssc, siglr_done)

        p.wait("sync", ssc, siglr_done)
        dma("sync", ld, scr[:], SIGLR[:])
        p.wait("sync", ld, ld.val)
        for h in range(H):
            dma("sync", ld, SIGT[:, h * NT:(h + 1) * NT],
                scr[h * T:(h + 1) * T].rearrange("(w q) -> q w", q=128),
                allow_slow_non_contiguous=True)
            dma("sync", ld, LRC[:, h * NSTEP:(h + 1) * NSTEP],
                scr[h * T:(h + 1) * T].rearrange("(n k) -> k n", k=K),
                allow_slow_non_contiguous=True)
        sig_resh = ld.val
        tap("sigt", SIGT[:], ld, sig_resh)
        tap("lrc", LRC[:], ld, sig_resh)

        p.wait("vector", ld, sig_resh)
        p.wait("vector", sv, ti15_done)

        def fle(v):
            v.tensor_scalar(LETOK[:], SIGT[:], TI15b[:], 1.0 / 128.0, A.mult, A.mult)
            v.tensor_scalar(LECB[:], LRC[:], TI15[:], 1.0 / 128.0, A.mult,
                            A.mult).then_inc(sv.h, 1)
        p.op("vector", fle)
        sv.val += 1
        letok_done = sv.val

        if STOP <= 3:
            _finish(p, sst, out_d, ACCH)
            return _close(block, p, nc)
        p.wait("tensor", ssc, siglr_done)
        pj_evac = {}
        for wi_, (wsrc, dstt) in enumerate(((WQ, XQT), (WK, XKT))):
            for fc in range(2):
                for jt in range(4):
                    gi = wi_ * 8 + fc * 4 + jt
                    bk = gi % 6
                    if gi >= 6:
                        p.wait("tensor", sv, pj_evac[gi - 6])

                    def fpj(t, wsrc=wsrc, fc=fc, jt=jt, bk=bk):
                        for dc in range(2):
                            mmi = t.matmul(
                                bank(bk),
                                wsrc[:, dc * dA + fc * 128: dc * dA + fc * 128 + 128],
                                HT[:, dc * T + jt * 512: dc * T + jt * 512 + 512],
                                start=(dc == 0), stop=(dc == 1))
                            if dc == 1:
                                mmi.then_inc(smm.h, 1)
                    p.op("tensor", fpj)
                    smm.val += 1
                    p.wait("vector", smm, smm.val)

                    def fev(v, dstt=dstt, fc=fc, jt=jt, bk=bk):
                        v.tensor_copy(dstt[:, fc * T + jt * 512: fc * T + jt * 512 + 512],
                                      bank(bk)).then_inc(sv.h, 1)
                    p.op("vector", fev)
                    sv.val += 1
                    pj_evac[gi] = sv.val
        proj_fmaj_done = sv.val
        tap("xqt", XQT, sv, proj_fmaj_done)
        tap("xkt", XKT, sv, proj_fmaj_done)

        if STOP <= 4:
            _finish(p, sst, out_d, ACCH)
            return _close(block, p, nc)
        p.wait("vector", sv, letok_done)
        p.wait("tensor", sv, proj_fmaj_done)
        tokmaj_evac = {}
        for it in range(NT):
            bkk = it % 2
            bkv = 2 + it % 2
            if it >= 2:
                p.wait("tensor", sv, tokmaj_evac[it - 2])

            def fkv(t, it=it, bkk=bkk, bkv=bkv):
                for dc in range(2):
                    t.matmul(bank(bkk)[:, 0:256],
                             HT[:, dc * T + it * 128: dc * T + it * 128 + 128],
                             WK[:, dc * dA: dc * dA + dA],
                             start=(dc == 0), stop=(dc == 1))
                for dc in range(2):
                    mmi = t.matmul(bank(bkv)[:, 0:256],
                                   HT[:, dc * T + it * 128: dc * T + it * 128 + 128],
                                   WV[:, dc * dA: dc * dA + dA],
                                   start=(dc == 0), stop=(dc == 1))
                    if dc == 1:
                        mmi.then_inc(smm.h, 1)
            p.op("tensor", fkv)
            smm.val += 1
            p.wait("scalar", smm, smm.val)

            def fxk(s, bkk=bkk):
                s.copy(XKtmp[:], bank(bkk)[:, 0:256]).then_inc(ssc.h, 1)
            p.op("scalar", fxk)
            ssc.val += 1
            p.wait("vector", ssc, ssc.val)

            def fxv(v, it=it, bkv=bkv):
                for h in range(H):
                    v.tensor_scalar(XKLE[:, it * dA + h * FH: it * dA + h * FH + FH],
                                    XKtmp[:, h * FH:(h + 1) * FH],
                                    LETOK[:, h * NT + it: h * NT + it + 1], None,
                                    A.mult)
                v.tensor_tensor(TME[:], XKtmp[:], bank(bkv)[:, 0:256], A.subtract)
                v.tensor_tensor(TME[:], TME[:], BB[:], A.add)
                v.tensor_tensor(TGT2[:, it * dA:(it + 1) * dA], TME[:], GB[:],
                                A.mult).then_inc(sv.h, 1)
            p.op("vector", fxv)
            sv.val += 1
            tokmaj_evac[it] = sv.val
        tokmaj_done = sv.val
        tap("xkle", XKLE, sv, tokmaj_done)
        tap("xktmp", XKtmp[:], sv, tokmaj_done)
        tap("tgt2", TGT2[:], sv, tokmaj_done)

        if STOP <= 5:
            _finish(p, sst, out_d, ACCH)
            return _close(block, p, nc)
        p.wait("tensor", sv, tokmaj_done)
        p.wait("tensor", sv, proj_fmaj_done)

        zbar_done = {}
        cast_done = {}
        grad_ready = {}
        upd_done = {}
        mm_all = {}

        tgt_pf = {}
        xkle_pf = {}
        stash_ld = {}
        NS_ = NSTEP if DEV_SCAN_STEPS is None else DEV_SCAN_STEPS
        for n in range(NS_):
            q = n % 2
            bA, bB, bC = bank(q), bank(2 + q), bank(4 + q)
            itile, r0 = n // 8, (n % 8) * 16
            slot = n % 4

            p.wait("sync", sv, tokmaj_evac[n // 8])
            if n >= 4:
                p.wait("sync", sv, grad_ready[n - 4])
                p.wait("sync", smm, mm_all[n - 4])
            dma("sync", ld, TGT16R[:, slot * dA:(slot + 1) * dA],
                TGT2[r0:r0 + 16, itile * dA:(itile + 1) * dA])
            tgt_pf[n] = ld.val
            dma("sync", ld, XKLE16R[:, slot * dA:(slot + 1) * dA],
                XKLE[r0:r0 + 16, itile * dA:(itile + 1) * dA])
            xkle_pf[n] = ld.val

            if n >= 1:
                p.wait("tensor", ssc, cast_done[n - 1])
            if n >= 2:
                p.wait("tensor", sv, upd_done[n - 2])
                p.wait("tensor", ssc, zbar_done[n - 2])
            p.wait("tensor", ld, xkle_pf[n])

            def f_z1(t, n=n, bA=bA):
                for h in range(H):
                    t.matmul(bA[0:16, h * FH:(h + 1) * FH],
                             XKT[:, h * T + n * K: h * T + n * K + K],
                             W1b[:, h * FH:(h + 1) * FH], start=True, stop=False)
                    t.matmul(bA[0:16, h * FH:(h + 1) * FH], ONES16f[:],
                             B1f[0:1, h * FH:(h + 1) * FH], start=False, stop=True)
                for h in range(H):
                    t.matmul(bA[32:48, h * K:(h + 1) * K],
                             XKT[:, h * T + n * K: h * T + n * K + K],
                             XQT[:, h * T + n * K: h * T + n * K + K],
                             start=True, stop=False)
                    mmi = t.matmul(bA[32:48, h * K:(h + 1) * K],
                                   ONESCb[0:1, 0:16], ONESCb[0:1, 0:16],
                                   start=False, stop=True)
                    if h == 1:
                        mmi.then_inc(smm.h, 1)
            p.op("tensor", f_z1)
            smm.val += 1

            mmz = smm.val
            p.wait("vector", smm, mmz)
            p.wait("scalar", smm, mmz)

            def f_sums(s, bA=bA):
                for h in range(H):
                    s.activation(SCR_SC, bA[0:16, h * FH:(h + 1) * FH], AF.Identity,
                                 bias=0.0, scale=1.0, accum_out=SQS[0:16, h:h + 1])
                for h in range(H):
                    s.activation(SCR_SC, bA[0:16, h * FH:(h + 1) * FH], AF.Square,
                                 bias=0.0, scale=1.0,
                                 accum_out=SQS[0:16, 2 + h:3 + h])
                s.drain()
                s.mul(BN6[0:16, 0:2], SQS[0:16, 0:2], 1.0 / 128.0)
                s.activation(BN6[0:16, 2:4], SQS[0:16, 0:2], AF.Square,
                             bias=0.0, scale=1.0 / 128.0)
                s.drain()
                s.mul(BN6[0:16, 4:6], BN6[0:16, 2:4], -1.0)
                s.drain()
                s.add(BN6[0:16, 6:8], BN6[0:16, 4:6], 1e-6)
                s.drain()
                for h in range(H):
                    s.activation(SQS[0:16, 2 + h:3 + h], SQS[0:16, 2 + h:3 + h],
                                 AF.Sqrt, bias=BN6[0:16, 6 + h:7 + h],
                                 scale=1.0 / 128.0)
                s.drain()
                s.copy(BN6[0:1, 8:9], CST[0:1, 3:4]).then_inc(ssc.h, 1)
            p.op("scalar", f_sums)
            ssc.val += 1
            p.wait("vector", ssc, ssc.val)
            p.wait("vector", ld, tgt_pf[n])

            def f_grad(v, n=n, bA=bA, slot=slot):
                v.reciprocal(RSTD[:], SQS[0:16, 2:4])
                v.tensor_tensor(CT1[:, 0:K], bA[32:48, 0:K], TOKLN[:], A.mult)
                v.tensor_tensor(CT1[:, K:2 * K], bA[32:48, K:2 * K], TOKLN[:],
                                A.mult)
                v.drain()
                for h in range(H):
                    v.tensor_scalar(XHs[:, h * FH:(h + 1) * FH],
                                    bA[0:16, h * FH:(h + 1) * FH],
                                    BN6[0:16, h: h + 1], RSTD[:, h: h + 1],
                                    A.subtract, A.mult)
                for h in range(H):
                    v.tensor_scalar(CTB[:, h * K:(h + 1) * K],
                                    CT1[:, h * K:(h + 1) * K],
                                    LRC[:, h * NSTEP + n: h * NSTEP + n + 1], None,
                                    A.mult)
                v.tensor_tensor(GYs[:].bitcast(BF16)[0:16, 0:256], XHs[:],
                                G2B[0:16, :], A.mult)
                v.tensor_tensor(GYs[:].bitcast(BF16)[0:16, 0:256],
                                GYs[:].bitcast(BF16)[0:16, 0:256],
                                TGT16R[:, slot * dA:(slot + 1) * dA], A.add)
                v.tensor_tensor(PRD[:].bitcast(BF16)[0:16, 0:256],
                                GYs[:].bitcast(BF16)[0:16, 0:256], XHs[:], A.mult)
                for h in range(H):
                    v.tensor_reduce(M1N[:, h:h + 1],
                                    PRD[:].bitcast(BF16)[0:16, h * FH:(h + 1) * FH],
                                    AX.X, A.add, negate=True)
                v.drain()
                v.tensor_scalar(M1N[:], M1N[:], 1.0 / 128.0, None, A.mult)
                for h in range(H):
                    v.tensor_scalar(PRD[:, h * FH:(h + 1) * FH],
                                    GYs[:].bitcast(BF16)[0:16, h * FH:(h + 1) * FH],
                                    1.0 / 128.0, 0.0, A.mult, A.add,
                                    accum_out=M2[:, h:h + 1])
                v.drain()
                for h in range(H):
                    v.scalar_tensor_tensor(GRF[:, h * FH:(h + 1) * FH],
                                           XHs[:, h * FH:(h + 1) * FH],
                                           M1N[:, h:h + 1],
                                           GYs[:].bitcast(BF16)[0:16, h * FH:(h + 1) * FH],
                                           A.mult, A.add)
                for h in range(H):
                    mmo = v.tensor_scalar(GRB[:, h * FH:(h + 1) * FH],
                                          GRF[:, h * FH:(h + 1) * FH],
                                          M2[:, h:h + 1], RSTD[:, h:h + 1],
                                          A.subtract, A.mult)
                    if h == 1:
                        mmo.then_inc(sv.h, 1)
            p.op("vector", f_grad)
            sv.val += 1
            grad_ready[n] = sv.val

            p.wait("tensor", sv, grad_ready[n])

            def f_zb(t, n=n, bA=bA, bB=bB, bC=bC, slot=slot):
                for h in range(H):
                    t.matmul(bB[0:16, h * FH:(h + 1) * FH],
                             XQT[:, h * T + n * K: h * T + n * K + K],
                             W1b[:, h * FH:(h + 1) * FH], start=True, stop=False)
                    t.matmul(bB[0:16, h * FH:(h + 1) * FH], ONES16f[:],
                             B1f[0:1, h * FH:(h + 1) * FH], start=False, stop=False)
                    t.matmul(bB[0:16, h * FH:(h + 1) * FH],
                             CTB[:, h * K:(h + 1) * K],
                             GRB[:, h * FH:(h + 1) * FH], start=False, stop=True)
                for h in range(H):
                    t.matmul(bC[0:128, h * FH:(h + 1) * FH],
                             XKLE16R[:, slot * dA + h * FH: slot * dA + (h + 1) * FH],
                             GRB[:, h * FH:(h + 1) * FH], start=True, stop=True)
                for h in range(H):
                    mmi = t.matmul(bA[64:65, 256 + h * FH: 256 + (h + 1) * FH],
                                   LECB[:, h * NSTEP + n: h * NSTEP + n + 1],
                                   GRB[:, h * FH:(h + 1) * FH], start=True, stop=True)
                    if h == 1:
                        mmi.then_inc(smm.h, 1)
            p.op("tensor", f_zb)
            smm.val += 1
            mm_all[n] = smm.val

            p.wait("scalar", smm, mm_all[n])
            if n >= 2:
                p.wait("scalar", ld, stash_ld[n - 2])

            def f_zc(s, bB=bB, n=n):
                s.copy(ZB16[:, (n % 2) * dA:(n % 2 + 1) * dA],
                       bB[0:16, 0:256]).then_inc(ssc.h, 1)
            p.op("scalar", f_zc)
            ssc.val += 1
            zbar_done[n] = ssc.val
            p.wait("sync", ssc, zbar_done[n])
            dma("sync", ld, TGT2[r0:r0 + 16, itile * dA:(itile + 1) * dA],
                ZB16[:, (n % 2) * dA:(n % 2 + 1) * dA])
            stash_ld[n] = ld.val

            p.wait("vector", smm, mm_all[n])

            def f_upd(v, bA=bA, bC=bC):
                v.tensor_tensor(W1f[:], W1f[:], bC[0:128, 0:256], A.subtract)
                v.tensor_tensor(B1f[:], B1f[:], bA[64:65, 256:512],
                                A.subtract).then_inc(sv.h, 1)
            p.op("vector", f_upd)
            sv.val += 1
            upd_done[n] = sv.val

            p.wait("scalar", sv, upd_done[n])

            def f_cast(s):
                s.copy(W1b[:], W1f[:]).then_inc(ssc.h, 1)
            p.op("scalar", f_cast)
            ssc.val += 1
            cast_done[n] = ssc.val

        scan_v, scan_sc = sv.val, ssc.val
        p.wait("gpsimd", ld, stash_ld[NS_ - 1])
        tap("zbar", TGT2[:], ssc, scan_sc)
        tap("w1f", W1f[:], ssc, scan_sc)
        tap("mv", MV[:], ssc, scan_sc)
        tap("rstd", RSTD[:], ssc, scan_sc)
        tap("grb", GRB, ssc, scan_sc)
        tap("xhs", XHs, ssc, scan_sc)
        tap("gys", GYs, ssc, scan_sc)
        tap("ctb", CTB[:], ssc, scan_sc)
        tap("m1n", M1N[:], ssc, scan_sc)
        tap("m2", M2[:], ssc, scan_sc)
        tap("std", STD[:], ssc, scan_sc)
        tap("sqs", SQS[:], ssc, scan_sc)
        tap("b1f", B1f[:], ssc, scan_sc)

        if STOP <= 6:
            _finish(p, sst, out_d, ACCH)
            return _close(block, p, nc)
        for eng in ("tensor", "vector", "scalar"):
            p.wait(eng, ssc, ssc.val)
            p.wait(eng, sv, sv.val)
            p.wait(eng, smm, smm.val)
            p.wait(eng, ld, ld.val)
            p.wait(eng, wt2, wt2.val)
        for it, dummy in [(0, True)] + [(i, False) for i in range(NT)]:
            b0, b1_, b3 = 0, 0, 1
            p.wait("tensor", sv, sv.val)

            def fq(t, it=it, b0=b0):
                for dc in range(2):
                    mmi = t.matmul(bank(b0)[:, 0:256],
                                   HT[:, dc * T + it * 128: dc * T + it * 128 + 128],
                                   WQ[:, dc * dA: dc * dA + dA],
                                   start=(dc == 0), stop=(dc == 1))
                    if dc == 1:
                        mmi.then_inc(smm.h, 1)
            p.op("tensor", fq)
            smm.val += 1

            zsrc = TGT2[:, it * dA:(it + 1) * dA]

            def fz1(v, zsrc=zsrc):
                v.tensor_reduce(STT[:, 0:1], zsrc[:, 0:128], AX.X, A.add)
                v.tensor_reduce(STT[:, 1:2], zsrc[:, 128:256], AX.X, A.add)
                v.tensor_tensor(TME2[:], zsrc, zsrc, A.mult)
                v.tensor_reduce(STT[:, 2:3], TME2[:, 0:128], AX.X, A.add)
                v.tensor_reduce(STT[:, 3:4], TME2[:, 128:256], AX.X, A.add)
                v.tensor_scalar(STT[:, 12:14], STT[:, 0:2], 1.0 / 128.0, None,
                                A.mult)
                v.tensor_scalar(STT[:, 2:4], STT[:, 2:4], 1.0 / 128.0, None,
                                A.mult)
                v.tensor_tensor(STT[:, 4:6], STT[:, 12:14], STT[:, 12:14], A.mult)
                v.tensor_tensor(STT[:, 2:4], STT[:, 2:4], STT[:, 4:6],
                                A.subtract).then_inc(sv.h, 1)
            p.op("vector", fz1)
            sv.val += 1
            p.wait("scalar", sv, sv.val)

            def fz2(s):
                s.activation(SQS[:, 2:4], STT[:, 2:4], AF.Sqrt,
                             bias=CST[:, 0:1], scale=1.0).then_inc(ssc.h, 1)
            p.op("scalar", fz2)
            ssc.val += 1
            p.wait("vector", ssc, ssc.val)
            p.wait("vector", smm, smm.val)
            def fz3(v, zsrc=zsrc, b0=b0):
                v.reciprocal(STT[:, 16:18], SQS[:, 2:4])
                v.drain()
                v.tensor_scalar(TME[:, 0:128], zsrc[:, 0:128], STT[:, 12:13],
                                STT[:, 16:17], A.subtract, A.mult)
                v.tensor_scalar(TME[:, 128:256], zsrc[:, 128:256], STT[:, 13:14],
                                STT[:, 17:18], A.subtract, A.mult)
                v.tensor_tensor(TME[:], TME[:], GB[:], A.mult)
                v.tensor_tensor(TME[:], TME[:], BB[:], A.add)
                v.tensor_tensor(TME[:], TME[:], bank(b0)[:, 0:256], A.add
                                ).then_inc(sv.h, 1)
            p.op("vector", fz3)
            sv.val += 1

            ln_stats(TME[:], dA, 1, TME2[:])
            def fp3(v):
                v.reciprocal(STT[:, 15:16], SQS[:, 2:3])
                v.drain()
                v.tensor_scalar(TME2[:], TME[:], STT[:, 12:13], STT[:, 15:16],
                                A.subtract, A.mult)
                v.tensor_tensor(TME2[:], TME2[:], PWB[:], A.mult)
                v.tensor_tensor(TMB[:], TME2[:], PBB[:], A.add).then_inc(sv.h, 1)
            p.op("vector", fp3)
            sv.val += 1
            p.wait("tensor", sv, sv.val)

            def ftw(t):
                t.transpose(PTRA[:, 0:128], TMB[:, 0:128], IDB[:])
                t.transpose(PTRA[:, 128:256], TMB[:, 128:256], IDB[:]
                            ).then_inc(smm.h, 1)
            p.op("tensor", ftw)
            smm.val += 1
            p.wait("scalar", smm, smm.val)
            def fpc(s):
                s.copy(PTT[:, 0:128], PTRA[:, 0:128])
                s.copy(PTT[:, 128:256], PTRA[:, 128:256]).then_inc(ssc.h, 1)
            p.op("scalar", fpc)
            ssc.val += 1
            p.wait("tensor", ssc, ssc.val)
            def fwo(t):
                for dc in range(2):
                    mmi = t.matmul(bank(0)[:, 256:512], PTT[:, dc * 128:(dc + 1) * 128],
                                   WO[:, dc * dA: dc * dA + dA],
                                   start=(dc == 0), stop=(dc == 1))
                    if dc == 1:
                        mmi.then_inc(smm.h, 1)
            p.op("tensor", fwo)
            smm.val += 1
            p.wait("vector", smm, smm.val)

            ln_stats(bank(0)[:, 256:512], dA, 1, TME2[:])
            def fl3(v):
                v.reciprocal(STT[:, 15:16], SQS[:, 2:3])
                v.drain()
                v.tensor_scalar(TME2[:], bank(0)[:, 256:512], STT[:, 12:13],
                                STT[:, 15:16], A.subtract, A.mult)
                v.tensor_tensor(TME2[:], TME2[:], LWB[:], A.mult)
                v.tensor_tensor(TMB[:], TME2[:], LBB[:], A.add).then_inc(sv.h, 1)
            p.op("vector", fl3)
            sv.val += 1
            p.wait("tensor", sv, sv.val)

            def ftu(t):
                t.transpose(PTRB[:, 0:128], TMB[:, 0:128], IDB[:])
                t.transpose(PTRB[:, 128:256], TMB[:, 128:256], IDB[:]
                            ).then_inc(smm.h, 1)
            p.op("tensor", ftu)
            smm.val += 1
            p.wait("scalar", smm, smm.val)
            def fpc2(s):
                s.copy(PTT[:, 0:128], PTRB[:, 0:128])
                s.copy(PTT[:, 128:256], PTRB[:, 128:256]).then_inc(ssc.h, 1)
            p.op("scalar", fpc2)
            ssc.val += 1
            p.wait("tensor", ssc, ssc.val)
            upps = PBIG1[:, 0:1024]
            def fup(t, upps=upps):
                for j in range(2):
                    for dc in range(2):
                        t.matmul(upps[:, j * 512:(j + 1) * 512],
                                 PTT[:, dc * 128:(dc + 1) * 128],
                                 UW[:, dc * D + j * 512: dc * D + j * 512 + 512],
                                 start=(dc == 0), stop=False)
                    mmi = t.matmul(upps[:, j * 512:(j + 1) * 512], ONESCb[:],
                                   UPB[0:1, j * 512:(j + 1) * 512],
                                   start=False, stop=True)
                    if j == 1:
                        mmi.then_inc(smm.h, 1)
            p.op("tensor", fup)
            smm.val += 1
            p.wait("vector", smm, smm.val)

            ln_stats(upps, D, 1, TMPA[:])
            xres_t = XRES[:, it * D:(it + 1) * D]
            def fx2(v, upps=upps, xres_t=xres_t, dummy=dummy):
                v.reciprocal(STT[:, 15:16], SQS[:, 2:3])
                v.drain()
                v.tensor_scalar(TMPA[:], upps, STT[:, 12:13], STT[:, 15:16],
                                A.subtract, A.mult)
                v.tensor_tensor(TMPD[:], xres_t, TMPA[:], A.add)
                if dummy:
                    v.tensor_copy(STT[:, 9:10], TMPD[:, 0:1]).then_inc(sv.h, 1)
                else:
                    v.tensor_copy(xres_t, TMPD[:]).then_inc(sv.h, 1)
            p.op("vector", fx2)
            sv.val += 1

            ln_stats(TMPD[:], D, 1, TMPA[:])
            def fl33(v):
                v.reciprocal(STT[:, 15:16], SQS[:, 2:3])
                v.drain()
                v.tensor_scalar(TMPA[:], TMPD[:], STT[:, 12:13], STT[:, 15:16],
                                A.subtract, A.mult).then_inc(sv.h, 1)
            p.op("vector", fl33)
            sv.val += 1
            p.wait("tensor", sv, sv.val)
            tb = PBIG0[:, 1024:2048]
            def ftr3(t, tb=tb):
                for c in range(8):
                    mmi = t.transpose(tb[:, c * 128:(c + 1) * 128],
                                      TMPA[:, c * 128:(c + 1) * 128], IDF[:])
                    if c == 7:
                        mmi.then_inc(smm.h, 1)
            p.op("tensor", ftr3)
            smm.val += 1
            p.wait("scalar", smm, smm.val)
            def fcp3(s, tb=tb):
                for c in range(8):
                    s.copy(TMPD[:, c * 128:(c + 1) * 128], tb[:, c * 128:(c + 1) * 128])
                s.copy(SQS[0:1, 3:4], CST[0:1, 3:4]).then_inc(ssc.h, 1)
            p.op("scalar", fcp3)
            ssc.val += 1
            p.wait("vector", ssc, ssc.val)
            p.wait("tensor", ssc, ssc.val)
            def fcst(v, it=it):
                v.tensor_copy(LNT[:].rearrange("p (c t) -> p c t", c=8)
                              [:, :, it * 128:(it + 1) * 128],
                              TMPD[:].rearrange("p (c t) -> p c t", c=8)
                              ).then_inc(sv.h, 1)
            p.op("vector", fcst)
            sv.val += 1
            gps = bank(2)
            def fgm(t, gps=gps):
                for dc in range(8):
                    mmi = t.matmul(gps[0:128, 0:16], TMPD[:, dc * 128:(dc + 1) * 128],
                                   GWT[:, dc * NE:(dc + 1) * NE],
                                   start=(dc == 0), stop=(dc == 7))
                    if dc == 7:
                        mmi.then_inc(smm.h, 1)
            p.op("tensor", fgm)
            smm.val += 1
            p.wait("scalar", smm, smm.val)
            def fsig(s, it=it, gps=gps):
                s.copy(GVAL[:, it * NE:(it + 1) * NE], gps[0:128, 0:16]
                       ).then_inc(ssc.h, 1)
            p.op("scalar", fsig)
            ssc.val += 1

        p.wait("scalar", ssc, ssc.val)

        def fsigall(s):
            s.activation(GVAL[:], GVAL[:], AF.Sigmoid, bias=0.0, scale=1.0
                         ).then_inc(ssc.h, 1)
        p.op("scalar", fsigall)
        ssc.val += 1
        p.wait("vector", ssc, ssc.val)
        for it in range(NT):
            def ftk(v, it=it):
                g0 = GVAL[:, it * NE:(it + 1) * NE]
                v.tensor_copy(GWRK[:], g0)
                v.drain()
                for _ in range(4):
                    v.reduce_max(GMAX[:], GWRK[:], AX.X)
                    v.drain()
                    v.tensor_scalar(GEQ[:], GWRK[:], GMAX[:], None, A.is_equal)
                    v.drain()
                    v.scalar_tensor_tensor(GWRK[:], GEQ[:], -1e30, GWRK[:],
                                           A.mult, A.add)
                    v.drain()
                v.tensor_tensor(COMB[:], GWRK[:], g0, A.not_equal)
                v.drain()
                v.tensor_tensor(COMB[:], COMB[:], g0, A.mult)
                v.drain()
                v.tensor_reduce(RS[:], COMB[:], AX.X, A.add)
                v.drain()
                v.reciprocal(RS[:], RS[:])
                v.drain()
                v.tensor_scalar(COMB[:], COMB[:], RS[:], None, A.mult
                                ).then_inc(sv.h, 1)
            p.op("vector", ftk)
            sv.val += 1
            p.wait("tensor", sv, sv.val)
            if it >= 1:
                p.wait("tensor", sv, sv.val)

            def ftc(t):
                t.transpose(bank(2)[0:16, 128:256], COMB[:], IDF[:]).then_inc(smm.h, 1)
            p.op("tensor", ftc)
            smm.val += 1
            p.wait("vector", smm, smm.val)

            def fcc(v, it=it):
                v.tensor_copy(COMBT[:, it * 128:(it + 1) * 128],
                              bank(2)[0:16, 128:256]).then_inc(sv.h, 1)
            p.op("vector", fcc)
            sv.val += 1
        post_v, post_sc = sv.val, ssc.val
        tap("x2", XRES[:], sv, post_v)
        tap("tme", TME[:], sv, post_v)
        tap("tmpa", TMPA[:], sv, post_v)
        tap("tmpd", TMPD[:], sv, post_v)
        tap("ptt", PTT[:], sv, post_v)
        tap("lnt3", LNT[:], sv, post_v)
        tap("gval", GVAL[:], sv, post_v)
        tap("combt", COMBT[:], sv, post_v)

        p.wait("sync", sv, post_v)
        dma("sync", ld, scr2[:], XRES[:])
        x2_spill = ld.val

        if STOP <= 7:
            _finish(p, sst, out_d, ACCH)
            return _close(block, p, nc)
        p.wait("gpsimd", sv, post_v)
        dma("gpsimd", wt2, WSIN, wsin_d[:])
        dma("gpsimd", wt2, VSIN, vsin_d[:])
        dma("gpsimd", wt2, WSOUT, wsout_d[:])
        JOBS = []
        for pa in range(2):
            JOBS.append((pa, "sh", 0))
            JOBS.append((pa, "sh", 1))
            for e in range(NE):
                JOBS.append((pa, "e", e))
        ejobs = [ji for ji, j in enumerate(JOBS) if j[1] == "e"]
        eidx = {ji: k for k, ji in enumerate(ejobs)}
        winv_ready = {}
        wout_ready = {}
        combrow_ready = {}
        in_done = {}
        out_done = {}
        for k in range(2):
            e = JOBS[ejobs[k]][2]
            dma("gpsimd", wt2, WINb[k % 2], ewin_d[e])
            dma("gpsimd", wt2, VINb[k % 2], evin_d[e])
            winv_ready[ejobs[k]] = wt2.val
        wt_shared_all = wt2.val

        last_facc = {}
        final_last_v = 0
        out_stores = 0
        p.wait("vector", ld, x2_spill)

        for ji, (pa, kind, e) in enumerate(JOBS):
            t0 = pa * TP
            first = (kind == "sh" and e == 0)
            hc0 = 4 * e if kind == "sh" else 0

            if ji == 0:
                p.wait("tensor", wt2, wt_shared_all)
            if pa == 1 and kind == "sh" and e == 0:
                p.wait("tensor", sv, final_last_v)
            if ji > 0:
                p.wait("tensor", sv, last_facc[ji - 1])
            if kind == "e":
                p.wait("tensor", wt2, winv_ready[ji])
                win_s, vin_s = WINb[eidx[ji] % 2], VINb[eidx[ji] % 2]
            else:
                win_s, vin_s = WSIN, VSIN

            if kind == "e":
                k_ = eidx[ji]
                if k_ not in combrow_ready:
                    p.wait("sync", smm, smm.val)
                    dma("sync", ld, COMBROW[0:1, (k_ % 2) * TP:(k_ % 2 + 1) * TP],
                        COMBT[e:e + 1, t0: t0 + TP])
                    combrow_ready[k_] = ld.val
                p.wait("tensor", ld, combrow_ready[k_])
                cslot = (k_ % 2) * TP

                def fcb(t, cslot=cslot):
                    t.matmul(bank(2)[:, 0:512], ONESCb[:],
                             COMBROW[0:1, cslot:cslot + 512], start=True, stop=True)
                    t.matmul(bank(3)[:, 0:512], ONESCb[:],
                             COMBROW[0:1, cslot + 512:cslot + 1024], start=True,
                             stop=True).then_inc(smm.h, 1)
                p.op("tensor", fcb)
                smm.val += 1
                p.wait("vector", smm, smm.val)

                def fcbc(v):
                    v.tensor_copy(CBX[:, 0:512], bank(2)[:, 0:512])
                    v.tensor_copy(CBX[:, 512:1024], bank(3)[:, 0:512]
                                  ).then_inc(sv.h, 1)
                p.op("vector", fcbc)
                sv.val += 1
                p.wait("tensor", sv, sv.val)
                if k_ + 1 < len(ejobs):
                    jn = ejobs[k_ + 1]
                    pa_n, _, e_n = JOBS[jn]
                    p.wait("sync", smm, smm.val)
                    dma("sync", ld,
                        COMBROW[0:1, ((k_ + 1) % 2) * TP:((k_ + 1) % 2 + 1) * TP],
                        COMBT[e_n:e_n + 1, pa_n * TP: pa_n * TP + TP])
                    combrow_ready[k_ + 1] = ld.val

            evac_vals = {}
            for hc in range(4):
                hcw = hc0 + hc
                hb = PBIG0[:, (hc % 2) * 1024:(hc % 2) * 1024 + 1024]
                vb = PBIG1[:, 0:1024]

                if hc >= 2:
                    p.wait("tensor", sv, evac_vals[hc - 2])

                def fiph(t, win_s=win_s, hcw=hcw, hb=hb, t0=t0, kind=kind):
                    for dc in range(8):
                        if kind == "sh":
                            lw_ = win_s[:, dc * SHID + hcw * 128:
                                        dc * SHID + hcw * 128 + 128]
                        else:
                            lw_ = win_s[:, dc * EH + hcw * 128:
                                        dc * EH + hcw * 128 + 128]
                        for j in range(2):
                            mmi = t.matmul(hb[:, j * 512:(j + 1) * 512], lw_,
                                           LNT[:, dc * T + t0 + j * 512:
                                               dc * T + t0 + j * 512 + 512],
                                           start=(dc == 0), stop=(dc == 7))
                            if dc == 7 and j == 1:
                                mmi.then_inc(smm.h, 1)
                p.op("tensor", fiph)
                smm.val += 1
                hin_mm = smm.val

                if hc >= 1:
                    p.wait("tensor", sv, evac_vals[hc - 1])

                def fipv(t, vin_s=vin_s, hcw=hcw, vb=vb, t0=t0, kind=kind):
                    for dc in range(8):
                        if kind == "sh":
                            lv_ = vin_s[:, dc * SHID + hcw * 128:
                                        dc * SHID + hcw * 128 + 128]
                        else:
                            lv_ = vin_s[:, dc * EH + hcw * 128:
                                        dc * EH + hcw * 128 + 128]
                        for j in range(2):
                            mmi = t.matmul(vb[:, j * 512:(j + 1) * 512], lv_,
                                           LNT[:, dc * T + t0 + j * 512:
                                               dc * T + t0 + j * 512 + 512],
                                           start=(dc == 0), stop=(dc == 7))
                            if dc == 7 and j == 1:
                                mmi.then_inc(smm.h, 1)
                p.op("tensor", fipv)
                smm.val += 1
                p.wait("scalar", smm, hin_mm)

                def fsi(s, hb=hb):
                    s.activation(MTMP[:], hb, AF.Silu, bias=0.0, scale=1.0
                                 ).then_inc(ssc.h, 1)
                p.op("scalar", fsi)
                ssc.val += 1
                p.wait("vector", ssc, ssc.val)
                p.wait("vector", smm, smm.val)

                def fhm(v, vb=vb, hc=hc, kind=kind):
                    if kind == "e":
                        v.tensor_tensor(HE[:, hc * TP:(hc + 1) * TP], MTMP[:], vb,
                                        A.mult)
                        v.tensor_tensor(HE[:, hc * TP:(hc + 1) * TP],
                                        HE[:, hc * TP:(hc + 1) * TP],
                                        CBX[:, 0:TP], A.mult).then_inc(sv.h, 1)
                    else:
                        v.tensor_tensor(HE[:, hc * TP:(hc + 1) * TP], MTMP[:], vb,
                                        A.mult).then_inc(sv.h, 1)
                p.op("vector", fhm)
                sv.val += 1
                evac_vals[hc] = sv.val
            in_done[ji] = smm.val
            h_ready = sv.val
            p.wait("tensor", sv, h_ready)

            if kind == "e":
                k = eidx[ji]
                if ji - 1 in out_done:
                    p.wait("gpsimd", smm, out_done[ji - 1])
                dma("gpsimd", wt2, WOUTb, ewout_d[e])
                wout_ready[ji] = wt2.val
                if k + 2 < len(ejobs):
                    p.wait("gpsimd", smm, in_done[ji])
                    e2 = JOBS[ejobs[k + 2]][2]
                    dma("gpsimd", wt2, WINb[(k + 2) % 2], ewin_d[e2])
                    dma("gpsimd", wt2, VINb[(k + 2) % 2], evin_d[e2])
                    winv_ready[ejobs[k + 2]] = wt2.val

            if kind == "e":
                p.wait("tensor", wt2, wout_ready[ji])
            facc_vals = {}
            OBS = [PBIG0[:, 0:1024], PBIG0[:, 1024:2048], PBIG1[:, 0:1024]]
            for dc in range(8):
                ob = OBS[dc % 3]
                if dc >= 3:
                    p.wait("tensor", sv, facc_vals[dc - 3])

                def fop(t, dc=dc, ob=ob, hc0=hc0, kind=kind):
                    for hc in range(4):
                        hcw = hc0 + hc
                        if kind == "sh":
                            lhs = WSOUT[:, hcw * D + dc * 128:
                                        hcw * D + dc * 128 + 128]
                        else:
                            lhs = WOUTb[:, hc * D + dc * 128:
                                        hc * D + dc * 128 + 128]
                        for j in range(2):
                            mmi = t.matmul(ob[:, j * 512:(j + 1) * 512], lhs,
                                           HE[:, hc * TP + j * 512:
                                               hc * TP + j * 512 + 512],
                                           start=(hc == 0), stop=(hc == 3))
                            if hc == 3 and j == 1:
                                mmi.then_inc(smm.h, 1)
                p.op("tensor", fop)
                smm.val += 1
                p.wait("vector", smm, smm.val)

                def facc(v, dc=dc, ob=ob, first=first):
                    dst = ACCH[:, dc * TP:(dc + 1) * TP]
                    if first:
                        v.tensor_copy(dst, ob).then_inc(sv.h, 1)
                    else:
                        v.tensor_tensor(dst, dst, ob, A.add).then_inc(sv.h, 1)
                p.op("vector", facc)
                sv.val += 1
                facc_vals[dc] = sv.val
            out_done[ji] = smm.val
            last_facc[ji] = sv.val

            if kind == "e" and e == NE - 1:
                moe_done_v = sv.val
                if pa == 0:
                    p.wait("sync", ld, x2_spill)
                for itl, fdummy in [(0, True)] + [(i, False) for i in range(8)]:
                    itg = pa * 8 + itl
                    pi = itl % 2
                    p.wait("sync", sv, sv.val)
                    dma("sync", ld, X2R[:], scr2[:, itg * D:(itg + 1) * D])
                    x2r_done = ld.val
                    p.wait("tensor", sv, sv.val)
                    tbf = PBIG0[:, pi * 1024:(pi + 1) * 1024]

                    def ftf(t, itl=itl, tbf=tbf):
                        for c in range(8):
                            mmi = t.transpose(tbf[:, c * 128:(c + 1) * 128],
                                              ACCH[:, c * TP + itl * 128:
                                                   c * TP + itl * 128 + 128],
                                              IDF[:])
                            if c == 7:
                                mmi.then_inc(smm.h, 1)
                    p.op("tensor", ftf)
                    smm.val += 1
                    p.wait("vector", smm, smm.val)
                    ln_stats(tbf, D, 1, TMPA[:])
                    p.wait("vector", ld, x2r_done)
                    if out_stores > 0:
                        p.wait("vector", sst, out_stores)

                    def ffin(v, tbf=tbf):
                        v.reciprocal(STT[:, 15:16], SQS[:, 2:3])
                        v.drain()
                        v.tensor_scalar(TMPA[:], tbf, STT[:, 12:13], STT[:, 15:16],
                                        A.subtract, A.mult)
                        v.tensor_tensor(OUTT[:], X2R[:], TMPA[:], A.add
                                        ).then_inc(sv.h, 1)
                    p.op("vector", ffin)
                    sv.val += 1
                    if not fdummy:
                        p.wait("sync", sv, sv.val)
                        dma("sync", sst, out_d[:, itg * D:(itg + 1) * D], OUTT[:])
                        out_stores = sst.val
                final_last_v = sv.val

        tap("acc0", ACCH[:], sv, sv.val)

        p.wait("gpsimd", sst, sst.val)
        return _close(block, p, nc)


def _finish(p, sst, out_d, ACCH):
    p.wait("gpsimd", sst, sst.val)


def _close(block, p, nc):
    if True:
        @block.sync
        def _(e):
            for fn in p.q["sync"]:
                fn(e)

        @block.gpsimd
        def _(e):
            for fn in p.q["gpsimd"]:
                fn(e)

        @block.tensor
        def _(e):
            for fn in p.q["tensor"]:
                fn(e)

        @block.vector
        def _(e):
            for fn in p.q["vector"]:
                fn(e)

        @block.scalar
        def _(e):
            for fn in p.q["scalar"]:
                fn(e)

    return nc


def _layouts(inputs):
    f = np.ascontiguousarray
    r = {}
    r["dw"] = f(inputs["L0_attn_down_w"].reshape(8, 128, dA).transpose(1, 0, 2)
                .reshape(128, 8 * dA))
    r["db"] = f(inputs["L0_attn_down_b"].reshape(2, 128).T)
    for nm, kk in [("wq", "L0_attn_wq"), ("wk", "L0_attn_wk"),
                   ("wv", "L0_attn_wv"), ("wo", "L0_attn_wo")]:
        r[nm] = f(inputs[kk].reshape(2, 128, dA).transpose(1, 0, 2).reshape(128, 2 * dA))
    r["uw"] = f(inputs["L0_attn_up_w"].reshape(2, 128, D).transpose(1, 0, 2)
                .reshape(128, 2 * D))
    r["upb"] = f(inputs["L0_attn_up_b"].reshape(1, D))
    r["w1"] = f(np.concatenate([inputs["L0_attn_W1"][0], inputs["L0_attn_W1"][1]], 1))
    r["b1"] = f(inputs["L0_attn_b1"].reshape(1, 2 * FH))
    r["glw"] = f(inputs["L0_attn_ttt_ln_w"].reshape(1, 2 * FH))
    r["glb"] = f(inputs["L0_attn_ttt_ln_b"].reshape(1, 2 * FH))
    r["lrw"] = f(inputs["L0_attn_lr_w"].reshape(2, 2, 128).transpose(2, 1, 0)
                 .reshape(128, 4))
    r["lrb"] = f(inputs["L0_attn_lr_b"].reshape(2, 1))
    r["tki"] = f(inputs["L0_attn_tok_idx"].reshape(1, K))
    r["pw"] = f(inputs["L0_attn_post_w"].reshape(1, dA))
    r["pb"] = f(inputs["L0_attn_post_b"].reshape(1, dA))
    r["lw"] = f(inputs["L0_attn_ln_w"].reshape(1, dA))
    r["lb"] = f(inputs["L0_attn_ln_b"].reshape(1, dA))
    r["gwt"] = f(inputs["L0_ffn_gate_w"].T.reshape(8, 128, NE).transpose(1, 0, 2)
                 .reshape(128, 8 * NE))
    r["wsin"] = f(inputs["L0_ffn_ws_in"].reshape(8, 128, SHID).transpose(1, 0, 2)
                  .reshape(128, 8 * SHID))
    r["vsin"] = f(inputs["L0_ffn_vs_in"].reshape(8, 128, SHID).transpose(1, 0, 2)
                  .reshape(128, 8 * SHID))
    r["wsout"] = f(inputs["L0_ffn_ws_out"].reshape(8, 128, D).transpose(1, 0, 2)
                   .reshape(128, 8 * D))
    r["ewin"] = f(inputs["L0_ffn_e_win"].reshape(NE, 8, 128, EH)
                  .transpose(0, 2, 1, 3).reshape(NE, 128, 8 * EH))
    r["evin"] = f(inputs["L0_ffn_e_vin"].reshape(NE, 8, 128, EH)
                  .transpose(0, 2, 1, 3).reshape(NE, 128, 8 * EH))
    r["ewout"] = f(inputs["L0_ffn_e_wout"].reshape(NE, 4, 128, D)
                   .transpose(0, 2, 1, 3).reshape(NE, 128, 4 * D))
    cstv = np.zeros((128, 8), np.float32)
    cstv[:, 0] = 1e-6
    cstv[:, 1] = 1e-5
    cstv[:, 2] = 1.0 / 128.0
    cstv[:, 3] = 1.0
    cstv[:, 5] = -1.0
    r["cst"] = cstv
    r["idf"] = np.eye(128, dtype=np.float32)
    r["ones16"] = np.ones((1, 16), np.float32)
    r["onesc"] = np.ones((1, 128), np.float32)
    r["negu"] = f(-np.triu(np.ones((16, 16), np.float32)) / 128.0)
    r["rk"] = (1.0 / np.arange(1, K + 1, dtype=np.float32)).reshape(1, K)
    return r


def kernel(**inputs):
    inputs = {k: np.asarray(v) for k, v in inputs.items()}
    shared = _layouts(inputs)
    x = inputs["L0_attn_x"]
    nc = build_program()
    in_maps = []
    for b in range(B):
        m = dict(shared)
        m["xin"] = np.ascontiguousarray(
            x[b].reshape(NT, 128, D).transpose(1, 0, 2).reshape(128, NT * D))
        in_maps.append(m)
    res = run_bass_kernel_spmd(nc, in_maps, core_ids=list(range(B)))
    outs = []
    for b in range(B):
        o = np.asarray(res.results[b]["out"])
        outs.append(o.reshape(128, NT, D).transpose(1, 0, 2).reshape(T, D))
    return np.stack(outs).astype(np.float32)


# version 05
# speedup vs baseline: 1.1135x; 1.0076x over previous
import numpy as np
import concourse.bass as bass
import concourse.mybir as mybir
from concourse.bass_utils import run_bass_kernel_spmd

F32 = mybir.dt.float32
BF16 = mybir.dt.bfloat16
A = mybir.AluOpType
AF = mybir.ActivationFunctionType
AX = mybir.AxisListType

B, T, D = 8, 2048, 1024
dA = 256
H, FH, K = 2, 128, 16
NSTEP = T // K
NT = T // 128
NE, EH, SHID = 16, 512, 1024
TP = T // 2
DEV_TAPS = []
STOP = 99
DEV_LN0_TILES = None
DEV_SCAN_STEPS = None


class Sem:
    def __init__(self, handle):
        self.h = handle
        self.val = 0


class P:
    def __init__(self):
        self.q = {e: [] for e in ("tensor", "vector", "scalar", "gpsimd", "sync")}

    def op(self, eng, fn):
        self.q[eng].append(fn)

    def wait(self, eng, sem, val):
        if val > 0:
            self.q[eng].append(lambda e, s=sem.h, v=val: e.wait_ge(s, v))


def build_program():
    nc = bass.Bass("TRN2", target_bir_lowering=False)

    def din(name, shape, dtype=F32):
        return nc.dram_tensor(name, list(shape), dtype, kind="ExternalInput").ap()

    xin = din("xin", (128, NT * D))
    dw_d = din("dw", (128, 8 * dA)); db_d = din("db", (128, 2))
    wq_d = din("wq", (128, 2 * dA)); wk_d = din("wk", (128, 2 * dA))
    wv_d = din("wv", (128, 2 * dA)); wo_d = din("wo", (128, 2 * dA))
    uw_d = din("uw", (128, 2 * D)); upb_d = din("upb", (1, D))
    w1_d = din("w1", (128, 2 * FH)); b1_d = din("b1", (1, 2 * FH))
    glw_d = din("glw", (1, 2 * FH)); glb_d = din("glb", (1, 2 * FH))
    lrw_d = din("lrw", (128, 4)); lrb_d = din("lrb", (2, 1))
    tki_d = din("tki", (1, K))
    pw_d = din("pw", (1, dA)); pb_d = din("pb", (1, dA))
    lw_d = din("lw", (1, dA)); lb_d = din("lb", (1, dA))
    gwt_d = din("gwt", (128, 8 * NE))
    wsin_d = din("wsin", (128, 8 * SHID)); vsin_d = din("vsin", (128, 8 * SHID))
    wsout_d = din("wsout", (128, 8 * D))
    ewin_d = din("ewin", (NE, 128, 8 * EH))
    evin_d = din("evin", (NE, 128, 8 * EH))
    ewout_d = din("ewout", (NE, 128, 4 * D))
    cst_d = din("cst", (128, 8))
    idf_d = din("idf", (128, 128))
    ones16_d = din("ones16", (1, 16))
    onesc_d = din("onesc", (1, 128))
    negu_d = din("negu", (16, 16))
    rk_d = din("rk", (1, K))

    out_d = nc.dram_tensor("out", [128, NT * D], F32, kind="ExternalOutput").ap()
    scr = nc.dram_tensor("scr", [2 * T], F32, kind="Internal").ap()
    scr2 = nc.dram_tensor("scr2", [128, NT * D], BF16, kind="Internal").ap()

    taps = {}
    tap_specs = {
        "lnt": [128, 8 * T], "ht": [128, 2 * T], "xqt": [128, 2 * T],
        "xkt": [128, 2 * T], "siglr": [2, T], "xkle": [128, NT * dA],
        "tgt2": [128, NT * dA], "zbar": [128, NT * dA], "x2": [128, NT * D],
        "lnt3": [128, 8 * T], "gval": [128, NT * NE], "combt": [16, T],
        "w1f": [128, 2 * FH], "b1f": [1, 2 * FH], "sigt": [128, 2 * K],
        "lrc": [16, 2 * NSTEP], "acc0": [128, 8 * TP],
        "he": [128, D], "stt": [128, 20], "xktmp": [128, dA],
        "mv": [16, 4], "rstd": [16, 2], "grb": [16, 2 * FH], "xhs": [16, 2 * FH],
        "gys": [16, 2 * FH], "ctb": [16, 2 * K], "m1n": [16, 2], "m2": [16, 2], "std": [16, 2], "sqs": [128, 4],
        "tme": [128, dA], "tmpa": [128, D], "tmpd": [128, D], "ptt": [128, dA],
    }
    for name in DEV_TAPS:
        dt_ = BF16 if name in ("combt",) else F32
        taps[name] = nc.dram_tensor("tap_" + name, tap_specs[name], dt_,
                                    kind="ExternalOutput").ap()

    p = P()

    from contextlib import ExitStack
    with ExitStack() as _es:
        XRES = _es.enter_context(nc.sbuf_tensor("XRES", [128, NT * D], BF16))
        LNT = _es.enter_context(nc.sbuf_tensor("LNT", [128, 8 * T], BF16))
        BIGP = _es.enter_context(nc.sbuf_tensor("BIGP", [128, 4 * T], BF16))
        BIGQ = _es.enter_context(nc.sbuf_tensor("BIGQ", [128, 4 * T], BF16))
        TGT2F = _es.enter_context(nc.sbuf_tensor("TGT2F", [128, 2 * NT * dA], BF16))
        HE = _es.enter_context(nc.sbuf_tensor("HE", [128, 4 * TP], BF16))
        WBIGA = _es.enter_context(nc.sbuf_tensor("WBIGA", [128, 8 * dA + 2 * D], BF16))
        WINbT = _es.enter_context(nc.sbuf_tensor("WINb", [128, 2 * 8 * EH], BF16))
        VINbT = _es.enter_context(nc.sbuf_tensor("VINb", [128, 2 * 8 * EH], BF16))
        CBX = _es.enter_context(nc.sbuf_tensor("CBX", [128, TP], BF16))
        TGT16R = _es.enter_context(nc.sbuf_tensor("TGT16R", [16, 4 * dA], BF16))
        XKLE16R = _es.enter_context(nc.sbuf_tensor("XKLE16R", [16, 4 * dA], BF16))
        ZB16 = _es.enter_context(nc.sbuf_tensor("ZB16", [16, 2 * dA], BF16))
        COMBROW = _es.enter_context(nc.sbuf_tensor("COMBROW", [1, 2 * TP], BF16))
        MTMP = _es.enter_context(nc.sbuf_tensor("MTMP", [128, TP], BF16))
        COMBT = _es.enter_context(nc.sbuf_tensor("COMBT", [16, T], BF16))
        DB = _es.enter_context(nc.sbuf_tensor("DB", [128, 2], F32))
        WQ = _es.enter_context(nc.sbuf_tensor("WQ", [128, 2 * dA], BF16))
        WK = _es.enter_context(nc.sbuf_tensor("WK", [128, 2 * dA], BF16))
        WV = _es.enter_context(nc.sbuf_tensor("WV", [128, 2 * dA], BF16))
        WO = _es.enter_context(nc.sbuf_tensor("WO", [128, 2 * dA], BF16))
        UPB = _es.enter_context(nc.sbuf_tensor("UPB", [1, D], BF16))
        W1f = _es.enter_context(nc.sbuf_tensor("W1f", [128, 2 * FH], F32))
        W1b = _es.enter_context(nc.sbuf_tensor("W1b", [128, 2 * FH], BF16))
        B1f = _es.enter_context(nc.sbuf_tensor("B1f", [1, 2 * FH], F32))
        LRWT = _es.enter_context(nc.sbuf_tensor("LRWT", [128, 4], BF16))
        LRB = _es.enter_context(nc.sbuf_tensor("LRB", [2, 1], F32))
        GWT = _es.enter_context(nc.sbuf_tensor("GWT", [128, 8 * NE], F32))
        CST = _es.enter_context(nc.sbuf_tensor("CST", [128, 8], F32))
        IDF = _es.enter_context(nc.sbuf_tensor("IDF", [128, 128], F32))
        IDB = _es.enter_context(nc.sbuf_tensor("IDB", [128, 128], BF16))
        ONES16f = _es.enter_context(nc.sbuf_tensor("ONES16f", [1, 16], F32))
        ONESCb = _es.enter_context(nc.sbuf_tensor("ONESCb", [1, 128], BF16))
        NEGU = _es.enter_context(nc.sbuf_tensor("NEGU", [16, 16], F32))
        RK = _es.enter_context(nc.sbuf_tensor("RK", [1, K], F32))
        TI = _es.enter_context(nc.sbuf_tensor("TI", [1, K], F32))
        TI15b = _es.enter_context(nc.sbuf_tensor("TI15b", [128, 1], F32))
        GB = _es.enter_context(nc.sbuf_tensor("GB", [128, 2 * FH], F32))
        BB = _es.enter_context(nc.sbuf_tensor("BB", [128, 2 * FH], F32))
        G2B = _es.enter_context(nc.sbuf_tensor("G2B", [128, 2 * FH], F32))
        PWB = _es.enter_context(nc.sbuf_tensor("PWB", [128, dA], F32))
        PBB = _es.enter_context(nc.sbuf_tensor("PBB", [128, dA], F32))
        LWB = _es.enter_context(nc.sbuf_tensor("LWB", [128, dA], F32))
        LBB = _es.enter_context(nc.sbuf_tensor("LBB", [128, dA], F32))
        SIGT = _es.enter_context(nc.sbuf_tensor("SIGT", [128, 2 * NT], F32))
        LETOK = _es.enter_context(nc.sbuf_tensor("LETOK", [128, 2 * NT], F32))
        LECB = _es.enter_context(nc.sbuf_tensor("LECB", [16, 2 * NSTEP], BF16))
        TOKLN = _es.enter_context(nc.sbuf_tensor("TOKLN", [16, 16], F32))
        BN6 = _es.enter_context(nc.sbuf_tensor("BN6", [16, 12], F32))
        MV = _es.enter_context(nc.sbuf_tensor("MV", [16, 4], F32))
        STD = _es.enter_context(nc.sbuf_tensor("STD", [16, 2], F32))
        RSTD = _es.enter_context(nc.sbuf_tensor("RSTD", [16, 2], F32))
        M1N = _es.enter_context(nc.sbuf_tensor("M1N", [16, 2], F32))
        M2 = _es.enter_context(nc.sbuf_tensor("M2", [16, 2], F32))
        CT1 = _es.enter_context(nc.sbuf_tensor("CT1", [16, 2 * K], F32))
        CTB = _es.enter_context(nc.sbuf_tensor("CTB", [16, 2 * K], BF16))
        TMPA = _es.enter_context(nc.sbuf_tensor("TMPA", [128, D], F32))
        TMPD = _es.enter_context(nc.sbuf_tensor("TMPD", [128, D], F32))
        TME = _es.enter_context(nc.sbuf_tensor("TME", [128, dA], F32))
        TME2 = _es.enter_context(nc.sbuf_tensor("TME2", [128, dA], F32))
        XKtmp = _es.enter_context(nc.sbuf_tensor("XKtmp", [128, dA], F32))
        TMB = _es.enter_context(nc.sbuf_tensor("TMB", [128, dA], BF16))
        PTT = _es.enter_context(nc.sbuf_tensor("PTT", [128, dA], BF16))
        STT = _es.enter_context(nc.sbuf_tensor("STT", [128, 20], F32))
        SQS = _es.enter_context(nc.sbuf_tensor("SQS", [128, 4], F32))
        PBIG0 = _es.enter_context(nc.psum_tensor("PBIG0", [128, 2048], F32))
        PBIG1 = _es.enter_context(nc.psum_tensor("PBIG1", [128, 1024], F32))
        PTRA = _es.enter_context(nc.psum_tensor("PTRA", [128, 1024], BF16))
        PTRB = _es.enter_context(nc.psum_tensor("PTRB", [128, 1024], BF16))
        _ld = _es.enter_context(nc.semaphore("s_ld"))
        _wt = _es.enter_context(nc.semaphore("s_wt"))
        _mm = _es.enter_context(nc.semaphore("s_mm"))
        _v = _es.enter_context(nc.semaphore("s_v"))
        _sc = _es.enter_context(nc.semaphore("s_sc"))
        _g = _es.enter_context(nc.semaphore("s_g"))
        _st = _es.enter_context(nc.semaphore("s_st"))
        block = _es.enter_context(nc.Block())

        ld, wt2, smm, sv, ssc, sg, sst = (Sem(_ld), Sem(_wt), Sem(_mm), Sem(_v),
                                          Sem(_sc), Sem(_g), Sem(_st))

        OUTT = TMPD
        SIGLR = WINbT[:].bitcast(F32)[0:2, 0:T]
        ROWS = VINbT[:].bitcast(F32)[0:1, 0:6 * dA]
        LRC = WINbT[:].bitcast(F32)[0:16, 2048:2048 + 2 * NSTEP]
        GVAL = VINbT[:].bitcast(F32)[:, 1536:1536 + NT * NE]
        SCR_SC = MTMP[:].bitcast(F32)[0:16, 0:128]
        X2R = MTMP
        TI15 = TI15b[0:16, :]
        XHs = TMPD[0:16, 0:256]
        GYs = TMPD[0:16, 256:512]
        PRD = TMPD[0:16, 512:768]
        GRF = TMPD[0:16, 768:1024]
        GRB = TMPA[:].bitcast(BF16)[0:16, 0:256]
        Z1S = TMPA[0:16, 256:512]
        GWRK = TME2[:, 0:16]
        GEQ = TME2[:, 16:32]
        COMB = TME2[:, 32:48]
        GMAX = TME2[:, 48:49]
        RS = TME2[:, 49:50]
        WINb = [WINbT[:, 0:4096], WINbT[:, 4096:8192]]
        VINb = [VINbT[:, 0:4096], VINbT[:, 4096:8192]]
        XQT = BIGP[:, 0:2 * T]
        XKT = BIGP[:, 2 * T:4 * T]
        HT = BIGQ[:, 0:2 * T]
        XKLE = BIGQ[:, 2 * T:2 * T + NT * dA]
        WSIN = BIGP[:]
        VSIN = BIGQ[:]
        TGT2 = TGT2F[:, 0:NT * dA]
        WSOUT = TGT2F[:]
        DW = WBIGA[:, 0:8 * dA]
        UW = WBIGA[:, 8 * dA:8 * dA + 2 * D]
        WOUTb = WBIGA[:]
        ACCH = XRES[:].bitcast(F32)

        def bank(i):
            assert i < 6
            if i < 4:
                return PBIG0[:, i * 512:(i + 1) * 512]
            return PBIG1[:, (i - 4) * 512:(i - 4 + 1) * 512]

        def tap(name, src, sem=None, val=0):
            if name in taps:
                if sem is not None:
                    p.wait("gpsimd", sem, val)
                dst = taps[name]
                def fn(g, dst=dst, s=src):
                    g.dma_start(out=dst, in_=s).then_inc(sst.h, 16)
                p.op("gpsimd", fn)
                sst.val += 16

        def dma(eng, sem, dst, src, **kw):
            def fn(e, dst=dst, src=src, sem=sem, kw=kw):
                e.dma_start(out=dst, in_=src, **kw).then_inc(sem.h, 16)
            p.op(eng, fn)
            sem.val += 16

        for dst, src in [(CST[:], cst_d[:]), (IDF[:], idf_d[:]),
                         (ONES16f[:], ones16_d[:]), (NEGU[:], negu_d[:]),
                         (RK[:], rk_d[:]), (W1f[:], w1_d[:]), (B1f[:], b1_d[:]),
                         (TI[:], tki_d[:]), (DB[:], db_d[:]), (GWT[:], gwt_d[:])]:
            dma("sync", ld, dst, src)
        dma("sync", ld, LRB[:], lrb_d[:], allow_slow_non_contiguous=True)
        for i, src in enumerate([glw_d, glb_d, pw_d, pb_d, lw_d, lb_d]):
            dma("sync", ld, ROWS[:, i * dA:(i + 1) * dA][:, 0:src.shape[1]], src)
        ld_small = ld.val

        for dst, src in [(XRES[:], xin[:]), (DW, dw_d[:]), (WQ[:], wq_d[:]),
                         (WK[:], wk_d[:]), (WV[:], wv_d[:]), (WO[:], wo_d[:]),
                         (UW, uw_d[:]), (UPB[:], upb_d[:]), (LRWT[:], lrw_d[:]),
                         (IDB[:], idf_d[:]), (ONESCb[:], onesc_d[:]),
                         (W1b[:], w1_d[:])]:
            dma("gpsimd", wt2, dst, src)
        wt_attn = wt2.val

        nc.const_aps.register(F32, 1e-6, CST[:, 0:1])
        nc.const_aps.register(F32, 1e-5, CST[:, 1:2])
        nc.const_aps.register(F32, 1.0 / 128.0, CST[:, 2:3])
        nc.const_aps.register(F32, 1.0, CST[:, 3:4])
        nc.const_aps.register(F32, -1.0, CST[:, 5:6])
        p.wait("tensor", ld, ld_small)
        p.wait("tensor", wt2, wt_attn)
        ONES128f = ONES16f[0:1, 0:1].broadcast_to([1, 128])

        bc_dst = [GB, BB, PWB, PBB, LWB, LBB]
        bc_w = [2 * FH, 2 * FH, dA, dA, dA, dA]

        def f_bc_mm(t):
            for i in range(6):
                t.matmul(bank(i)[:, 0:bc_w[i]], ONES128f,
                         ROWS[0:1, i * dA: i * dA + bc_w[i]], start=True,
                         stop=True).then_inc(smm.h, 1)
        p.op("tensor", f_bc_mm)
        smm.val += 6
        p.wait("vector", smm, smm.val)

        def f_bc_cp(v):
            for i in range(6):
                v.tensor_copy(bc_dst[i][:, 0:bc_w[i]], bank(i)[:, 0:bc_w[i]])
            v.tensor_tensor(G2B[:], GB[:], GB[:], A.mult)
            v.tensor_tensor(TI[:], TI[:], RK[:], A.add)
            v.drain()
            v.tensor_relu(TI[:], TI[:])
            v.drain()
            v.tensor_copy(STT[0:1, 0:1], TI[0:1, 0:1]).then_inc(sv.h, 1)
        p.op("vector", f_bc_cp)
        sv.val += 1
        p.wait("tensor", sv, sv.val)

        def f_tok(t):
            t.matmul(bank(5)[0:16, 0:16], ONES16f[:], TI[:],
                     start=True, stop=True).then_inc(smm.h, 1)
        p.op("tensor", f_tok)
        smm.val += 1
        p.wait("vector", smm, smm.val)

        def f_tok2(v):
            v.tensor_tensor(TOKLN[:], bank(5)[0:16, 0:16], NEGU[:],
                            A.mult).then_inc(sv.h, 1)
        p.op("vector", f_tok2)
        sv.val += 1
        p.wait("tensor", sv, sv.val)

        def f_ti15(t):
            t.matmul(bank(5)[:, 16:17], ONES128f, TI[0:1, 15:16],
                     start=True, stop=True).then_inc(smm.h, 1)
        p.op("tensor", f_ti15)
        smm.val += 1
        p.wait("vector", smm, smm.val)

        def f_ti15c(v):
            v.tensor_copy(TI15b[:], bank(5)[:, 16:17]).then_inc(sv.h, 1)
        p.op("vector", f_ti15c)
        sv.val += 1
        ti15_done = sv.val

        def ln_stats(eng_src, width, eps_col, scrap):
            w = width

            def f1(v, src=eng_src, scrap=scrap, w=w):
                v.tensor_reduce(STT[:, 9:10], src, AX.X, A.add)
                v.tensor_copy(scrap[:, 0:w], src)
                v.tensor_tensor(scrap[:, 0:w], scrap[:, 0:w], scrap[:, 0:w],
                                A.mult)
                v.tensor_reduce(STT[:, 10:11], scrap[:, 0:w], AX.X, A.add)
                v.drain()
                v.tensor_scalar(STT[:, 12:13], STT[:, 9:10], 1.0 / w, None, A.mult)
                v.tensor_scalar(STT[:, 13:14], STT[:, 10:11], 1.0 / w, None, A.mult)
                v.drain()
                v.tensor_tensor(STT[:, 16:17], STT[:, 12:13], STT[:, 12:13], A.mult)
                v.drain()
                v.tensor_tensor(STT[:, 13:14], STT[:, 13:14], STT[:, 16:17],
                                A.subtract)
                v.drain()
                v.tensor_copy(STT[:, 8:9], STT[:, 13:14]).then_inc(sv.h, 1)
            p.op("vector", f1)
            sv.val += 1
            p.wait("scalar", sv, sv.val)

            def f2(sc, eps_col=eps_col):
                sc.activation(SQS[:, 2:3], STT[:, 13:14], AF.Sqrt,
                              bias=CST[:, eps_col:eps_col + 1], scale=1.0
                              ).then_inc(ssc.h, 1)
            p.op("scalar", f2)
            ssc.val += 1
            p.wait("vector", ssc, ssc.val)

        p.wait("vector", wt2, wt_attn)
        tr_done = {}
        p.wait("scalar", wt2, wt_attn)
        for it in range(NT if DEV_LN0_TILES is None else DEV_LN0_TILES):
            xsrc = XRES[:, it * D:(it + 1) * D]
            ln_stats(xsrc, D, 1, TMPA[:])
            if it >= 1:
                p.wait("vector", smm, tr_done[it - 1])

            def f3(v, src=xsrc):
                v.reciprocal(STT[:, 15:16], SQS[:, 2:3])
                v.drain()
                v.tensor_scalar(HE[:, 0:D], src, STT[:, 12:13], STT[:, 15:16],
                                A.subtract, A.mult).then_inc(sv.h, 1)
            p.op("vector", f3)
            sv.val += 1
            p.wait("tensor", sv, sv.val)
            if it >= 2:
                p.wait("tensor", ssc, ssc.val - 1)

            def ftr(t, it=it):
                ptr = PTRA if it % 2 == 0 else PTRB
                for c in range(8):
                    mmi = t.transpose(ptr[:, c * 128:(c + 1) * 128],
                                      HE[:, c * 128:(c + 1) * 128], IDB[:])
                    if c == 7:
                        mmi.then_inc(smm.h, 1)
            p.op("tensor", ftr)
            smm.val += 1
            tr_done[it] = smm.val
            p.wait("scalar", smm, smm.val)

            def fcp(s, it=it):
                ptr = PTRA if it % 2 == 0 else PTRB
                for c in range(8):
                    s.copy(LNT[:, c * T + it * 128: c * T + it * 128 + 128],
                           ptr[:, c * 128:(c + 1) * 128])
                s.copy(SQS[0:1, 3:4], CST[0:1, 3:4]).then_inc(ssc.h, 1)
            p.op("scalar", fcp)
            ssc.val += 1
        ln0_done_sc = ssc.val
        tap("lnt", LNT[:], ssc, ln0_done_sc)
        tap("he", HE[:, 0:D], ssc, ln0_done_sc)
        tap("stt", STT[:], ssc, ln0_done_sc)

        if STOP <= 1:
            _finish(p, sst, out_d, ACCH)
            return _close(block, p, nc)
        p.wait("tensor", ssc, ln0_done_sc)
        dp_evac = {}
        for fc in range(2):
            for jt in range(4):
                gi = fc * 4 + jt
                bk = gi % 6
                if gi >= 6:
                    p.wait("tensor", ssc, dp_evac[gi - 6])

                def fdp(t, fc=fc, jt=jt, bk=bk):
                    for dc in range(8):
                        mmi = t.matmul(
                            bank(bk),
                            DW[:, dc * dA + fc * 128: dc * dA + fc * 128 + 128],
                            LNT[:, dc * T + jt * 512: dc * T + jt * 512 + 512],
                            start=(dc == 0), stop=(dc == 7))
                        if dc == 7:
                            mmi.then_inc(smm.h, 1)
                p.op("tensor", fdp)
                smm.val += 1
                p.wait("scalar", smm, smm.val)

                def fsl(s, fc=fc, jt=jt, bk=bk):
                    s.activation(HT[:, fc * T + jt * 512: fc * T + jt * 512 + 512],
                                 bank(bk), AF.Silu, bias=DB[:, fc:fc + 1],
                                 scale=1.0).then_inc(ssc.h, 1)
                p.op("scalar", fsl)
                ssc.val += 1
                dp_evac[gi] = ssc.val
        ht_done = ssc.val
        tap("ht", HT, ssc, ht_done)

        if STOP <= 2:
            _finish(p, sst, out_d, ACCH)
            return _close(block, p, nc)
        p.wait("tensor", ssc, ht_done)
        for j in range(4):
            def flr(t, j=j):
                for dc in range(2):
                    mmi = t.matmul(bank(j)[0:2, 0:512],
                                   LRWT[:, dc * 2: dc * 2 + 2],
                                   HT[:, dc * T + j * 512: dc * T + j * 512 + 512],
                                   start=(dc == 0), stop=(dc == 1))
                    if dc == 1:
                        mmi.then_inc(smm.h, 1)
            p.op("tensor", flr)
            smm.val += 1
            p.wait("scalar", smm, smm.val)

            def fsg(s, j=j):
                s.activation(SIGLR[:, j * 512:(j + 1) * 512], bank(j)[0:2, 0:512],
                             AF.Sigmoid, bias=LRB[:], scale=1.0).then_inc(ssc.h, 1)
            p.op("scalar", fsg)
            ssc.val += 1
        siglr_done = ssc.val
        tap("siglr", SIGLR[:], ssc, siglr_done)

        p.wait("sync", ssc, siglr_done)
        dma("sync", ld, scr[:], SIGLR[:])
        p.wait("sync", ld, ld.val)
        for h in range(H):
            dma("sync", ld, SIGT[:, h * NT:(h + 1) * NT],
                scr[h * T:(h + 1) * T].rearrange("(w q) -> q w", q=128),
                allow_slow_non_contiguous=True)
            dma("sync", ld, LRC[:, h * NSTEP:(h + 1) * NSTEP],
                scr[h * T:(h + 1) * T].rearrange("(n k) -> k n", k=K),
                allow_slow_non_contiguous=True)
        sig_resh = ld.val
        tap("sigt", SIGT[:], ld, sig_resh)
        tap("lrc", LRC[:], ld, sig_resh)

        p.wait("vector", ld, sig_resh)
        p.wait("vector", sv, ti15_done)

        def fle(v):
            v.tensor_scalar(LETOK[:], SIGT[:], TI15b[:], 1.0 / 128.0, A.mult, A.mult)
            v.tensor_scalar(LECB[:], LRC[:], TI15[:], 1.0 / 128.0, A.mult,
                            A.mult).then_inc(sv.h, 1)
        p.op("vector", fle)
        sv.val += 1
        letok_done = sv.val

        if STOP <= 3:
            _finish(p, sst, out_d, ACCH)
            return _close(block, p, nc)
        p.wait("tensor", ssc, siglr_done)
        pj_evac = {}
        for wi_, (wsrc, dstt) in enumerate(((WQ, XQT), (WK, XKT))):
            for fc in range(2):
                for jt in range(4):
                    gi = wi_ * 8 + fc * 4 + jt
                    bk = gi % 6
                    if gi >= 6:
                        p.wait("tensor", sv, pj_evac[gi - 6])

                    def fpj(t, wsrc=wsrc, fc=fc, jt=jt, bk=bk):
                        for dc in range(2):
                            mmi = t.matmul(
                                bank(bk),
                                wsrc[:, dc * dA + fc * 128: dc * dA + fc * 128 + 128],
                                HT[:, dc * T + jt * 512: dc * T + jt * 512 + 512],
                                start=(dc == 0), stop=(dc == 1))
                            if dc == 1:
                                mmi.then_inc(smm.h, 1)
                    p.op("tensor", fpj)
                    smm.val += 1
                    p.wait("vector", smm, smm.val)

                    def fev(v, dstt=dstt, fc=fc, jt=jt, bk=bk):
                        v.tensor_copy(dstt[:, fc * T + jt * 512: fc * T + jt * 512 + 512],
                                      bank(bk)).then_inc(sv.h, 1)
                    p.op("vector", fev)
                    sv.val += 1
                    pj_evac[gi] = sv.val
        proj_fmaj_done = sv.val
        tap("xqt", XQT, sv, proj_fmaj_done)
        tap("xkt", XKT, sv, proj_fmaj_done)

        if STOP <= 4:
            _finish(p, sst, out_d, ACCH)
            return _close(block, p, nc)
        p.wait("vector", sv, letok_done)
        p.wait("tensor", sv, proj_fmaj_done)
        tokmaj_evac = {}
        for it in range(NT):
            bkk = it % 2
            bkv = 2 + it % 2
            if it >= 2:
                p.wait("tensor", sv, tokmaj_evac[it - 2])

            def fkv(t, it=it, bkk=bkk, bkv=bkv):
                for dc in range(2):
                    t.matmul(bank(bkk)[:, 0:256],
                             HT[:, dc * T + it * 128: dc * T + it * 128 + 128],
                             WK[:, dc * dA: dc * dA + dA],
                             start=(dc == 0), stop=(dc == 1))
                for dc in range(2):
                    mmi = t.matmul(bank(bkv)[:, 0:256],
                                   HT[:, dc * T + it * 128: dc * T + it * 128 + 128],
                                   WV[:, dc * dA: dc * dA + dA],
                                   start=(dc == 0), stop=(dc == 1))
                    if dc == 1:
                        mmi.then_inc(smm.h, 1)
            p.op("tensor", fkv)
            smm.val += 1
            p.wait("scalar", smm, smm.val)

            def fxk(s, bkk=bkk):
                s.copy(XKtmp[:], bank(bkk)[:, 0:256]).then_inc(ssc.h, 1)
            p.op("scalar", fxk)
            ssc.val += 1
            p.wait("vector", ssc, ssc.val)

            def fxv(v, it=it, bkv=bkv):
                for h in range(H):
                    v.tensor_scalar(XKLE[:, it * dA + h * FH: it * dA + h * FH + FH],
                                    XKtmp[:, h * FH:(h + 1) * FH],
                                    LETOK[:, h * NT + it: h * NT + it + 1], None,
                                    A.mult)
                v.tensor_tensor(TME[:], XKtmp[:], bank(bkv)[:, 0:256], A.subtract)
                v.tensor_tensor(TME[:], TME[:], BB[:], A.add)
                v.tensor_tensor(TGT2[:, it * dA:(it + 1) * dA], TME[:], GB[:],
                                A.mult).then_inc(sv.h, 1)
            p.op("vector", fxv)
            sv.val += 1
            tokmaj_evac[it] = sv.val
        tokmaj_done = sv.val
        tap("xkle", XKLE, sv, tokmaj_done)
        tap("xktmp", XKtmp[:], sv, tokmaj_done)
        tap("tgt2", TGT2[:], sv, tokmaj_done)

        if STOP <= 5:
            _finish(p, sst, out_d, ACCH)
            return _close(block, p, nc)
        p.wait("tensor", sv, tokmaj_done)
        p.wait("tensor", sv, proj_fmaj_done)

        zbar_done = {}
        cast_done = {}
        grad_ready = {}
        upd_done = {}
        mm_all = {}

        tgt_pf = {}
        xkle_pf = {}
        stash_ld = {}
        NS_ = NSTEP if DEV_SCAN_STEPS is None else DEV_SCAN_STEPS
        for n in range(NS_):
            q = n % 2
            bA, bB, bC = bank(q), bank(2 + q), bank(4 + q)
            itile, r0 = n // 8, (n % 8) * 16
            slot = n % 4

            p.wait("sync", sv, tokmaj_evac[n // 8])
            if n >= 4:
                p.wait("sync", sv, grad_ready[n - 4])
                p.wait("sync", smm, mm_all[n - 4])
            dma("sync", ld, TGT16R[:, slot * dA:(slot + 1) * dA],
                TGT2[r0:r0 + 16, itile * dA:(itile + 1) * dA])
            tgt_pf[n] = ld.val
            dma("sync", ld, XKLE16R[:, slot * dA:(slot + 1) * dA],
                XKLE[r0:r0 + 16, itile * dA:(itile + 1) * dA])
            xkle_pf[n] = ld.val

            if n >= 1:
                p.wait("tensor", ssc, cast_done[n - 1])
            if n >= 2:
                p.wait("tensor", sv, upd_done[n - 2])
                p.wait("tensor", ssc, zbar_done[n - 2])
            p.wait("tensor", ld, xkle_pf[n])

            def f_z1(t, n=n, bA=bA):
                for h in range(H):
                    t.matmul(bA[0:16, h * FH:(h + 1) * FH],
                             XKT[:, h * T + n * K: h * T + n * K + K],
                             W1b[:, h * FH:(h + 1) * FH], start=True, stop=False)
                    t.matmul(bA[0:16, h * FH:(h + 1) * FH], ONES16f[:],
                             B1f[0:1, h * FH:(h + 1) * FH], start=False, stop=True)
                for h in range(H):
                    t.matmul(bA[32:48, h * K:(h + 1) * K],
                             XKT[:, h * T + n * K: h * T + n * K + K],
                             XQT[:, h * T + n * K: h * T + n * K + K],
                             start=True, stop=False)
                    mmi = t.matmul(bA[32:48, h * K:(h + 1) * K],
                                   ONESCb[0:1, 0:16], ONESCb[0:1, 0:16],
                                   start=False, stop=True)
                    if h == 1:
                        mmi.then_inc(smm.h, 1)
            p.op("tensor", f_z1)
            smm.val += 1

            mmz = smm.val
            p.wait("vector", smm, mmz)
            p.wait("scalar", smm, mmz)

            def f_sums(s, bA=bA):
                for h in range(H):
                    s.activation(SCR_SC, bA[0:16, h * FH:(h + 1) * FH], AF.Identity,
                                 bias=0.0, scale=1.0, accum_out=SQS[0:16, h:h + 1])
                for h in range(H):
                    s.activation(SCR_SC, bA[0:16, h * FH:(h + 1) * FH], AF.Square,
                                 bias=0.0, scale=1.0,
                                 accum_out=SQS[0:16, 2 + h:3 + h])
                s.drain()
                s.mul(BN6[0:16, 0:2], SQS[0:16, 0:2], 1.0 / 128.0)
                s.activation(BN6[0:16, 2:4], SQS[0:16, 0:2], AF.Square,
                             bias=0.0, scale=1.0 / 128.0)
                s.drain()
                s.activation(BN6[0:16, 6:8], BN6[0:16, 2:4], AF.Identity,
                             bias=1e-6, scale=-1.0)
                s.drain()
                for h in range(H):
                    s.activation(SQS[0:16, 2 + h:3 + h], SQS[0:16, 2 + h:3 + h],
                                 AF.Sqrt, bias=BN6[0:16, 6 + h:7 + h],
                                 scale=1.0 / 128.0)
                s.drain()
                s.copy(BN6[0:1, 8:9], CST[0:1, 3:4]).then_inc(ssc.h, 1)
            p.op("scalar", f_sums)
            ssc.val += 1
            p.wait("vector", ssc, ssc.val)
            p.wait("vector", ld, tgt_pf[n])

            def f_grad(v, n=n, bA=bA, slot=slot):
                v.reciprocal(RSTD[:], SQS[0:16, 2:4])
                v.tensor_tensor(CT1[:, 0:K], bA[32:48, 0:K], TOKLN[:], A.mult)
                v.tensor_tensor(CT1[:, K:2 * K], bA[32:48, K:2 * K], TOKLN[:],
                                A.mult)
                v.drain()
                for h in range(H):
                    v.tensor_scalar(XHs[:, h * FH:(h + 1) * FH],
                                    bA[0:16, h * FH:(h + 1) * FH],
                                    BN6[0:16, h: h + 1], RSTD[:, h: h + 1],
                                    A.subtract, A.mult)
                for h in range(H):
                    v.tensor_scalar(CTB[:, h * K:(h + 1) * K],
                                    CT1[:, h * K:(h + 1) * K],
                                    LRC[:, h * NSTEP + n: h * NSTEP + n + 1], None,
                                    A.mult)
                v.tensor_tensor(GYs[:].bitcast(BF16)[0:16, 0:256], XHs[:],
                                G2B[0:16, :], A.mult)
                v.tensor_tensor(GYs[:].bitcast(BF16)[0:16, 0:256],
                                GYs[:].bitcast(BF16)[0:16, 0:256],
                                TGT16R[:, slot * dA:(slot + 1) * dA], A.add)
                v.tensor_tensor(PRD[:].bitcast(BF16)[0:16, 0:256],
                                GYs[:].bitcast(BF16)[0:16, 0:256], XHs[:], A.mult)
                for h in range(H):
                    v.tensor_reduce(M1N[:, h:h + 1],
                                    PRD[:].bitcast(BF16)[0:16, h * FH:(h + 1) * FH],
                                    AX.X, A.add, negate=True)
                v.drain()
                v.tensor_scalar(M1N[:], M1N[:], 1.0 / 128.0, None, A.mult)
                for h in range(H):
                    v.tensor_scalar(PRD[:, h * FH:(h + 1) * FH],
                                    GYs[:].bitcast(BF16)[0:16, h * FH:(h + 1) * FH],
                                    1.0 / 128.0, 0.0, A.mult, A.add,
                                    accum_out=M2[:, h:h + 1])
                v.drain()
                for h in range(H):
                    v.scalar_tensor_tensor(GRF[:, h * FH:(h + 1) * FH],
                                           XHs[:, h * FH:(h + 1) * FH],
                                           M1N[:, h:h + 1],
                                           GYs[:].bitcast(BF16)[0:16, h * FH:(h + 1) * FH],
                                           A.mult, A.add)
                for h in range(H):
                    mmo = v.tensor_scalar(GRB[:, h * FH:(h + 1) * FH],
                                          GRF[:, h * FH:(h + 1) * FH],
                                          M2[:, h:h + 1], RSTD[:, h:h + 1],
                                          A.subtract, A.mult)
                    if h == 1:
                        mmo.then_inc(sv.h, 1)
            p.op("vector", f_grad)
            sv.val += 1
            grad_ready[n] = sv.val

            p.wait("tensor", sv, grad_ready[n])

            def f_zb(t, n=n, bA=bA, bB=bB, bC=bC, slot=slot):
                for h in range(H):
                    t.matmul(bB[0:16, h * FH:(h + 1) * FH],
                             XQT[:, h * T + n * K: h * T + n * K + K],
                             W1b[:, h * FH:(h + 1) * FH], start=True, stop=False)
                    t.matmul(bB[0:16, h * FH:(h + 1) * FH], ONES16f[:],
                             B1f[0:1, h * FH:(h + 1) * FH], start=False, stop=False)
                    t.matmul(bB[0:16, h * FH:(h + 1) * FH],
                             CTB[:, h * K:(h + 1) * K],
                             GRB[:, h * FH:(h + 1) * FH], start=False, stop=True)
                for h in range(H):
                    t.matmul(bC[0:128, h * FH:(h + 1) * FH],
                             XKLE16R[:, slot * dA + h * FH: slot * dA + (h + 1) * FH],
                             GRB[:, h * FH:(h + 1) * FH], start=True, stop=True)
                for h in range(H):
                    mmi = t.matmul(bA[64:65, 256 + h * FH: 256 + (h + 1) * FH],
                                   LECB[:, h * NSTEP + n: h * NSTEP + n + 1],
                                   GRB[:, h * FH:(h + 1) * FH], start=True, stop=True)
                    if h == 1:
                        mmi.then_inc(smm.h, 1)
            p.op("tensor", f_zb)
            smm.val += 1
            mm_all[n] = smm.val

            p.wait("scalar", smm, mm_all[n])
            if n >= 2:
                p.wait("scalar", ld, stash_ld[n - 2])

            def f_zc(s, bB=bB, n=n):
                s.copy(ZB16[:, (n % 2) * dA:(n % 2 + 1) * dA],
                       bB[0:16, 0:256]).then_inc(ssc.h, 1)
            p.op("scalar", f_zc)
            ssc.val += 1
            zbar_done[n] = ssc.val
            p.wait("sync", ssc, zbar_done[n])
            dma("sync", ld, TGT2[r0:r0 + 16, itile * dA:(itile + 1) * dA],
                ZB16[:, (n % 2) * dA:(n % 2 + 1) * dA])
            stash_ld[n] = ld.val

            p.wait("vector", smm, mm_all[n])

            def f_upd(v, bA=bA, bC=bC):
                v.tensor_tensor(W1f[:], W1f[:], bC[0:128, 0:256], A.subtract)
                v.tensor_tensor(B1f[:], B1f[:], bA[64:65, 256:512],
                                A.subtract).then_inc(sv.h, 1)
            p.op("vector", f_upd)
            sv.val += 1
            upd_done[n] = sv.val

            p.wait("scalar", sv, upd_done[n])

            def f_cast(s):
                s.copy(W1b[:], W1f[:]).then_inc(ssc.h, 1)
            p.op("scalar", f_cast)
            ssc.val += 1
            cast_done[n] = ssc.val

        scan_v, scan_sc = sv.val, ssc.val
        p.wait("gpsimd", ld, stash_ld[NS_ - 1])
        tap("zbar", TGT2[:], ssc, scan_sc)
        tap("w1f", W1f[:], ssc, scan_sc)
        tap("mv", MV[:], ssc, scan_sc)
        tap("rstd", RSTD[:], ssc, scan_sc)
        tap("grb", GRB, ssc, scan_sc)
        tap("xhs", XHs, ssc, scan_sc)
        tap("gys", GYs, ssc, scan_sc)
        tap("ctb", CTB[:], ssc, scan_sc)
        tap("m1n", M1N[:], ssc, scan_sc)
        tap("m2", M2[:], ssc, scan_sc)
        tap("std", STD[:], ssc, scan_sc)
        tap("sqs", SQS[:], ssc, scan_sc)
        tap("b1f", B1f[:], ssc, scan_sc)

        if STOP <= 6:
            _finish(p, sst, out_d, ACCH)
            return _close(block, p, nc)
        for eng in ("tensor", "vector", "scalar"):
            p.wait(eng, ssc, ssc.val)
            p.wait(eng, sv, sv.val)
            p.wait(eng, smm, smm.val)
            p.wait(eng, ld, ld.val)
            p.wait(eng, wt2, wt2.val)
        for it, dummy in [(0, True)] + [(i, False) for i in range(NT)]:
            b0, b1_, b3 = 0, 0, 1
            p.wait("tensor", sv, sv.val)

            def fq(t, it=it, b0=b0):
                for dc in range(2):
                    mmi = t.matmul(bank(b0)[:, 0:256],
                                   HT[:, dc * T + it * 128: dc * T + it * 128 + 128],
                                   WQ[:, dc * dA: dc * dA + dA],
                                   start=(dc == 0), stop=(dc == 1))
                    if dc == 1:
                        mmi.then_inc(smm.h, 1)
            p.op("tensor", fq)
            smm.val += 1

            zsrc = TGT2[:, it * dA:(it + 1) * dA]

            def fz1(v, zsrc=zsrc):
                v.tensor_reduce(STT[:, 0:1], zsrc[:, 0:128], AX.X, A.add)
                v.tensor_reduce(STT[:, 1:2], zsrc[:, 128:256], AX.X, A.add)
                v.tensor_tensor(TME2[:], zsrc, zsrc, A.mult)
                v.tensor_reduce(STT[:, 2:3], TME2[:, 0:128], AX.X, A.add)
                v.tensor_reduce(STT[:, 3:4], TME2[:, 128:256], AX.X, A.add)
                v.tensor_scalar(STT[:, 12:14], STT[:, 0:2], 1.0 / 128.0, None,
                                A.mult)
                v.tensor_scalar(STT[:, 2:4], STT[:, 2:4], 1.0 / 128.0, None,
                                A.mult)
                v.tensor_tensor(STT[:, 4:6], STT[:, 12:14], STT[:, 12:14], A.mult)
                v.tensor_tensor(STT[:, 2:4], STT[:, 2:4], STT[:, 4:6],
                                A.subtract).then_inc(sv.h, 1)
            p.op("vector", fz1)
            sv.val += 1
            p.wait("scalar", sv, sv.val)

            def fz2(s):
                s.activation(SQS[:, 2:4], STT[:, 2:4], AF.Sqrt,
                             bias=CST[:, 0:1], scale=1.0).then_inc(ssc.h, 1)
            p.op("scalar", fz2)
            ssc.val += 1
            p.wait("vector", ssc, ssc.val)
            p.wait("vector", smm, smm.val)
            def fz3(v, zsrc=zsrc, b0=b0):
                v.reciprocal(STT[:, 16:18], SQS[:, 2:4])
                v.drain()
                v.tensor_scalar(TME[:, 0:128], zsrc[:, 0:128], STT[:, 12:13],
                                STT[:, 16:17], A.subtract, A.mult)
                v.tensor_scalar(TME[:, 128:256], zsrc[:, 128:256], STT[:, 13:14],
                                STT[:, 17:18], A.subtract, A.mult)
                v.tensor_tensor(TME[:], TME[:], GB[:], A.mult)
                v.tensor_tensor(TME[:], TME[:], BB[:], A.add)
                v.tensor_tensor(TME[:], TME[:], bank(b0)[:, 0:256], A.add
                                ).then_inc(sv.h, 1)
            p.op("vector", fz3)
            sv.val += 1

            ln_stats(TME[:], dA, 1, TME2[:])
            def fp3(v):
                v.reciprocal(STT[:, 15:16], SQS[:, 2:3])
                v.drain()
                v.tensor_scalar(TME2[:], TME[:], STT[:, 12:13], STT[:, 15:16],
                                A.subtract, A.mult)
                v.tensor_tensor(TME2[:], TME2[:], PWB[:], A.mult)
                v.tensor_tensor(TMB[:], TME2[:], PBB[:], A.add).then_inc(sv.h, 1)
            p.op("vector", fp3)
            sv.val += 1
            p.wait("tensor", sv, sv.val)

            def ftw(t):
                t.transpose(PTRA[:, 0:128], TMB[:, 0:128], IDB[:])
                t.transpose(PTRA[:, 128:256], TMB[:, 128:256], IDB[:]
                            ).then_inc(smm.h, 1)
            p.op("tensor", ftw)
            smm.val += 1
            p.wait("scalar", smm, smm.val)
            def fpc(s):
                s.copy(PTT[:, 0:128], PTRA[:, 0:128])
                s.copy(PTT[:, 128:256], PTRA[:, 128:256]).then_inc(ssc.h, 1)
            p.op("scalar", fpc)
            ssc.val += 1
            p.wait("tensor", ssc, ssc.val)
            def fwo(t):
                for dc in range(2):
                    mmi = t.matmul(bank(0)[:, 256:512], PTT[:, dc * 128:(dc + 1) * 128],
                                   WO[:, dc * dA: dc * dA + dA],
                                   start=(dc == 0), stop=(dc == 1))
                    if dc == 1:
                        mmi.then_inc(smm.h, 1)
            p.op("tensor", fwo)
            smm.val += 1
            p.wait("vector", smm, smm.val)

            ln_stats(bank(0)[:, 256:512], dA, 1, TME2[:])
            def fl3(v):
                v.reciprocal(STT[:, 15:16], SQS[:, 2:3])
                v.drain()
                v.tensor_scalar(TME2[:], bank(0)[:, 256:512], STT[:, 12:13],
                                STT[:, 15:16], A.subtract, A.mult)
                v.tensor_tensor(TME2[:], TME2[:], LWB[:], A.mult)
                v.tensor_tensor(TMB[:], TME2[:], LBB[:], A.add).then_inc(sv.h, 1)
            p.op("vector", fl3)
            sv.val += 1
            p.wait("tensor", sv, sv.val)

            def ftu(t):
                t.transpose(PTRB[:, 0:128], TMB[:, 0:128], IDB[:])
                t.transpose(PTRB[:, 128:256], TMB[:, 128:256], IDB[:]
                            ).then_inc(smm.h, 1)
            p.op("tensor", ftu)
            smm.val += 1
            p.wait("scalar", smm, smm.val)
            def fpc2(s):
                s.copy(PTT[:, 0:128], PTRB[:, 0:128])
                s.copy(PTT[:, 128:256], PTRB[:, 128:256]).then_inc(ssc.h, 1)
            p.op("scalar", fpc2)
            ssc.val += 1
            p.wait("tensor", ssc, ssc.val)
            upps = PBIG1[:, 0:1024]
            def fup(t, upps=upps):
                for j in range(2):
                    for dc in range(2):
                        t.matmul(upps[:, j * 512:(j + 1) * 512],
                                 PTT[:, dc * 128:(dc + 1) * 128],
                                 UW[:, dc * D + j * 512: dc * D + j * 512 + 512],
                                 start=(dc == 0), stop=False)
                    mmi = t.matmul(upps[:, j * 512:(j + 1) * 512], ONESCb[:],
                                   UPB[0:1, j * 512:(j + 1) * 512],
                                   start=False, stop=True)
                    if j == 1:
                        mmi.then_inc(smm.h, 1)
            p.op("tensor", fup)
            smm.val += 1
            p.wait("vector", smm, smm.val)

            ln_stats(upps, D, 1, TMPA[:])
            xres_t = XRES[:, it * D:(it + 1) * D]
            def fx2(v, upps=upps, xres_t=xres_t, dummy=dummy):
                v.reciprocal(STT[:, 15:16], SQS[:, 2:3])
                v.drain()
                v.tensor_scalar(TMPA[:], upps, STT[:, 12:13], STT[:, 15:16],
                                A.subtract, A.mult)
                v.tensor_tensor(TMPD[:], xres_t, TMPA[:], A.add)
                if dummy:
                    v.tensor_copy(STT[:, 9:10], TMPD[:, 0:1]).then_inc(sv.h, 1)
                else:
                    v.tensor_copy(xres_t, TMPD[:]).then_inc(sv.h, 1)
            p.op("vector", fx2)
            sv.val += 1

            ln_stats(TMPD[:], D, 1, TMPA[:])
            def fl33(v):
                v.reciprocal(STT[:, 15:16], SQS[:, 2:3])
                v.drain()
                v.tensor_scalar(TMPA[:], TMPD[:], STT[:, 12:13], STT[:, 15:16],
                                A.subtract, A.mult).then_inc(sv.h, 1)
            p.op("vector", fl33)
            sv.val += 1
            p.wait("tensor", sv, sv.val)
            tb = PBIG0[:, 1024:2048]
            def ftr3(t, tb=tb):
                for c in range(8):
                    mmi = t.transpose(tb[:, c * 128:(c + 1) * 128],
                                      TMPA[:, c * 128:(c + 1) * 128], IDF[:])
                    if c == 7:
                        mmi.then_inc(smm.h, 1)
            p.op("tensor", ftr3)
            smm.val += 1
            p.wait("scalar", smm, smm.val)
            def fcp3(s, tb=tb):
                for c in range(8):
                    s.copy(TMPD[:, c * 128:(c + 1) * 128], tb[:, c * 128:(c + 1) * 128])
                s.copy(SQS[0:1, 3:4], CST[0:1, 3:4]).then_inc(ssc.h, 1)
            p.op("scalar", fcp3)
            ssc.val += 1
            p.wait("vector", ssc, ssc.val)
            p.wait("tensor", ssc, ssc.val)
            def fcst(v, it=it):
                v.tensor_copy(LNT[:].rearrange("p (c t) -> p c t", c=8)
                              [:, :, it * 128:(it + 1) * 128],
                              TMPD[:].rearrange("p (c t) -> p c t", c=8)
                              ).then_inc(sv.h, 1)
            p.op("vector", fcst)
            sv.val += 1
            gps = bank(2)
            def fgm(t, gps=gps):
                for dc in range(8):
                    mmi = t.matmul(gps[0:128, 0:16], TMPD[:, dc * 128:(dc + 1) * 128],
                                   GWT[:, dc * NE:(dc + 1) * NE],
                                   start=(dc == 0), stop=(dc == 7))
                    if dc == 7:
                        mmi.then_inc(smm.h, 1)
            p.op("tensor", fgm)
            smm.val += 1
            p.wait("scalar", smm, smm.val)
            def fsig(s, it=it, gps=gps):
                s.copy(GVAL[:, it * NE:(it + 1) * NE], gps[0:128, 0:16]
                       ).then_inc(ssc.h, 1)
            p.op("scalar", fsig)
            ssc.val += 1

        p.wait("scalar", ssc, ssc.val)

        def fsigall(s):
            s.activation(GVAL[:], GVAL[:], AF.Sigmoid, bias=0.0, scale=1.0
                         ).then_inc(ssc.h, 1)
        p.op("scalar", fsigall)
        ssc.val += 1
        p.wait("vector", ssc, ssc.val)
        for it in range(NT):
            def ftk(v, it=it):
                g0 = GVAL[:, it * NE:(it + 1) * NE]
                v.tensor_copy(GWRK[:], g0)
                v.drain()
                for _ in range(4):
                    v.reduce_max(GMAX[:], GWRK[:], AX.X)
                    v.drain()
                    v.tensor_scalar(GEQ[:], GWRK[:], GMAX[:], None, A.is_equal)
                    v.drain()
                    v.scalar_tensor_tensor(GWRK[:], GEQ[:], -1e30, GWRK[:],
                                           A.mult, A.add)
                    v.drain()
                v.tensor_tensor(COMB[:], GWRK[:], g0, A.not_equal)
                v.drain()
                v.tensor_tensor(COMB[:], COMB[:], g0, A.mult)
                v.drain()
                v.tensor_reduce(RS[:], COMB[:], AX.X, A.add)
                v.drain()
                v.reciprocal(RS[:], RS[:])
                v.drain()
                v.tensor_scalar(COMB[:], COMB[:], RS[:], None, A.mult
                                ).then_inc(sv.h, 1)
            p.op("vector", ftk)
            sv.val += 1
            p.wait("tensor", sv, sv.val)
            if it >= 1:
                p.wait("tensor", sv, sv.val)

            def ftc(t):
                t.transpose(bank(2)[0:16, 128:256], COMB[:], IDF[:]).then_inc(smm.h, 1)
            p.op("tensor", ftc)
            smm.val += 1
            p.wait("vector", smm, smm.val)

            def fcc(v, it=it):
                v.tensor_copy(COMBT[:, it * 128:(it + 1) * 128],
                              bank(2)[0:16, 128:256]).then_inc(sv.h, 1)
            p.op("vector", fcc)
            sv.val += 1
        post_v, post_sc = sv.val, ssc.val
        tap("x2", XRES[:], sv, post_v)
        tap("tme", TME[:], sv, post_v)
        tap("tmpa", TMPA[:], sv, post_v)
        tap("tmpd", TMPD[:], sv, post_v)
        tap("ptt", PTT[:], sv, post_v)
        tap("lnt3", LNT[:], sv, post_v)
        tap("gval", GVAL[:], sv, post_v)
        tap("combt", COMBT[:], sv, post_v)

        p.wait("sync", sv, post_v)
        dma("sync", ld, scr2[:], XRES[:])
        x2_spill = ld.val

        if STOP <= 7:
            _finish(p, sst, out_d, ACCH)
            return _close(block, p, nc)
        p.wait("gpsimd", sv, post_v)
        dma("gpsimd", wt2, WSIN, wsin_d[:])
        dma("gpsimd", wt2, VSIN, vsin_d[:])
        dma("gpsimd", wt2, WSOUT, wsout_d[:])
        JOBS = []
        for pa in range(2):
            JOBS.append((pa, "sh", 0))
            JOBS.append((pa, "sh", 1))
            for e in range(NE):
                JOBS.append((pa, "e", e))
        ejobs = [ji for ji, j in enumerate(JOBS) if j[1] == "e"]
        eidx = {ji: k for k, ji in enumerate(ejobs)}
        winv_ready = {}
        wout_ready = {}
        combrow_ready = {}
        in_done = {}
        out_done = {}
        for k in range(2):
            e = JOBS[ejobs[k]][2]
            dma("gpsimd", wt2, WINb[k % 2], ewin_d[e])
            dma("gpsimd", wt2, VINb[k % 2], evin_d[e])
            winv_ready[ejobs[k]] = wt2.val
        wt_shared_all = wt2.val

        last_facc = {}
        final_last_v = 0
        out_stores = 0
        p.wait("vector", ld, x2_spill)

        for ji, (pa, kind, e) in enumerate(JOBS):
            t0 = pa * TP
            first = (kind == "sh" and e == 0)
            hc0 = 4 * e if kind == "sh" else 0

            if ji == 0:
                p.wait("tensor", wt2, wt_shared_all)
            if pa == 1 and kind == "sh" and e == 0:
                p.wait("tensor", sv, final_last_v)
            if ji > 0:
                p.wait("tensor", sv, last_facc[ji - 1])
            if kind == "e":
                p.wait("tensor", wt2, winv_ready[ji])
                win_s, vin_s = WINb[eidx[ji] % 2], VINb[eidx[ji] % 2]
            else:
                win_s, vin_s = WSIN, VSIN

            if kind == "e":
                k_ = eidx[ji]
                if k_ not in combrow_ready:
                    p.wait("sync", smm, smm.val)
                    dma("sync", ld, COMBROW[0:1, (k_ % 2) * TP:(k_ % 2 + 1) * TP],
                        COMBT[e:e + 1, t0: t0 + TP])
                    combrow_ready[k_] = ld.val
                p.wait("tensor", ld, combrow_ready[k_])
                cslot = (k_ % 2) * TP

                def fcb(t, cslot=cslot):
                    t.matmul(bank(2)[:, 0:512], ONESCb[:],
                             COMBROW[0:1, cslot:cslot + 512], start=True, stop=True)
                    t.matmul(bank(3)[:, 0:512], ONESCb[:],
                             COMBROW[0:1, cslot + 512:cslot + 1024], start=True,
                             stop=True).then_inc(smm.h, 1)
                p.op("tensor", fcb)
                smm.val += 1
                p.wait("vector", smm, smm.val)

                def fcbc(v):
                    v.tensor_copy(CBX[:, 0:512], bank(2)[:, 0:512])
                    v.tensor_copy(CBX[:, 512:1024], bank(3)[:, 0:512]
                                  ).then_inc(sv.h, 1)
                p.op("vector", fcbc)
                sv.val += 1
                p.wait("tensor", sv, sv.val)
                if k_ + 1 < len(ejobs):
                    jn = ejobs[k_ + 1]
                    pa_n, _, e_n = JOBS[jn]
                    p.wait("sync", smm, smm.val)
                    dma("sync", ld,
                        COMBROW[0:1, ((k_ + 1) % 2) * TP:((k_ + 1) % 2 + 1) * TP],
                        COMBT[e_n:e_n + 1, pa_n * TP: pa_n * TP + TP])
                    combrow_ready[k_ + 1] = ld.val

            evac_vals = {}
            for hc in range(4):
                hcw = hc0 + hc
                hb = PBIG0[:, (hc % 2) * 1024:(hc % 2) * 1024 + 1024]
                vb = PBIG1[:, 0:1024]

                if hc >= 2:
                    p.wait("tensor", sv, evac_vals[hc - 2])

                def fiph(t, win_s=win_s, hcw=hcw, hb=hb, t0=t0, kind=kind):
                    for dc in range(8):
                        if kind == "sh":
                            lw_ = win_s[:, dc * SHID + hcw * 128:
                                        dc * SHID + hcw * 128 + 128]
                        else:
                            lw_ = win_s[:, dc * EH + hcw * 128:
                                        dc * EH + hcw * 128 + 128]
                        for j in range(2):
                            mmi = t.matmul(hb[:, j * 512:(j + 1) * 512], lw_,
                                           LNT[:, dc * T + t0 + j * 512:
                                               dc * T + t0 + j * 512 + 512],
                                           start=(dc == 0), stop=(dc == 7))
                            if dc == 7 and j == 1:
                                mmi.then_inc(smm.h, 1)
                p.op("tensor", fiph)
                smm.val += 1
                hin_mm = smm.val

                if hc >= 1:
                    p.wait("tensor", sv, evac_vals[hc - 1])

                def fipv(t, vin_s=vin_s, hcw=hcw, vb=vb, t0=t0, kind=kind):
                    for dc in range(8):
                        if kind == "sh":
                            lv_ = vin_s[:, dc * SHID + hcw * 128:
                                        dc * SHID + hcw * 128 + 128]
                        else:
                            lv_ = vin_s[:, dc * EH + hcw * 128:
                                        dc * EH + hcw * 128 + 128]
                        for j in range(2):
                            mmi = t.matmul(vb[:, j * 512:(j + 1) * 512], lv_,
                                           LNT[:, dc * T + t0 + j * 512:
                                               dc * T + t0 + j * 512 + 512],
                                           start=(dc == 0), stop=(dc == 7))
                            if dc == 7 and j == 1:
                                mmi.then_inc(smm.h, 1)
                p.op("tensor", fipv)
                smm.val += 1
                p.wait("scalar", smm, hin_mm)

                def fsi(s, hb=hb):
                    s.activation(MTMP[:], hb, AF.Silu, bias=0.0, scale=1.0
                                 ).then_inc(ssc.h, 1)
                p.op("scalar", fsi)
                ssc.val += 1
                p.wait("vector", ssc, ssc.val)
                p.wait("vector", smm, smm.val)

                def fhm(v, vb=vb, hc=hc, kind=kind):
                    if kind == "e":
                        v.tensor_tensor(HE[:, hc * TP:(hc + 1) * TP], MTMP[:], vb,
                                        A.mult)
                        v.tensor_tensor(HE[:, hc * TP:(hc + 1) * TP],
                                        HE[:, hc * TP:(hc + 1) * TP],
                                        CBX[:, 0:TP], A.mult).then_inc(sv.h, 1)
                    else:
                        v.tensor_tensor(HE[:, hc * TP:(hc + 1) * TP], MTMP[:], vb,
                                        A.mult).then_inc(sv.h, 1)
                p.op("vector", fhm)
                sv.val += 1
                evac_vals[hc] = sv.val
            in_done[ji] = smm.val
            h_ready = sv.val
            p.wait("tensor", sv, h_ready)

            if kind == "e":
                k = eidx[ji]
                if ji - 1 in out_done:
                    p.wait("gpsimd", smm, out_done[ji - 1])
                dma("gpsimd", wt2, WOUTb, ewout_d[e])
                wout_ready[ji] = wt2.val
                if k + 2 < len(ejobs):
                    p.wait("gpsimd", smm, in_done[ji])
                    e2 = JOBS[ejobs[k + 2]][2]
                    dma("gpsimd", wt2, WINb[(k + 2) % 2], ewin_d[e2])
                    dma("gpsimd", wt2, VINb[(k + 2) % 2], evin_d[e2])
                    winv_ready[ejobs[k + 2]] = wt2.val

            if kind == "e":
                p.wait("tensor", wt2, wout_ready[ji])
            facc_vals = {}
            OBS = [PBIG0[:, 0:1024], PBIG0[:, 1024:2048], PBIG1[:, 0:1024]]
            for dc in range(8):
                ob = OBS[dc % 3]
                if dc >= 3:
                    p.wait("tensor", sv, facc_vals[dc - 3])

                def fop(t, dc=dc, ob=ob, hc0=hc0, kind=kind):
                    for hc in range(4):
                        hcw = hc0 + hc
                        if kind == "sh":
                            lhs = WSOUT[:, hcw * D + dc * 128:
                                        hcw * D + dc * 128 + 128]
                        else:
                            lhs = WOUTb[:, hc * D + dc * 128:
                                        hc * D + dc * 128 + 128]
                        for j in range(2):
                            mmi = t.matmul(ob[:, j * 512:(j + 1) * 512], lhs,
                                           HE[:, hc * TP + j * 512:
                                               hc * TP + j * 512 + 512],
                                           start=(hc == 0), stop=(hc == 3))
                            if hc == 3 and j == 1:
                                mmi.then_inc(smm.h, 1)
                p.op("tensor", fop)
                smm.val += 1
                p.wait("vector", smm, smm.val)

                def facc(v, dc=dc, ob=ob, first=first):
                    dst = ACCH[:, dc * TP:(dc + 1) * TP]
                    if first:
                        v.tensor_copy(dst, ob).then_inc(sv.h, 1)
                    else:
                        v.tensor_tensor(dst, dst, ob, A.add).then_inc(sv.h, 1)
                p.op("vector", facc)
                sv.val += 1
                facc_vals[dc] = sv.val
            out_done[ji] = smm.val
            last_facc[ji] = sv.val

            if kind == "e" and e == NE - 1:
                moe_done_v = sv.val
                if pa == 0:
                    p.wait("sync", ld, x2_spill)
                for itl, fdummy in [(0, True)] + [(i, False) for i in range(8)]:
                    itg = pa * 8 + itl
                    pi = itl % 2
                    p.wait("sync", sv, sv.val)
                    dma("sync", ld, X2R[:], scr2[:, itg * D:(itg + 1) * D])
                    x2r_done = ld.val
                    p.wait("tensor", sv, sv.val)
                    tbf = PBIG0[:, pi * 1024:(pi + 1) * 1024]

                    def ftf(t, itl=itl, tbf=tbf):
                        for c in range(8):
                            mmi = t.transpose(tbf[:, c * 128:(c + 1) * 128],
                                              ACCH[:, c * TP + itl * 128:
                                                   c * TP + itl * 128 + 128],
                                              IDF[:])
                            if c == 7:
                                mmi.then_inc(smm.h, 1)
                    p.op("tensor", ftf)
                    smm.val += 1
                    p.wait("vector", smm, smm.val)
                    ln_stats(tbf, D, 1, TMPA[:])
                    p.wait("vector", ld, x2r_done)
                    if out_stores > 0:
                        p.wait("vector", sst, out_stores)

                    def ffin(v, tbf=tbf):
                        v.reciprocal(STT[:, 15:16], SQS[:, 2:3])
                        v.drain()
                        v.tensor_scalar(TMPA[:], tbf, STT[:, 12:13], STT[:, 15:16],
                                        A.subtract, A.mult)
                        v.tensor_tensor(OUTT[:], X2R[:], TMPA[:], A.add
                                        ).then_inc(sv.h, 1)
                    p.op("vector", ffin)
                    sv.val += 1
                    if not fdummy:
                        p.wait("sync", sv, sv.val)
                        dma("sync", sst, out_d[:, itg * D:(itg + 1) * D], OUTT[:])
                        out_stores = sst.val
                final_last_v = sv.val

        tap("acc0", ACCH[:], sv, sv.val)

        p.wait("gpsimd", sst, sst.val)
        return _close(block, p, nc)


def _finish(p, sst, out_d, ACCH):
    p.wait("gpsimd", sst, sst.val)


def _close(block, p, nc):
    if True:
        @block.sync
        def _(e):
            for fn in p.q["sync"]:
                fn(e)

        @block.gpsimd
        def _(e):
            for fn in p.q["gpsimd"]:
                fn(e)

        @block.tensor
        def _(e):
            for fn in p.q["tensor"]:
                fn(e)

        @block.vector
        def _(e):
            for fn in p.q["vector"]:
                fn(e)

        @block.scalar
        def _(e):
            for fn in p.q["scalar"]:
                fn(e)

    return nc


def _layouts(inputs):
    f = np.ascontiguousarray
    r = {}
    r["dw"] = f(inputs["L0_attn_down_w"].reshape(8, 128, dA).transpose(1, 0, 2)
                .reshape(128, 8 * dA))
    r["db"] = f(inputs["L0_attn_down_b"].reshape(2, 128).T)
    for nm, kk in [("wq", "L0_attn_wq"), ("wk", "L0_attn_wk"),
                   ("wv", "L0_attn_wv"), ("wo", "L0_attn_wo")]:
        r[nm] = f(inputs[kk].reshape(2, 128, dA).transpose(1, 0, 2).reshape(128, 2 * dA))
    r["uw"] = f(inputs["L0_attn_up_w"].reshape(2, 128, D).transpose(1, 0, 2)
                .reshape(128, 2 * D))
    r["upb"] = f(inputs["L0_attn_up_b"].reshape(1, D))
    r["w1"] = f(np.concatenate([inputs["L0_attn_W1"][0], inputs["L0_attn_W1"][1]], 1))
    r["b1"] = f(inputs["L0_attn_b1"].reshape(1, 2 * FH))
    r["glw"] = f(inputs["L0_attn_ttt_ln_w"].reshape(1, 2 * FH))
    r["glb"] = f(inputs["L0_attn_ttt_ln_b"].reshape(1, 2 * FH))
    r["lrw"] = f(inputs["L0_attn_lr_w"].reshape(2, 2, 128).transpose(2, 1, 0)
                 .reshape(128, 4))
    r["lrb"] = f(inputs["L0_attn_lr_b"].reshape(2, 1))
    r["tki"] = f(inputs["L0_attn_tok_idx"].reshape(1, K))
    r["pw"] = f(inputs["L0_attn_post_w"].reshape(1, dA))
    r["pb"] = f(inputs["L0_attn_post_b"].reshape(1, dA))
    r["lw"] = f(inputs["L0_attn_ln_w"].reshape(1, dA))
    r["lb"] = f(inputs["L0_attn_ln_b"].reshape(1, dA))
    r["gwt"] = f(inputs["L0_ffn_gate_w"].T.reshape(8, 128, NE).transpose(1, 0, 2)
                 .reshape(128, 8 * NE))
    r["wsin"] = f(inputs["L0_ffn_ws_in"].reshape(8, 128, SHID).transpose(1, 0, 2)
                  .reshape(128, 8 * SHID))
    r["vsin"] = f(inputs["L0_ffn_vs_in"].reshape(8, 128, SHID).transpose(1, 0, 2)
                  .reshape(128, 8 * SHID))
    r["wsout"] = f(inputs["L0_ffn_ws_out"].reshape(8, 128, D).transpose(1, 0, 2)
                   .reshape(128, 8 * D))
    r["ewin"] = f(inputs["L0_ffn_e_win"].reshape(NE, 8, 128, EH)
                  .transpose(0, 2, 1, 3).reshape(NE, 128, 8 * EH))
    r["evin"] = f(inputs["L0_ffn_e_vin"].reshape(NE, 8, 128, EH)
                  .transpose(0, 2, 1, 3).reshape(NE, 128, 8 * EH))
    r["ewout"] = f(inputs["L0_ffn_e_wout"].reshape(NE, 4, 128, D)
                   .transpose(0, 2, 1, 3).reshape(NE, 128, 4 * D))
    cstv = np.zeros((128, 8), np.float32)
    cstv[:, 0] = 1e-6
    cstv[:, 1] = 1e-5
    cstv[:, 2] = 1.0 / 128.0
    cstv[:, 3] = 1.0
    cstv[:, 5] = -1.0
    r["cst"] = cstv
    r["idf"] = np.eye(128, dtype=np.float32)
    r["ones16"] = np.ones((1, 16), np.float32)
    r["onesc"] = np.ones((1, 128), np.float32)
    r["negu"] = f(-np.triu(np.ones((16, 16), np.float32)) / 128.0)
    r["rk"] = (1.0 / np.arange(1, K + 1, dtype=np.float32)).reshape(1, K)
    return r


def kernel(**inputs):
    inputs = {k: np.asarray(v) for k, v in inputs.items()}
    shared = _layouts(inputs)
    x = inputs["L0_attn_x"]
    nc = build_program()
    in_maps = []
    for b in range(B):
        m = dict(shared)
        m["xin"] = np.ascontiguousarray(
            x[b].reshape(NT, 128, D).transpose(1, 0, 2).reshape(128, NT * D))
        in_maps.append(m)
    res = run_bass_kernel_spmd(nc, in_maps, core_ids=list(range(B)))
    outs = []
    for b in range(B):
        o = np.asarray(res.results[b]["out"])
        outs.append(o.reshape(128, NT, D).transpose(1, 0, 2).reshape(T, D))
    return np.stack(outs).astype(np.float32)
